# version 02
# baseline (speedup 1.0000x reference)
import math
from contextlib import ExitStack

import numpy as np
import ml_dtypes

import concourse.bass as bass
import concourse.mybir as mybir
from concourse.bass_utils import run_bass_kernel_spmd

F32 = mybir.dt.float32
BF16 = mybir.dt.bfloat16
F16 = mybir.dt.float16
U8 = mybir.dt.uint8
I32 = mybir.dt.int32
ALU = mybir.AluOpType
ACT = mybir.ActivationFunctionType
AXX = mybir.AxisListType.X
BF = ml_dtypes.bfloat16

B, T, D = 4, 4096, 1024
NT = T // 128
NH, HD = 8, 128
WIN, TOPK = 64, 32
IFF = 2816
H1, HD1 = 16, 64
INNER, IH, HID = 512, 64, 2730
VHALO = 1152
VRNG = VHALO + 2048
QLO0 = 512
QLO1 = 1024
W1P = 2816
XNW = 1 + VHALO + T
LBIG = 30000.0
SENT = -1.0e30
NEGINF = -3.0e38

ENGINES = ("sync", "gpsimd", "vector", "scalar", "tensor")

CONSTS = [0.0, 1.0, -1.0, 0.5, -0.5, 0.125, -0.125, 1e-8, 1e-5, -20.0, -30.0,
          1.0 / 1024.0, 8.0, 2.0, 1e-24, 0.04, 64.0]


class Prog:
    DMA_SEMS = ("q0", "qg0", "qx0", "qx1", "qt", "qt0", "qt1", "qw", "qw2", "qsp0", "qsp1", "qout", "qaux", "qga")

    def __init__(self, nc, stack):
        self.nc = nc
        self.sems = {}
        self.cnt = {}
        self.ops = {e: [] for e in ENGINES}
        for s in ("ev", "es", "et", "eg", "esp") + self.DMA_SEMS:
            self.sems[s] = stack.enter_context(nc.semaphore(s))
            self.cnt[s] = 0
        self.base = []

    def emit(self, eng, fn, deps=(), dma=None, hazard=False):
        if dma:
            sem, step = dma, 16
        else:
            sem = {"vector": "ev", "scalar": "es", "tensor": "et",
                   "gpsimd": "eg", "sync": "esp"}[eng]
            step = 1
        seen = {}
        for dp in list(deps) + self.base:
            if dp is None:
                continue
            s, v = dp
            if v <= 0:
                continue
            if seen.get(s, 0) < v:
                seen[s] = v
        waits = sorted(seen.items())
        self.cnt[sem] += step
        tok = (sem, self.cnt[sem])
        self.ops[eng].append((waits, fn, sem, step))
        return tok

    def qtot(self, q):
        return (q, self.cnt[q])

    def check(self):
        pos = {e: 0 for e in ENGINES}
        val = {s: 0 for s in self.sems}
        progress = True
        while progress:
            progress = False
            for e in ENGINES:
                while pos[e] < len(self.ops[e]):
                    waits, fn, sem, step = self.ops[e][pos[e]]
                    if all(val[s] >= v for s, v in waits):
                        val[sem] += step
                        pos[e] += 1
                        progress = True
                    else:
                        break
        stuck = {e: pos[e] for e in ENGINES if pos[e] < len(self.ops[e])}
        if stuck:
            msg = []
            for e, pp in stuck.items():
                msg.append(f"{e}@{pp}/{len(self.ops[e])} waits={self.ops[e][pp][0]}")
            raise RuntimeError("deadlock: " + "; ".join(msg) + f" state={val}")

    def replay(self, block):
        prog = self

        def mk(eng_name):
            def body(eng):
                wm = {}
                for waits, fn, sem, step in prog.ops[eng_name]:
                    for s, v in waits:
                        if wm.get(s, 0) >= v:
                            continue
                        eng.wait_ge(prog.sems[s], v)
                        wm[s] = v
                    try:
                        inst = fn(eng)
                    except Exception:
                        import traceback
                        print(f"FAILED op: engine={eng_name} "
                              f"index={prog.ops[eng_name].index((waits, fn, sem, step))}")
                        raise
                    inst.then_inc(prog.sems[sem], step)
            return body

        block.sync(mk("sync"))
        block.gpsimd(mk("gpsimd"))
        block.vector(mk("vector"))
        block.scalar(mk("scalar"))
        block.tensor(mk("tensor"))


class TB:

    __slots__ = ("w", "r")

    def __init__(self, tok=None):
        self.w = tok
        self.r = []

    def rd(self):
        return [self.w] if self.w else []

    def wr(self):
        d = list(self.r)
        if self.w:
            d.append(self.w)
        return d

    def did_rd(self, tok):
        self.r.append(tok)
        if len(self.r) > 6:
            self.r = self.r[-6:] + [max(self.r, key=lambda t: t[1])] if False else self.r[-6:]
        return tok

    def did_wr(self, tok):
        self.w = tok
        self.r = []
        return tok



def _rotary_tables():
    n, dh = T, 64
    half = np.arange(0, dh, 2, dtype=np.float32)
    inv_freq = (np.float32(1.0) / np.float32(10000.0) ** (half / np.float32(dh))).astype(np.float32)
    t = np.arange(n, dtype=np.float32)
    freqs = (t[:, None] * inv_freq[None, :]).astype(np.float32)
    freqs = np.concatenate([freqs, freqs], axis=-1)
    base = ((half + np.float32(0.4 * dh)) / np.float32(1.4 * dh)).astype(np.float32)
    power = ((t - np.float32(n // 2)) / np.float32(256.0)).astype(np.float32)
    scale = (base[None, :] ** power[:, None]).astype(np.float32)
    scale = np.concatenate([scale, scale], axis=-1)
    cos = np.cos(freqs).astype(np.float32)
    sin = np.sin(freqs).astype(np.float32)
    q1 = (cos * scale).astype(np.float32)
    q2 = (sin * scale).astype(np.float32)
    q2[:, :32] = -q2[:, :32]
    k1 = (cos / scale).astype(np.float32)
    k2 = (sin / scale).astype(np.float32)
    k2[:, :32] = -k2[:, :32]
    return q1, q2, k1, k2


def _l0_masks():
    m1 = np.zeros((128, 128), np.float32)
    m2 = np.full((128, 128), -LBIG, np.float32)
    m1_t0 = np.zeros((128, 128), np.float32)
    m2_t0 = np.full((128, 128), -LBIG, np.float32)
    for p in range(128):
        for c in range(128):
            pos_rel = (c - 64) if p < 64 else c
            dd = p - pos_rel
            if 0 <= dd <= 63:
                m2[p, c] = 0.0
                if dd != 0:
                    m1[p, c] = 1.0
                if pos_rel >= 0:
                    m2_t0[p, c] = 0.0
                    if dd != 0:
                        m1_t0[p, c] = 1.0
    dup = np.full((128, 64), -LBIG, np.float32)
    for p in range(64):
        if p <= 62:
            dup[p, 0:(63 - p)] = 0.0
    return m1, m2, m1_t0, m2_t0, dup


def _p4_groups(l):
    qlo = QLO1 if l == 1 else QLO0
    gs = []
    q = qlo
    while q < VRNG:
        w = min(512, VRNG - q)
        gs.append((q, w))
        q += w
    return gs


def _l1_inner_masks(half):
    off = 2048 * half - VHALO
    masks = {}
    for l in range(2):
        for gi, (Q0, W) in enumerate(_p4_groups(l)):
            nch = (512 + W) // 128
            arr = np.zeros((nch, 128, W), np.float32)
            vq = Q0 + np.arange(W)[None, :]
            rq = vq + off
            qbad = (rq < 0) | (rq >= T)
            for r in range(nch):
                C0 = Q0 - 512 + 128 * r
                vk = C0 + np.arange(128)[:, None]
                rk = vk + off
                dd = vq - vk
                ok = (dd >= 0) & (dd <= 512) & (rk >= 0) & (rk < T)
                arr[r] = np.where(qbad | ok, 1.0, 0.0)
            masks[(l, gi)] = arr.astype(ml_dtypes.bfloat16)
    return masks


def prep_inputs(ins):
    sh = {}
    f32 = lambda a: np.ascontiguousarray(a, dtype=np.float32)
    sh["encq"] = f32(np.transpose(ins["L0_attn_encoder_query"], (1, 0, 2)).reshape(D, NH * HD))
    sh["encv"] = f32(np.transpose(ins["L0_attn_encoder_value"], (1, 0, 2)).reshape(D, NH * HD))
    sh["mnw1"] = f32(ins["L0_attn_mn_w1"])
    sh["mnb1"] = f32(ins["L0_attn_mn_b1"]).reshape(1, 512)
    sh["mnw2"] = f32(ins["L0_attn_mn_w2"])
    sh["mnb2"] = f32(ins["L0_attn_mn_b2"]).reshape(1, D)
    sh["basem"] = f32(ins["L0_attn_base_metric"]).reshape(1, NH * HD)
    sh["hebut"] = f32(np.transpose(ins["L0_attn_hebbian_U"], (0, 2, 1)))
    sh["hebv"] = f32(ins["L0_attn_hebbian_V"])
    sh["dec"] = f32(ins["L0_attn_decoder"])
    sh["wg_nat"] = f32(ins["L0_ffn_w_gate"])
    sh["wu_nat"] = f32(ins["L0_ffn_w_up"])
    sh["wd_nat"] = f32(ins["L0_ffn_w_down"])
    Wqkv = ins["L1_attn_Wqkv"].reshape(H1, 3 * HD1, D)
    sh["wqT"] = f32(np.transpose(Wqkv[:, 0:HD1, :], (2, 0, 1)).reshape(D, H1 * HD1))
    sh["wkT"] = f32(np.transpose(Wqkv[:, HD1:2 * HD1, :], (2, 0, 1)).reshape(D, H1 * HD1))
    sh["wvT"] = f32(np.transpose(Wqkv[:, 2 * HD1:, :], (2, 0, 1)).reshape(D, H1 * HD1))
    bqkv = ins["L1_attn_bqkv"].reshape(H1, 3 * HD1)
    sh["bq1"] = f32(bqkv[:, 0:HD1].reshape(1, H1 * HD1))
    sh["bk1"] = f32(bqkv[:, HD1:2 * HD1].reshape(1, H1 * HD1))
    sh["bv1"] = f32(bqkv[:, 2 * HD1:].reshape(1, H1 * HD1))
    sh["wo1"] = f32(ins["L1_attn_Wo"].T)
    sh["bo1"] = f32(ins["L1_attn_bo"]).reshape(1, D)
    sh["fqkv"] = f32(ins["L1_ffn_wqkv"])
    sh["fwo"] = f32(ins["L1_ffn_wo"])
    w2p = np.zeros((2, W1P, D), np.float32)
    w2p[:, 0:HID, :] = ins["L1_ffn_w2"]
    sh["fw2"] = f32(w2p)
    perm = (np.arange(INNER) // IH) * IH + (np.arange(INNER) % IH + 32) % IH
    fqkv = f32(ins["L1_ffn_wqkv"])
    sh["fqkvp"] = np.ascontiguousarray(
        np.concatenate([fqkv[:, :, perm], fqkv[:, :, INNER + perm]], axis=2))
    w1 = f32(ins["L1_ffn_w1"])
    w1a = np.zeros((2, D, W1P), np.float32)
    w1g = np.zeros((2, D, W1P), np.float32)
    w1a[:, :, 0:HID] = w1[:, :, 0:HID]
    w1g[:, :, 0:HID] = w1[:, :, HID:2 * HID]
    sh["w1a"], sh["w1g"] = w1a, w1g
    b1 = f32(ins["L1_ffn_b1"])
    b1c = np.zeros((2, 2 * W1P), np.float32)
    b1c[:, 0:HID] = b1[:, 0:HID]
    b1c[:, W1P:W1P + HID] = b1[:, HID:2 * HID]
    sh["fb1col"] = np.ascontiguousarray(
        b1c.reshape(2, 44, 128).transpose(0, 2, 1))
    sh["flgrow"] = np.ascontiguousarray(
        f32(ins["L1_ffn_ln_g"]).reshape(2, 8, 128))
    sh["flbrow"] = np.ascontiguousarray(
        f32(ins["L1_ffn_ln_b"]).reshape(2, 8, 128))
    bones = np.zeros((128, 32), np.float32)
    for oc in range(4):
        bones[0:64, 8 * oc + 2 * oc] = 1.0
        bones[64:128, 8 * oc + 2 * oc + 1] = 1.0
    sh["bones"] = bones
    hsel8 = np.zeros((8, 4 * 128), np.float32)
    for oc in range(4):
        hsel8[2 * oc, oc * 128: oc * 128 + 64] = 1.0
        hsel8[2 * oc + 1, oc * 128 + 64: oc * 128 + 128] = 1.0
    sh["hsel"] = hsel8
    sh["fb2col"] = np.ascontiguousarray(
        f32(ins["L1_ffn_b2"]).reshape(2, 8, 128).transpose(0, 2, 1))
    qs = f32(ins["L1_ffn_q_scale"])
    ks = f32(ins["L1_ffn_k_scale"])
    pm64 = (np.arange(IH) + 32) % IH
    sh["fqscol"] = np.ascontiguousarray(np.tile(qs, (1, 2))[:, :, None])
    sh["fqspcol"] = np.ascontiguousarray(np.tile(qs[:, pm64], (1, 2))[:, :, None])
    sh["fkscol"] = np.ascontiguousarray(np.tile(ks, (1, 2))[:, :, None])
    sh["fkspcol"] = np.ascontiguousarray(np.tile(ks[:, pm64], (1, 2))[:, :, None])
    m1, m2, m1t0, m2t0, dup = _l0_masks()
    sh["dm1"], sh["dm2"], sh["dm1t0"], sh["dm2t0"], sh["ddup"] = m1, m2, m1t0, m2t0, dup
    sh["ident"] = np.eye(128, dtype=np.float32)
    sh["cpool"] = np.tile(np.array(CONSTS, np.float32), (128, 1))
    q1, q2, k1, k2 = _rotary_tables()
    x = ins["L0_attn_x"]
    percore = []
    for c in range(8):
        b, half = c // 2, c % 2
        S0 = 2048 * half
        off = S0 - VHALO
        rc = np.clip(np.arange(VRNG) + off, 0, T - 1)

        def fm(tab, scale8):
            out = tab[rc].T
            if scale8:
                out = out * np.float32(8.0)
            return np.ascontiguousarray(np.tile(out, (2, 1)), np.float32)
        mk = _l1_inner_masks(half)
        pc = {
            "x0": f32(x[b]),
            "rt1q": fm(q1, True), "rt2q": fm(q2, True),
            "rt1k": fm(k1, False), "rt2k": fm(k2, False),
            "msel": np.full((128, 1), float(half), np.float32),
        }
        for l in range(2):
            for gi in range(len(_p4_groups(l))):
                pc[f"pmsk{l}_{gi}"] = mk[(l, gi)]
        percore.append(pc)
    return sh, percore



class KB:
    def __init__(self, nc, stack, taps=()):
        self.nc = nc
        self.p = Prog(nc, stack)
        self.taps = set(taps)
        self.d = {}
        self.s = {}
        self.psd = {}
        self.tok = {}
        self.scope = None
        self._declare()

    def open_scope(self):
        assert self.scope is None
        self.scope = ExitStack()

    def close_scope(self):
        self.scope.close()
        self.scope = None
        self.p.base = [(s_, v) for s_, v in self.p.cnt.items() if v > 0]

    def dram(self, name, shape, dtype=F32, kind="Internal"):
        self.d[name] = self.nc.dram_tensor(name, list(shape), dtype, kind=kind).ap()
        return self.d[name]

    def _uniq(self, name):
        self._nmc = getattr(self, "_nmc", {})
        n = self._nmc.get(name, 0)
        self._nmc[name] = n + 1
        return name if n == 0 else f"{name}__{n}"

    def sb(self, name, shape, dtype=F32, scoped=False, **kw):
        un = self._uniq(name)
        if scoped:
            h = self.scope.enter_context(
                self.nc.sbuf_tensor(un, list(shape), dtype, **kw))
            self.s[name] = h.ap() if hasattr(h, "ap") else h[:]
        else:
            self.s[name] = self.nc.alloc_sbuf_tensor(
                un, list(shape), dtype, **kw).ap()
        return self.s[name]

    def psum(self, name, shape, dtype=F32, scoped=False):
        un = self._uniq(name)
        if scoped:
            h = self.scope.enter_context(
                self.nc.psum_tensor(un, list(shape), dtype))
            self.psd[name] = h.ap() if hasattr(h, "ap") else h[:]
        else:
            self.psd[name] = self.nc.alloc_psum_tensor(un, list(shape), dtype).ap()
        return self.psd[name]

    def _declare(self):
        d = self.dram
        d("x0", (T, D), kind="ExternalInput")
        for nm, shape, dt in [
            ("encq", (D, NH * HD), F32), ("encv", (D, NH * HD), F32),
            ("mnw1", (D, 512), F32), ("mnb1", (1, 512), F32),
            ("mnw2", (512, D), F32), ("mnb2", (1, D), F32),
            ("basem", (1, NH * HD), F32),
            ("hebut", (NH, 32, HD), F32), ("hebv", (NH, 32, HD), F32),
            ("dec", (D, D), F32),
            ("wg_nat", (IFF, D), F32), ("wu_nat", (IFF, D), F32),
            ("wd_nat", (D, IFF), F32),
            ("wqT", (D, H1 * HD1), F32), ("wkT", (D, H1 * HD1), F32),
            ("wvT", (D, H1 * HD1), F32),
            ("bq1", (1, H1 * HD1), F32), ("bk1", (1, H1 * HD1), F32),
            ("bv1", (1, H1 * HD1), F32),
            ("wo1", (D, D), F32), ("bo1", (1, D), F32),
            ("fqkv", (2, D, 3 * INNER), F32),
            ("fqkvp", (2, D, 2 * INNER), F32), ("fwo", (2, INNER, D), F32),
            ("w1a", (2, D, W1P), F32), ("w1g", (2, D, W1P), F32),
            ("fb1col", (2, 128, 44), F32),
            ("flgrow", (2, 8, 128), F32), ("flbrow", (2, 8, 128), F32),
            ("fb2col", (2, 128, 8), F32),
            ("bones", (128, 32), F32), ("hsel", (8, 4 * 128), F32),
            ("fw2", (2, W1P, D), F32),
            ("fqscol", (2, 128, 1), F32), ("fqspcol", (2, 128, 1), F32),
            ("fkscol", (2, 128, 1), F32), ("fkspcol", (2, 128, 1), F32),
            ("dm1", (128, 128), F32), ("dm2", (128, 128), F32),
            ("dm1t0", (128, 128), F32), ("dm2t0", (128, 128), F32),
            ("ddup", (128, 64), F32),
            ("ident", (128, 128), F32), ("cpool", (128, len(CONSTS)), F32),
            ("rt1q", (128, VRNG), F32), ("rt2q", (128, VRNG), F32),
            ("rt1k", (128, VRNG), F32), ("rt2k", (128, VRNG), F32),
            ("msel", (128, 1), F32),
        ]:
            d(nm, shape, dt, kind="ExternalInput")
        for l in range(2):
            for gi, (Q0, W) in enumerate(_p4_groups(l)):
                d(f"pmsk{l}_{gi}", ((512 + W) // 128, 128, W), BF16,
                  kind="ExternalInput")
        d("x2", (D, 2048), kind="ExternalOutput")
        d("s_ffn0", (T, D))
        d("s_hq", (T, IFF), BF16)
        d("s_x1", (VHALO + T, D))
        d("s_xn1fm", (D, XNW), BF16)
        d("s_xv", (D, 1 + VRNG), BF16)
        d("s_mid", (D, VRNG), BF16)
        d("s_out", (D, VRNG), BF16)
        d("s_h3", (W1P, VRNG - QLO0), BF16)
        cp = self.sb("c_pool", (128, len(CONSTS)), F32)
        for i, v in enumerate(CONSTS):
            self.nc.const_aps.register(F32, float(v), cp[:, i:i + 1])

    def tap(self, name, shape, dtype=F32):
        if name not in self.taps:
            return None
        return self.nc.dram_tensor("tap_" + name, list(shape), dtype,
                                   kind="ExternalOutput").ap()

    def dma(self, eng, out, in_, deps=(), q="qaux"):
        if q == "qout":
            deps = list(deps) + [self.p.qtot("qout")]
        return self.p.emit(eng, lambda e, o=out, i=in_: e.dma_start(o, i),
                           deps=deps, dma=q)

    def dmat(self, out, in_, deps=(), q="qt"):
        return self.p.emit("sync", lambda e, o=out, i=in_: e.dma_start_transpose(o, i),
                           deps=deps, dma=q)

    def ve(self, fn, deps=(), hazard=False):
        return self.p.emit("vector", fn, deps=deps, hazard=hazard)

    def sc(self, fn, deps=(), hazard=False):
        return self.p.emit("scalar", fn, deps=deps, hazard=hazard)

    def te(self, fn, deps=(), hazard=False):
        return self.p.emit("tensor", fn, deps=deps, hazard=hazard)

    def ge(self, fn, deps=(), hazard=False):
        return self.p.emit("gpsimd", fn, deps=deps, hazard=hazard)


def wload_chunked(kb, dst_name, src_name, nchunk, width, q="qw", deps=()):
    src = bass.AP(kb.d[src_name].tensor, 0,
                  [[width, 128], [128 * width, nchunk], [1, width]])
    dst = bass.AP(kb.s[dst_name].tensor, 0,
                  [[nchunk * width, 128], [width, nchunk], [1, width]])
    return kb.dma("gpsimd", dst, src, q=q, deps=deps)



def emit_p0(kb):
    s, d = kb.sb, kb.d
    s("c_ident", (128, 128), BF16)
    s("c_identf", (128, 128), F32)
    s("c_ones_col", (128, 1), BF16)
    s("c_ones_row", (1, 128), BF16)
    s("r_mnb1", (1, 512), F32)
    s("r_mnb2", (1, 1024), F32)
    s("r_basem", (1, 1024), F32)
    s("m_dm1", (128, 128), F32)
    s("m_dm2", (128, 128), F32)
    s("m_dm1t0", (128, 128), F32)
    s("m_dm2t0", (128, 128), F32)
    s("m_ddup", (128, 64), F32)
    toks = []
    toks.append(kb.dma("sync", kb.s["c_pool"], d["cpool"], q="q0"))
    toks.append(kb.dma("gpsimd", kb.s["c_ident"], d["ident"], q="qg0"))
    toks.append(kb.dma("sync", kb.s["c_identf"], d["ident"], q="q0"))

    def chunked(dst_name, src_name, nchunk, width):
        src = bass.AP(d[src_name].tensor, 0,
                      [[width, 128], [128 * width, nchunk], [1, width]])
        dstt = kb.s[dst_name]
        dst = bass.AP(dstt.tensor, 0,
                      [[nchunk * width, 128], [width, nchunk], [1, width]])
        toks.append(kb.dma("gpsimd", dst, src, q="qg0"))

    for nm, dn in [("r_mnb1", "mnb1"), ("r_mnb2", "mnb2"), ("r_basem", "basem")]:
        toks.append(kb.dma("sync", kb.s[nm], d[dn], q="q0"))
    for nm, dn in [("m_dm1", "dm1"), ("m_dm2", "dm2"), ("m_dm1t0", "dm1t0"),
                   ("m_dm2t0", "dm2t0"), ("m_ddup", "ddup")]:
        toks.append(kb.dma("sync", kb.s[nm], d[dn], q="q0"))
    s("ln_mean", (128, NT), F32)
    s("ln_inv", (128, NT), F32)
    s("ln1_mean", (128, NT), F32)
    s("ln1_inv", (128, NT), F32)
    s("st_sum", (128, NT), F32)
    s("st_s2", (128, NT), F32)
    s("st_m2", (128, NT), F32)
    s("sig_acc", (128, NH), F32)
    s("sig8_acc", (128, NH), F32)
    s("metcol2", (128, NH), F32)
    s("w_hm", (128, 8 * 128), BF16)
    s("c_msel", (128, 1), F32)
    toks.append(kb.ge(lambda e: e.memset(kb.s["c_ones_col"], 1.0)))
    toks.append(kb.ge(lambda e: e.memset(kb.s["c_ones_row"], 1.0)))
    toks.append(kb.dma("sync", kb.s["c_msel"], d["msel"], q="q0"))
    kb.tok["p0"] = [t for t in toks if t]
    kb.p.base = list(kb.tok["p0"])



def emit_p1(kb):
    s, d = kb.sb, kb.d
    s("w_encq", (128, 8 * 1024), BF16, scoped=True)
    t_encq = wload_chunked(kb, "w_encq", "encq", 8, 1024)
    kb.p.base = kb.p.base + [t_encq]
    for i in range(2):
        s(f"p1_x{i}", (128, 4 * D), F32, scoped=True)
        s(f"p1_fm{i}", (128, 8 * 512), BF16, scoped=True, is_transpose_dest=True)
        s(f"p1_xn{i}", (128, D), BF16, scoped=True)
        s(f"p1_sq{i}", (128, D), F32, scoped=True)
    s("p1_rl", (128, 512), F32, scoped=True)
    s("p1_a1", (128, 1), F32, scoped=True)
    s("p1_a2", (128, 1), F32, scoped=True)
    kb.psum("pq0", (128, 512), scoped=True)
    kb.psum("pq1", (128, 512), scoped=True)
    z1 = kb.ve(lambda e: e.memset(kb.s["sig_acc"], 0.0))
    z2 = kb.ve(lambda e: e.memset(kb.s["sig8_acc"], 0.0))
    p0 = kb.tok["p0"]
    x0 = d["x0"]
    buf_free = [None, None]
    ps_free = [None, None]
    sig_last = (z1, z2)
    relu_free = None
    for g in range(8):
        bi = g % 2
        xg, fmg = kb.s[f"p1_x{bi}"], kb.s[f"p1_fm{bi}"]
        xnb, sqb = kb.s[f"p1_xn{bi}"], kb.s[f"p1_sq{bi}"]
        src = bass.AP(x0.tensor, g * 512 * D, [[D, 128], [128 * D, 4], [1, D]])
        dst = bass.AP(xg.tensor, 0, [[4 * D, 128], [D, 4], [1, D]])
        tld = kb.dma("sync", dst, src, deps=[buf_free[bi]], q=f"qx{bi}")
        tr_toks = []
        t_xn_prev = None
        for j in range(4):
            ti = g * 4 + j
            xt = xg[:, j * D:(j + 1) * D]
            c_sum = kb.s["st_sum"][:, ti:ti + 1]
            c_s2 = kb.s["st_s2"][:, ti:ti + 1]
            c_m2 = kb.s["st_m2"][:, ti:ti + 1]
            c_mean = kb.s["ln_mean"][:, ti:ti + 1]
            c_inv = kb.s["ln_inv"][:, ti:ti + 1]
            t1 = kb.ve(lambda e, xt=xt, c=c_sum: e.reduce_sum(c, xt, axis=AXX),
                       deps=[tld])
            tq = kb.sc(lambda e, xt=xt, sq=sqb: e.square(sq, xt),
                       deps=[tld, t_xn_prev])
            t2 = kb.ve(lambda e, sq=sqb, c=c_s2: e.reduce_sum(c, sq, axis=AXX),
                       deps=[tq])
            tm = kb.ve(lambda e, a=c_sum, o=c_mean: e.tensor_scalar(
                o, a, 1.0 / 1024.0, None, ALU.mult), deps=[t1])
            tm2 = kb.ve(lambda e, a=c_mean, o=c_m2: e.tensor_tensor(
                o, a, a, ALU.mult), deps=[tm], hazard=True)
            tv = kb.ve(lambda e, a=c_s2, b2=c_m2: e.scalar_tensor_tensor(
                a, a, 1.0 / 1024.0, b2, ALU.mult, ALU.subtract),
                deps=[t2, tm2], hazard=True)
            tl = kb.sc(lambda e, a=c_s2: e.activation(a, a, ACT.Ln, bias=1e-5),
                       deps=[tv])
            tiv = kb.sc(lambda e, a=c_s2, o=c_inv: e.activation(
                o, a, ACT.Exp, scale=-0.5), deps=[tl], hazard=True)
            t_xn = kb.ve(lambda e, xt=xt, m=c_mean, iv=c_inv, o=xnb:
                         e.scalar_tensor_tensor(
                             o, xt, m, iv.to_broadcast([128, D]),
                             ALU.subtract, ALU.mult),
                         deps=[tiv, tm] + tr_toks[-8:], hazard=True)
            t_xn_prev = t_xn
            for kc in range(8):
                tt = kb.dmat(fmg[:, kc * 512 + j * 128: kc * 512 + (j + 1) * 128],
                             xnb[:, kc * 128:(kc + 1) * 128],
                             deps=[t_xn, ps_free[bi]])
                tr_toks.append(tt)
        mm_last = {}
        for h in range(NH):
            pb = kb.psd[f"pq{h % 2}"]
            deps0 = tr_toks + p0 + [mm_last.get(h % 2), relu_free]
            tmm = None
            for kc in range(8):
                lhs = kb.s["w_encq"][:, kc * 1024 + h * 128: kc * 1024 + (h + 1) * 128]
                rhs = fmg[:, kc * 512:(kc + 1) * 512]
                tmm = kb.te(lambda e, pb=pb, lhs=lhs, rhs=rhs, kc=kc: e.matmul(
                    pb, lhs, rhs, start=(kc == 0), stop=(kc == 7)),
                    deps=deps0 if kc == 0 else ())
            t_r = kb.sc(lambda e, pb=pb: e.activation(
                kb.s["p1_rl"], pb, ACT.Relu, accum_out=kb.s["p1_a1"]),
                deps=[tmm, sig_last[0]])
            v8 = kb.s["p1_rl"].rearrange("p (a b) -> p b a", b=8)[:, 0:1, :]
            t_r8 = kb.ve(lambda e, v8=v8: e.reduce_sum(
                kb.s["p1_a2"], v8, axis=AXX), deps=[t_r, sig_last[1]])
            ta = kb.ve(lambda e, h=h: e.tensor_tensor(
                kb.s["sig_acc"][:, h:h + 1], kb.s["sig_acc"][:, h:h + 1],
                kb.s["p1_a1"], ALU.add), deps=[t_r, sig_last[0]], hazard=True)
            tb = kb.ve(lambda e, h=h: e.tensor_tensor(
                kb.s["sig8_acc"][:, h:h + 1], kb.s["sig8_acc"][:, h:h + 1],
                kb.s["p1_a2"], ALU.add), deps=[t_r8, sig_last[1]], hazard=True)
            sig_last = (ta, tb)
            relu_free = tb
            mm_last[h % 2] = t_r8
        buf_free[bi] = sig_last[1]
        ps_free[bi] = sig_last[1]
    kb.tok["p1"] = list(sig_last)
    kb.tok["p1_taps"] = []
    tp = kb.tap("sig", (128, NH))
    if tp is not None:
        kb.dma("sync", tp, kb.s["sig_acc"], deps=list(sig_last), q="qout")
        for nm, buf in [("sig8", "sig8_acc"), ("lnmean", "ln_mean"),
                        ("lninv", "ln_inv")]:
            ap = kb.tap(nm, (128, NH if nm == "sig8" else NT))
            if ap is not None:
                kb.dma("sync", ap, kb.s[buf], deps=list(sig_last), q="qout")
        kb.tok["p1_taps"] = [kb.p.qtot("qout")]



def emit_p2(kb):
    s = kb.sb
    p1 = kb.tok["p1"]
    p0 = kb.tok["p0"]
    s("w_mnw1", (128, 8 * 512), BF16, scoped=True)
    s("w_mnw2", (128, 4 * 1024), BF16, scoped=True)
    s("w_hebut", (32, 8 * 128), BF16, scoped=True)
    s("w_hebv", (32, 8 * 128), BF16, scoped=True)
    wload_chunked(kb, "w_mnw1", "mnw1", 8, 512)
    wload_chunked(kb, "w_mnw2", "mnw2", 4, 1024)
    for nm, dn in [("w_hebut", "hebut"), ("w_hebv", "hebv")]:
        src = bass.AP(kb.d[dn].tensor, 0, [[128, 32], [32 * 128, 8], [1, 128]])
        dst = bass.AP(kb.s[nm].tensor, 0, [[8 * 128, 32], [128, 8], [1, 128]])
        kb.dma("gpsimd", dst, src, q="qw")
    kb.p.base = kb.p.base + [kb.p.qtot("qw")]
    s("sigb", (128, NH), BF16, scoped=True)
    s("r_hid", (1, 512), F32, scoped=True)
    s("r_hidb", (1, 512), BF16, scoped=True)
    s("hidcol", (128, 4), BF16, scoped=True)
    s("r_met", (1, 1024), F32, scoped=True)
    ps_mn = kb.psum("ps_mn", (1, 512), scoped=True)
    ps_mod = kb.psum("ps_mod", (1, 1024), scoped=True)
    ps_tc = kb.psum("ps_tc", (128, 8), BF16, scoped=True)
    ps_hm = kb.psum("ps_hm", (128, 128), scoped=True)
    ps_mc = kb.psum("ps_mc", (128, 8), scoped=True)
    t0 = kb.ve(lambda e: e.tensor_scalar(
        kb.s["sig8_acc"], kb.s["sig8_acc"], 0.5 / 512.0, None, ALU.mult),
        deps=p1 + kb.tok.get("p1_taps", []), hazard=True)
    t0 = kb.ve(lambda e: e.scalar_tensor_tensor(
        kb.s["sigb"], kb.s["sig_acc"], 1.0 / 4096.0, kb.s["sig8_acc"],
        ALU.mult, ALU.add), deps=[t0], hazard=True)
    tmm = None
    for kc in range(8):
        h = kc
        lhs = kb.s["sigb"][:, kc:kc + 1]
        rhs = kb.s["w_mnw1"][:, kc * 512:(kc + 1) * 512]
        tmm = kb.te(lambda e, lhs=lhs, rhs=rhs, kc=kc: e.matmul(
            ps_mn, lhs, rhs, start=(kc == 0), stop=(kc == 7)),
            deps=[t0] + p0 if kc == 0 else ())
    th = kb.ve(lambda e: e.tensor_tensor(kb.s["r_hid"], ps_mn, kb.s["r_mnb1"],
                                         ALU.add), deps=[tmm])
    th = kb.sc(lambda e: e.activation(kb.s["r_hid"], kb.s["r_hid"], ACT.Gelu),
               deps=[th], hazard=True)
    th = kb.ve(lambda e: e.tensor_copy(kb.s["r_hidb"], kb.s["r_hid"]), deps=[th])
    tt = []
    for kc in range(4):
        t = kb.te(lambda e, kc=kc: e.transpose(
            ps_tc[:, 2 * kc:2 * kc + 1], kb.s["r_hidb"][0:1, kc * 128:(kc + 1) * 128],
            kb.s["c_ident"][0:1, 0:1]), deps=[th])
        tt.append(t)
    pv = ps_tc.rearrange("p (a b) -> p a b", b=2)[:, :, 0:1]
    tev = kb.ve(lambda e: e.tensor_copy(
        kb.s["hidcol"].rearrange("p a -> p a 1" if False else "p (a b) -> p a b", b=1), pv), deps=tt)
    tmm2 = None
    for kc in range(4):
        for nh_ in range(2):
            lhs = kb.s["hidcol"][:, kc:kc + 1]
            rhs = kb.s["w_mnw2"][:, kc * 1024 + nh_ * 512: kc * 1024 + nh_ * 512 + 512]
            tmm2 = kb.te(lambda e, lhs=lhs, rhs=rhs, kc=kc, nh_=nh_: e.matmul(
                ps_mod[:, nh_ * 512:(nh_ + 1) * 512], lhs, rhs,
                start=(kc == 0), stop=(kc == 3)), deps=[tev] if kc == 0 else ())
    tz = kb.ve(lambda e: e.tensor_tensor(kb.s["r_met"], ps_mod, kb.s["r_mnb2"],
                                         ALU.add), deps=[tmm2])
    tz = kb.ve(lambda e: e.scalar_tensor_tensor(
        kb.s["r_met"], kb.s["r_met"], 0.1, kb.s["r_basem"], ALU.mult, ALU.add),
        deps=[tz], hazard=True)
    tz = kb.sc(lambda e: e.activation(kb.s["r_met"], kb.s["r_met"], ACT.Exp),
               deps=[tz])
    tz = kb.sc(lambda e: e.activation(kb.s["r_met"], kb.s["r_met"], ACT.Ln,
                                      bias=1.0), deps=[tz], hazard=True)
    tz = kb.ve(lambda e: e.tensor_scalar(kb.s["r_met"], kb.s["r_met"], -2.0,
                                         None, ALU.mult), deps=[tz])
    tp = kb.tap("metric", (1, 1024))
    if tp is not None:
        kb.dma("sync", tp, kb.s["r_met"], deps=[tz], q="qout")
    tt2 = []
    for h in range(NH):
        t = kb.te(lambda e, h=h: e.transpose(
            ps_mc[:, h:h + 1], kb.s["r_met"][0:1, h * 128:(h + 1) * 128],
            kb.s["c_identf"][0:1, 0:1]), deps=[tz])
        tt2.append(t)
    tmc = kb.ve(lambda e: e.tensor_copy(kb.s["metcol2"], ps_mc), deps=tt2)
    hm_toks = []
    tprev = None
    for h in range(NH):
        t = kb.te(lambda e, h=h: e.matmul(
            ps_hm, kb.s["w_hebut"][:, h * 128:(h + 1) * 128],
            kb.s["w_hebv"][:, h * 128:(h + 1) * 128], start=True, stop=True),
            deps=p0 + tt2 + [tprev])
        t2 = kb.ve(lambda e, h=h: e.tensor_copy(
            kb.s["w_hm"][:, h * 128:(h + 1) * 128], ps_hm), deps=[t])
        tprev = t2
        hm_toks.append(t2)
    kb.tok["p2"] = [tmc] + hm_toks





def emit_quant(kb, src, src_is_f32, W, out, scr, deps):
    nb = W // 16
    bm, scf, rcp = scr["bm"][:, 0:nb], scr["scf"][:, 0:nb], scr["rcp"][:, 0:nb]
    rcpb, scb = scr["rcpb"][:, 0:nb], scr["scb"][:, 0:nb]
    u = scr["u"][:, 0:W]
    tA, tB = scr["tA"][:, 0:W], scr["tB"][:, 0:W]
    tC, tD = scr["tC"][:, 0:W], scr["tD"][:, 0:W]
    tE = scr["tE"][:, 0:W]
    src3 = src.rearrange("p (a b) -> p a b", b=16)

    def bc16(ap):
        return bass.AP(ap.tensor, 0, [[ap.tensor.shape[1], 128], [1, nb], [0, 16]])

    t = kb.ve(lambda e: e.reduce_max(bm, src3, axis=AXX,
                                     apply_absolute_value=True), deps=deps)
    t = kb.ve(lambda e: e.tensor_scalar(scf, bm, 6e-12, 1.0 / 6.0,
                                        ALU.max, ALU.mult), deps=[t])
    t = kb.ve(lambda e: e.reciprocal(rcp, scf), deps=[t])
    tsc = kb.ve(lambda e: e.tensor_copy(scb, scf), deps=[t])
    if src_is_f32:
        rr, trc = rcp, t
    else:
        trc = kb.ve(lambda e: e.tensor_copy(rcpb, rcp), deps=[t])
        rr = rcpb
    u3 = u.rearrange("p (a b) -> p a b", b=16)
    t = kb.ve(lambda e, rr=rr: e.tensor_tensor(u3, src3, bc16(rr), ALU.mult),
              deps=[trc, tsc] + deps)
    tsg = kb.ve(lambda e: e.tensor_scalar(tE, u, 0.0, 2.0, ALU.is_gt, ALU.mult),
                deps=[t])
    tau = kb.sc(lambda e: e.activation(u, u, ACT.Abs), deps=[tsg])
    masks = []
    for buf, thr, dg in [(tA, 0.25, 0.5), (tB, 0.75, 0.5), (tC, 1.25, 0.5),
                         (tD, 1.75, 0.5)]:
        masks.append(kb.ve(lambda e, b_=buf, th=thr, dg=dg: e.tensor_scalar(
            b_, u, th, dg, ALU.is_gt, ALU.mult), deps=[tau]))
    t1 = kb.ve(lambda e: e.tensor_tensor(tA, tA, tB, ALU.add), deps=masks)
    t2 = kb.ve(lambda e: e.tensor_tensor(tC, tC, tD, ALU.add), deps=masks[2:])
    t3 = kb.ve(lambda e: e.tensor_tensor(tA, tA, tC, ALU.add), deps=[t1, t2])
    m5 = kb.ve(lambda e: e.tensor_scalar(tB, u, 2.5, 1.0, ALU.is_gt, ALU.mult),
               deps=[t1, t3])
    m6 = kb.ve(lambda e: e.tensor_scalar(tD, u, 3.5, 1.0, ALU.is_gt, ALU.mult),
               deps=[t2, t3])
    m7 = kb.ve(lambda e: e.tensor_scalar(tC, u, 5.0, 2.0, ALU.is_gt, ALU.mult),
               deps=[t3])
    t4 = kb.ve(lambda e: e.tensor_tensor(tB, tB, tD, ALU.add), deps=[m5, m6])
    t5 = kb.ve(lambda e: e.tensor_tensor(tA, tA, tC, ALU.add), deps=[t3, m7])
    tf = kb.ve(lambda e: e.tensor_tensor(tA, tA, tB, ALU.add), deps=[t4, t5])
    tA3 = tA.rearrange("p (a b) -> p a b", b=16)
    tqa = kb.ve(lambda e: e.tensor_tensor(tA3, tA3, bc16(scb), ALU.mult),
                deps=[tf])
    tw = kb.ve(lambda e: e.tensor_tensor(tC, tA, tE, ALU.mult), deps=[tqa])
    return kb.ve(lambda e: e.tensor_tensor(out, tC, tA, ALU.subtract),
                 deps=[tw])


def mk_qscr(kb, pfx, width=IFF):
    scr = {}
    for nm, w_, dt_ in [("bm", width // 16, F32), ("scf", width // 16, F32),
                        ("rcp", width // 16, F32), ("rcpb", width // 16, BF16),
                        ("scb", width // 16, mybir.dt.float16),
                        ("u", width, F32),
                        ("tA", width, mybir.dt.float16),
                        ("tB", width, mybir.dt.float16),
                        ("tC", width, mybir.dt.float16),
                        ("tD", width, mybir.dt.float16),
                        ("tE", width, mybir.dt.float16)]:
        scr[nm] = kb.sb(pfx + "_q_" + nm, (128, w_), dt_, scoped=True)
    return scr


def emit_wquant(kb, kb_s_dst, src_dram, nrows, kwidth, scr, ldbuf, qbuf, deps):
    ntile = nrows // 128
    nkc = kwidth // 128
    toks = []
    prev_q = deps
    prev_tr = kb.p.qtot("qt")
    for ni in range(ntile):
        tld = kb.dma("sync", ldbuf[:, 0:kwidth],
                     src_dram[ni * 128:(ni + 1) * 128, :],
                     deps=prev_q, q="qaux")
        tq = emit_quant(kb, ldbuf[:, 0:kwidth], True, kwidth, qbuf[:, 0:kwidth],
                        scr, [tld, prev_tr])
        trs = []
        for kc in range(nkc):
            trs.append(kb.dmat(
                kb.s[kb_s_dst][:, kc * nrows + ni * 128: kc * nrows + (ni + 1) * 128],
                qbuf[:, kc * 128:(kc + 1) * 128], deps=[tq], q="qt"))
        prev_tr = kb.p.qtot("qt")
        prev_q = [tq]
        toks = trs
    return [kb.p.qtot("qt")]



def emit_p3f(kb):
    s, d = kb.sb, kb.d
    NCH = [512] * 5 + [256]
    s("w_wg", (128, 8 * IFF), BF16, scoped=True)
    s("w_wu", (128, 8 * IFF), BF16, scoped=True)
    for i in range(2):
        s(f"f_x{i}", (128, D), F32, scoped=True)
        kb.s[f"f_xn{i}"] = kb.s[f"f_x{i}"]
        s(f"f_xq{i}", (128, D), BF16, scoped=True)
        s(f"f_xqfm{i}", (128, 8 * 128), BF16, scoped=True, is_transpose_dest=True)
        s(f"f_h{i}", (128, IFF), BF16, scoped=True)
        s(f"f_hq{i}", (128, IFF), BF16, scoped=True)
    s("f_sl", (128, 512), F32, scoped=True)
    s("f_wnat", (128, D), F32, scoped=True)
    scr = mk_qscr(kb, "f")
    scrx = mk_qscr(kb, "fx", width=D)
    pg = kb.psum("pg", (128, 512), scoped=True)
    pu = kb.psum("pu", (128, 512), scoped=True)
    emit_wquant(kb, "w_wg", kb.d["wg_nat"], IFF, D, scr, kb.s["f_wnat"],
                kb.s["f_hq0"], [])
    tw_end = emit_wquant(kb, "w_wu", kb.d["wu_nat"], IFF, D, scr,
                         kb.s["f_wnat"], kb.s["f_hq0"], [])
    twq = kb.p.qtot("qt")
    tap_xq = kb.tap("xq", (T, D), BF16)
    tap_hq = kb.tap("hq", (128, IFF), BF16)

    x0 = d["x0"]
    free_tok = {}
    free_tok[("hqv", 0)] = twq
    quant_free = kb.p.qtot("ev")
    qf_x = quant_free
    sl_free = None
    pgpu_free = None
    for ti in range(NT):
        bi = ti % 2
        xb = kb.s[f"f_x{bi}"]
        xn = kb.s[f"f_xn{bi}"]
        xq = kb.s[f"f_xq{bi}"]
        xqfm = kb.s[f"f_xqfm{bi}"]
        hh = kb.s[f"f_h{bi}"]
        hq = kb.s[f"f_hq{bi}"]
        tld = kb.dma("sync", xb, x0[ti * 128:(ti + 1) * 128, :],
                     deps=[free_tok.get(("x", bi))], q=f"qx{bi}")
        c_mean = kb.s["ln_mean"][:, ti:ti + 1]
        c_inv = kb.s["ln_inv"][:, ti:ti + 1]
        txn = kb.ve(lambda e, xb=xb, xn=xn, m=c_mean, iv=c_inv:
                    e.scalar_tensor_tensor(xn, xb, m, iv.to_broadcast([128, D]),
                                           ALU.subtract, ALU.mult),
                    deps=[tld, free_tok.get(("xn", bi))])
        tq = emit_quant(kb, xn, True, D, xq, scrx,
                        [txn, qf_x, free_tok.get(("xq", bi)),
                         free_tok.get(("xq_t", bi))])
        qf_x = tq
        free_tok[("x", bi)] = tq
        free_tok[("xn", bi)] = tq
        ttr = []
        for kc in range(8):
            ttr.append(kb.dmat(xqfm[:, kc * 128:(kc + 1) * 128],
                               xq[:, kc * 128:(kc + 1) * 128],
                               deps=[tq, free_tok.get(("xqfm", bi))],
                               q=f"qt{bi}"))
        free_tok[("xq", bi)] = ttr[-1]
        th_toks = []
        hq_free = free_tok.get(("hq", bi))
        for ci, cw in enumerate(NCH):
            co = sum(NCH[:ci])
            tmmg = tmmu = None
            for kc in range(8):
                lhs = xqfm[:, kc * 128:(kc + 1) * 128]
                rg = kb.s["w_wg"][:, kc * IFF + co: kc * IFF + co + cw]
                ru = kb.s["w_wu"][:, kc * IFF + co: kc * IFF + co + cw]
                dep0 = ttr + [twq, pgpu_free] if kc == 0 else []
                tmmg = kb.te(lambda e, l_=lhs, r_=rg, kc=kc, cw=cw: e.matmul(
                    pg[:, 0:cw], l_, r_, start=(kc == 0), stop=(kc == 7)),
                    deps=dep0)
                tmmu = kb.te(lambda e, l_=lhs, r_=ru, kc=kc, cw=cw: e.matmul(
                    pu[:, 0:cw], l_, r_, start=(kc == 0), stop=(kc == 7)),
                    deps=[])
            tsl = kb.sc(lambda e, cw=cw: e.activation(
                kb.s["f_sl"][:, 0:cw], pg[:, 0:cw], ACT.Silu),
                deps=[tmmg, tmmu, sl_free])
            th = kb.ve(lambda e, hh=hh, co=co, cw=cw: e.tensor_tensor(
                hh[:, co:co + cw], pu[:, 0:cw], kb.s["f_sl"][:, 0:cw],
                ALU.mult), deps=[tsl, hq_free])
            sl_free = th
            pgpu_free = th
            th_toks.append(th)
        tqh = emit_quant(kb, hh, False, IFF, hq, scr,
                         th_toks + [quant_free, free_tok.get(("hqv", bi)),
                                    free_tok.get(("hqv_t", bi))])
        quant_free = tqh
        tsp = kb.dma("sync", d["s_hq"][ti * 128:(ti + 1) * 128, :], hq,
                     deps=[tqh], q=f"qsp{bi}")
        if tap_xq is not None:
            kb.dma("sync", tap_xq[ti * 128:(ti + 1) * 128, :], xq,
                   deps=[tq], q="qout")
            free_tok[("xq_t", bi)] = kb.p.qtot("qout")
        if tap_hq is not None and ti == 3:
            kb.dma("sync", tap_hq, hq, deps=[tqh], q="qout")
        free_tok[("hqv", bi)] = kb.p.qtot(f"qsp{bi}")
        free_tok[("hqv_t", bi)] = (kb.p.qtot("qout") if tap_hq is not None
                                   else None)
        free_tok[("xqfm", bi)] = kb.p.qtot("et")
    kb.tok["p3f"] = [kb.p.qtot("qsp0"), kb.p.qtot("qsp1")]


def emit_p3fb(kb):
    s, d = kb.sb, kb.d
    s("w_wd", (128, 22 * 1024), BF16, scoped=True)
    for i in range(2):
        s(f"fb_hqfm{i}", (128, 22 * 128), BF16, scoped=True,
          is_transpose_dest=True)
        s(f"fb_out{i}", (128, D), F32, scoped=True)
    s("fb_wnat", (128, IFF), F32, scoped=True)
    s("fb_wq", (128, IFF), BF16, scoped=True)
    pd = [kb.psum(f"pd{i}", (128, 512), scoped=True) for i in range(2)]
    scr = mk_qscr(kb, "fb")
    emit_wquant(kb, "w_wd", kb.d["wd_nat"], D, IFF, scr, kb.s["fb_wnat"],
                kb.s["fb_wq"], [])
    twq = kb.p.qtot("qt")
    tap_ffn = kb.tap("ffn0", (T, D))
    free_tok = {}
    pd_free = [None, None]
    for ti in range(NT):
        bi = ti % 2
        hqfm = kb.s[f"fb_hqfm{bi}"]
        fout = kb.s[f"fb_out{bi}"]
        ttr2 = []
        for kc in range(22):
            ttr2.append(kb.dmat(
                hqfm[:, kc * 128:(kc + 1) * 128],
                d["s_hq"][ti * 128:(ti + 1) * 128, kc * 128:(kc + 1) * 128],
                deps=[free_tok.get(("hqfm", bi))] + kb.tok["p3f"],
                q=f"qt{bi}"))
        ev_toks = []
        for half in range(2):
            tmm = None
            for kc in range(22):
                lhs = hqfm[:, kc * 128:(kc + 1) * 128]
                rd_ = kb.s["w_wd"][:, kc * 1024 + half * 512:
                                   kc * 1024 + half * 512 + 512]
                tmm = kb.te(lambda e, l_=lhs, r_=rd_, kc=kc, half=half: e.matmul(
                    pd[half], l_, r_, start=(kc == 0), stop=(kc == 21)),
                    deps=ttr2 + [twq, pd_free[half]] if kc == 0 else [])
            tev = kb.ve(lambda e, fout=fout, half=half: e.tensor_copy(
                fout[:, half * 512:(half + 1) * 512], pd[half]),
                deps=[tmm, free_tok.get(("fout", bi)),
                      free_tok.get(("fout_t", bi))])
            pd_free[half] = tev
            ev_toks.append(tev)
        free_tok[("hqfm", bi)] = kb.p.qtot("et")
        kb.dma("sync", d["s_ffn0"][ti * 128:(ti + 1) * 128, :], fout,
               deps=ev_toks, q=f"qsp{bi}")
        if tap_ffn is not None:
            kb.dma("sync", tap_ffn[ti * 128:(ti + 1) * 128, :], fout,
                   deps=ev_toks, q="qout")
        free_tok[("fout", bi)] = kb.p.qtot(f"qsp{bi}")
        free_tok[("fout_t", bi)] = (kb.p.qtot("qout") if tap_ffn is not None
                                    else None)
    kb.tok["p3fb"] = [kb.p.qtot("qsp0"), kb.p.qtot("qsp1")]






def emit_p3a(kb):
    s, d = kb.sb, kb.d
    s("a_encq", (128, 8 * 1024), BF16, scoped=True)
    s("a_encv", (128, 8 * 1024), BF16, scoped=True)
    s("a_dec", (128, 8 * 1024), BF16, scoped=True)
    wload_chunked(kb, "a_encq", "encq", 8, 1024)
    wload_chunked(kb, "a_encv", "encv", 8, 1024)
    wload_chunked(kb, "a_dec", "dec", 8, 1024)
    kb.p.base = kb.p.base + [kb.p.qtot("qw")]
    for i in range(2):
        s(f"a_x{i}", (128, D), F32, scoped=True)
        s(f"a_ff{i}", (128, D), F32, scoped=True)
        s(f"a_fm{i}", (128, 8 * 128), BF16, scoped=True, is_transpose_dest=True)
        s(f"a_q{i}", (128, 8 * 128), BF16, scoped=True)
        s(f"a_v{i}", (128, 8 * 128), BF16, scoped=True)
        s(f"a_fm1{i}", (128, 8 * 128), BF16, scoped=True, is_transpose_dest=True)
    s("a_vmid", (128, 8 * 128), BF16, scoped=True)
    s("a_xn", (128, D), F32, scoped=True)
    s("a_xnb", (128, D), BF16, scoped=True)
    s("a_xn1b", (128, D), BF16, scoped=True)
    s("a_mqn", (128, 8 * 192), BF16, scoped=True)
    s("a_mq2", (128, 8 * 128), BF16, scoped=True)
    s("a_nsq", (128, 8 * 128), BF16, scoped=True)
    s("a_aring", (1, 8 * 192), BF16, scoped=True)
    s("a_crow", (1, 8 * 128), BF16, scoped=True)
    s("a_scol", (128, 16), F32, scoped=True)
    s("a_nsqc", (128, 8), F32, scoped=True)
    s("a_sct", (16, 128), BF16, scoped=True)
    s("a_u", (128, 8 * 192), F32, scoped=True)
    s("a_s1", (128, 8 * 192), F32, scoped=True)
    s("a_s2", (128, 8 * 192), F32, scoped=True)
    s("a_m8", (128, 8), F32, scoped=True)
    s("a_eall", (128, 8 * 192), F32, scoped=True)
    s("a_sall", (128, NH), F32, scoped=True)
    s("a_suns", (128, NH), F32, scoped=True)
    s("a_srec", (128, NH), F32, scoped=True)
    s("a_wsel", (128, 8 * 128), BF16, scoped=True)
    s("a_wst", (128, 8 * 128), BF16, scoped=True, is_transpose_dest=True)
    s("a_os", (128, 8 * 128), BF16, scoped=True)
    s("a_xc", (128, D), F32, scoped=True)
    s("a_stt", (128, 8), F32, scoped=True)
    pD = [kb.psum(f"pD{i}", (128, 256), F32, scoped=True) for i in range(4)]
    pS = kb.psum("pS", (128, 16), F32, scoped=True)
    pS2 = kb.psum("pS2", (128, 128), F32, scoped=True)
    pdec = kb.psum("pdec", (128, 512), F32, scoped=True)
    pdec2 = kb.psum("pdec2", (128, 512), F32, scoped=True)
    x0 = d["x0"]
    tap_attn = kb.tap("attn0", (T, D))
    tap_x1 = kb.tap("x1", (T, D))
    tap_xn1 = kb.tap("xn1fm", (D, T), BF16)

    zs = []
    zs.append(kb.ve(lambda e: e.memset(kb.s["a_mqn"], 0.0)))
    zs.append(kb.ve(lambda e: e.memset(kb.s["a_aring"], 0.0)))
    zs.append(kb.ve(lambda e: e.memset(kb.s["a_v0"], 0.0)))
    zs.append(kb.ve(lambda e: e.memset(kb.s["a_v1"], 0.0)))
    zs.append(kb.ve(lambda e: e.memset(kb.s["a_u"], 0.0)))
    zq = zs[-1]

    def hd(ap, h, w=128):
        return ap[:, h * w:(h + 1) * w]

    def bch(ap, width, nrep=8, off=0):
        return bass.AP(ap.tensor, off, [[ap.tensor.shape[1], 128],
                                        [0, nrep], [1, width]])

    ft = {}
    mq_last = None
    ar_last = zq
    ctx_free = [None] * 8
    dec_free = [None, None]
    sc_free = None
    u_free = None
    w_free = None
    os_free = None
    for ti in range(NT):
        bi = ti % 2
        T0 = ti * 128
        first = (ti == 0)
        WD = 192 if first else 128
        m1c = kb.s["m_dm1t0" if first else "m_dm1"]
        m2c = kb.s["m_dm2t0" if first else "m_dm2"]
        xb, ffb = kb.s[f"a_x{bi}"], kb.s[f"a_ff{bi}"]
        fmb = kb.s[f"a_fm{bi}"]
        qf, vb = kb.s[f"a_q{bi}"], kb.s[f"a_v{bi}"]
        vprev = kb.s[f"a_v{1 - bi}"]
        fm1 = kb.s[f"a_fm1{bi}"]
        tldx = kb.dma("sync", xb, x0[T0:T0 + 128, :],
                      deps=[ft.get(("x", bi)), zq], q=f"qx{bi}")
        tldf = kb.dma("sync", ffb, d["s_ffn0"][T0:T0 + 128, :],
                      deps=[ft.get(("ff", bi)), zq] + kb.tok["p3fb"], q=f"qx{bi}")
        tldx = tldf = kb.p.qtot(f"qx{bi}")
        c_mean = kb.s["ln_mean"][:, ti:ti + 1]
        c_inv = kb.s["ln_inv"][:, ti:ti + 1]
        txn = kb.ve(lambda e, xb=xb, m=c_mean, iv=c_inv: e.scalar_tensor_tensor(
            kb.s["a_xn"], xb, m, iv.to_broadcast([128, D]),
            ALU.subtract, ALU.mult), deps=[tldx, ft.get(("xn",))])
        txc = kb.ve(lambda e: e.tensor_copy(kb.s["a_xnb"], kb.s["a_xn"]),
                    deps=[txn, ft.get(("xnb",))])
        ft[("xn",)] = txc
        ttr = []
        for kc in range(8):
            ttr.append(kb.dmat(fmb[:, kc * 128:(kc + 1) * 128],
                               kb.s["a_xnb"][:, kc * 128:(kc + 1) * 128],
                               deps=[txc, ft.get(("fm", bi))], q=f"qt{bi}"))
        ft[("xnb",)] = ttr[-1]
        tq_toks = []
        for h in range(NH):
            pb = pD[h // 2][:, (h % 2) * 128:(h % 2) * 128 + 128]
            tmm = None
            for kc in range(8):
                lhs = kb.s["a_encq"][:, kc * 1024 + h * 128: kc * 1024 + (h + 1) * 128]
                rhs = fmb[:, kc * 128:(kc + 1) * 128]
                tmm = kb.te(lambda e, pb=pb, l_=lhs, r_=rhs, kc=kc: e.matmul(
                    pb, l_, r_, start=(kc == 0), stop=(kc == 7)),
                    deps=ttr + [ctx_free[h], os_free] +
                    ([tq_toks[-1]] if h % 2 == 1 else []) if kc == 0 else [])
            t1 = kb.sc(lambda e, pb=pb, h=h, qf=qf: e.activation(
                hd(qf, h), pb, ACT.Relu), deps=[tmm, ft.get(("qf", bi))])
            tq_toks.append(t1)
        ft[("qf", bi)] = None
        tv_toks = []
        for half in range(2):
            pb = pdec if half == 0 else pdec2
            tmm = None
            for kc in range(8):
                lhs = fmb[:, kc * 128:(kc + 1) * 128]
                rhs = kb.s["a_encv"][:, kc * 1024 + half * 512:
                                     kc * 1024 + half * 512 + 512]
                tmm = kb.te(lambda e, pb=pb, l_=lhs, r_=rhs, kc=kc: e.matmul(
                    pb, l_, r_, start=(kc == 0), stop=(kc == 7)),
                    deps=ttr + [dec_free[half]] if kc == 0 else [])
            t1 = kb.sc(lambda e, pb=pb, half=half, vb=vb: e.activation(
                vb[:, half * 512:(half + 1) * 512], pb, ACT.Relu),
                deps=[tmm, ft.get(("vb", bi))])
            tv_toks.append(t1)
        ft[("vb", bi)] = None
        tvm1 = kb.dma("gpsimd", kb.s["a_vmid"][0:64, :], vprev[64:128, :],
                      deps=[ft.get(("vmid",)), zq, kb.p.qtot("qga")], q="qga")
        tvm2 = kb.dma("gpsimd", kb.s["a_vmid"][64:128, :], vb[0:64, :],
                      deps=tv_toks + [kb.p.qtot("qga")], q="qga")
        tvm = kb.p.qtot("qga")
        ttail = kb.ve(lambda e: e.tensor_copy(
            bass.AP(kb.s["a_mqn"].tensor, 0, [[8 * 192, 128], [192, 8], [1, 64]]),
            bass.AP(kb.s["a_mqn"].tensor, 128, [[8 * 192, 128], [192, 8], [1, 64]])),
            deps=[mq_last, zq])
        tmq = []
        for h in range(NH):
            t1 = kb.ve(lambda e, h=h, qf=qf: e.tensor_scalar(
                kb.s["a_mqn"][:, h * 192 + 64: h * 192 + 192], hd(qf, h),
                kb.s["metcol2"][:, h:h + 1], None, ALU.mult),
                deps=[tq_toks[h], ttail] + kb.tok["p2"], hazard=True)
            tmq.append(t1)
        mqcur = bass.AP(kb.s["a_mqn"].tensor, 64, [[8 * 192, 128], [192, 8], [1, 128]])
        qf8 = bass.AP(qf.tensor, 0, [[8 * 128, 128], [128, 8], [1, 128]])
        tmq2 = kb.ve(lambda e, qf8=qf8, mqcur=mqcur: e.tensor_tensor(
            kb.s["a_mq2"].rearrange("p (h c) -> p h c", c=128), qf8, mqcur,
            ALU.mult), deps=tmq + [ft.get(("mq2",))])
        tnsq = kb.ve(lambda e, qf=qf: e.tensor_tensor(kb.s["a_nsq"], qf, qf, ALU.mult),
                     deps=tq_toks + [ft.get(("nsq",))])
        tac = []
        tprev = sc_free
        for h in range(NH):
            t1 = kb.te(lambda e, h=h: e.matmul(
                pS[:, h:h + 1], hd(kb.s["a_mq2"], h), kb.s["c_ones_col"],
                start=True, stop=True),
                deps=[tmq2, tprev] if h == 0 else [tmq2])
            t2 = kb.te(lambda e, h=h: e.matmul(
                pS[:, 8 + h:9 + h], hd(kb.s["a_nsq"], h), kb.s["c_ones_col"],
                start=True, stop=True), deps=[tnsq] if h == 0 else [])
            tac += [t1, t2]
        tsc1 = kb.ve(lambda e: e.tensor_scalar(
            kb.s["a_scol"][:, 0:8], pS[:, 0:8], -0.5, None, ALU.mult),
            deps=tac + [sc_free])
        tsc2 = kb.ve(lambda e: e.tensor_copy(kb.s["a_nsqc"], pS[:, 8:16]),
                     deps=tac + [sc_free])
        sc_free = tsc2
        tc1 = kb.sc(lambda e: e.activation(kb.s["a_stt"], kb.s["a_nsqc"],
                                           ACT.Ln, bias=1e-24),
                    deps=[tsc2, ft.get(("stt",))])
        tc2 = kb.sc(lambda e: e.activation(kb.s["a_stt"], kb.s["a_stt"],
                                           ACT.Exp, scale=0.5),
                    deps=[tc1], hazard=True)
        tc3 = kb.ve(lambda e: e.tensor_scalar(kb.s["a_stt"], kb.s["a_stt"],
                                              1e-12, None, ALU.max), deps=[tc2])
        tc4 = kb.ve(lambda e: e.reciprocal(kb.s["a_stt"], kb.s["a_stt"]),
                    deps=[tc3], hazard=True)
        gate_c = float(np.float32(0.2) * np.float32(0.2))
        tc5 = kb.ve(lambda e: e.tensor_scalar(
            kb.s["a_scol"][:, 8:16], kb.s["a_nsqc"], gate_c, 0.1,
            ALU.is_gt, ALU.mult), deps=[tsc2, tsc1])
        tc6 = kb.ve(lambda e: e.tensor_tensor(
            kb.s["a_scol"][:, 8:16], kb.s["a_scol"][:, 8:16], kb.s["a_stt"],
            ALU.mult), deps=[tc4, tc5], hazard=True)
        ft[("stt",)] = tc6
        tT = kb.te(lambda e: e.transpose(pS2[0:16, 0:128], kb.s["a_scol"],
                                         kb.s["c_identf"]),
                   deps=[tsc1, tc6, ft.get(("pS2",))])
        tev = kb.ve(lambda e: e.tensor_copy(kb.s["a_sct"], pS2[0:16, 0:128]),
                    deps=[tT, ft.get(("sct",))])
        ft[("pS2",)] = tev
        tat = kb.ve(lambda e: e.tensor_copy(
            bass.AP(kb.s["a_aring"].tensor, 0, [[8 * 192, 1], [192, 8], [1, 64]]),
            bass.AP(kb.s["a_aring"].tensor, 128, [[8 * 192, 1], [192, 8], [1, 64]])),
            deps=[ar_last])
        tda = kb.dma("gpsimd", bass.AP(kb.s["a_aring"].tensor, 64,
                                       [[8 * 192, 1], [192, 8], [1, 128]]),
                     kb.s["a_sct"][0:8, :], deps=[tev, tat, kb.p.qtot("qga")],
                     q="qga")
        tdc = kb.dma("gpsimd", kb.s["a_crow"].rearrange("p (h c) -> p h c", c=128),
                     kb.s["a_sct"][8:16, :],
                     deps=[tev, ft.get(("crow",)), kb.p.qtot("qga")], q="qga")
        tdq = kb.p.qtot("qga")
        ft[("sct",)] = tdq
        ar_last = tdq
        td_toks = []
        for h in range(NH):
            pb = pD[h // 2][:, (h % 2) * 128:(h % 2) * 128 + 128]
            mq_w = kb.s["a_mqn"][:, h * 192: h * 192 + 192]
            ar_w = kb.s["a_aring"][0:1, h * 192: h * 192 + 192]
            deps0 = [tmq[h], tdq, tq_toks[h | 1], u_free, ctx_free[h]]
            t1 = kb.te(lambda e, pb=pb, h=h, mq_w=mq_w, qf=qf: e.matmul(
                pb[0:64, :], hd(qf, h)[:, 0:64], mq_w[:, 0:128],
                start=True, stop=False), deps=deps0)
            kb.te(lambda e, pb=pb, h=h, ar_w=ar_w: e.matmul(
                pb[0:64, :], ar_w[:, 64:128], kb.s["c_ones_row"],
                start=False, stop=False), deps=[])
            kb.te(lambda e, pb=pb, h=h, ar_w=ar_w: e.matmul(
                pb[0:64, :], kb.s["c_ones_row"][:, 0:64], ar_w[:, 0:128],
                start=False, stop=True), deps=[])
            kb.te(lambda e, pb=pb, h=h, mq_w=mq_w, qf=qf: e.matmul(
                pb[64:128, :], hd(qf, h)[:, 64:128], mq_w[:, 64:192],
                start=True, stop=False, tile_position=(0, 64)), deps=[])
            kb.te(lambda e, pb=pb, h=h, ar_w=ar_w: e.matmul(
                pb[64:128, :], ar_w[:, 128:192], kb.s["c_ones_row"],
                start=False, stop=False, tile_position=(0, 64)), deps=[])
            t6 = kb.te(lambda e, pb=pb, h=h, ar_w=ar_w: e.matmul(
                pb[64:128, :], kb.s["c_ones_row"][:, 0:64], ar_w[:, 64:192],
                start=False, stop=True, tile_position=(0, 64)), deps=[])
            td_toks.append(t6)
        tu_toks = []
        for b_ in range(4):
            srcv = pD[b_].rearrange("p (h c) -> p h c", c=128)
            dstv = bass.AP(kb.s["a_u"].tensor, (2 * b_) * WD,
                           [[8 * 192, 128], [WD, 2], [1, 128]])
            t1 = kb.ve(lambda e, b_=b_, srcv=srcv, dstv=dstv, m1c=m1c: e.tensor_tensor(
                dstv, srcv, bch(m1c, 128, 2), ALU.mult),
                deps=td_toks[2 * b_:2 * b_ + 2] + [u_free], hazard=True)
            tu_toks.append(t1)
        UW = 8 * WD
        tln = kb.sc(lambda e, UW=UW: e.activation(kb.s["a_u"][:, 0:UW],
                                           kb.s["a_u"][:, 0:UW], ACT.Ln,
                                           bias=1e-8), deps=tu_toks)
        texp = kb.sc(lambda e, UW=UW: e.activation(kb.s["a_u"][:, 0:UW],
                                            kb.s["a_u"][:, 0:UW], ACT.Exp,
                                            scale=0.5), deps=[tln], hazard=True)
        sv = bass.AP(kb.s["a_s1"].tensor, 0, [[8 * 192, 128], [WD, 8], [1, 128]])
        uv = bass.AP(kb.s["a_u"].tensor, 0, [[8 * 192, 128], [WD, 8], [1, 128]])
        ts1 = kb.ve(lambda e, sv=sv, uv=uv, m2c=m2c: e.tensor_tensor(sv, bch(m2c, 128), uv,
                                              ALU.subtract),
                    deps=[texp, ft.get(("s1",))])
        sdeps = [ts1]
        if first:
            for h in range(NH):
                d0 = kb.s["a_u"][:, h * WD + 64: h * WD + 65]
                t1 = kb.ve(lambda e, h=h, d0=d0, WD=WD: e.tensor_tensor(
                    kb.s["a_s1"][:, h * WD + 128: h * WD + 192],
                    kb.s["m_ddup"], d0.to_broadcast([128, 64]), ALU.subtract),
                    deps=[texp, ts1])
                sdeps.append(t1)
        te_toks = []
        for h in range(NH):
            t1 = kb.sc(lambda e, h=h, WD=WD: e.activation(
                kb.s["a_eall"][:, h * WD:(h + 1) * WD],
                kb.s["a_s1"][:, h * WD:(h + 1) * WD], ACT.Exp,
                accum_out=kb.s["a_sall"][:, h:h + 1]),
                deps=sdeps + [ft.get(("eall",))])
            te_toks.append(t1)
        tmr_last = []
        for h in range(NH):
            cur, nxt = kb.s["a_s1"], kb.s["a_s2"]
            tprev = te_toks[h]
            for rnd in range(4):
                tm8 = kb.ve(lambda e, h=h, cur=cur, WD=WD: e.max(
                    out=kb.s["a_m8"], in_=cur[:, h * WD:(h + 1) * WD]),
                    deps=[tprev, ft.get(("m8",))], hazard=True)
                tmr = kb.ve(lambda e, h=h, cur=cur, nxt=nxt, WD=WD: e.match_replace(
                    out=nxt[:, h * WD:(h + 1) * WD], in_to_replace=kb.s["a_m8"],
                    in_values=cur[:, h * WD:(h + 1) * WD], imm_value=SENT),
                    deps=[tm8, ft.get(("s2", h)) if rnd == 0 else None],
                    hazard=True)
                ft[("m8",)] = tmr
                tprev = tmr
                cur, nxt = nxt, cur
            tmr_last.append(tprev)
            ft[("s2", h)] = tprev
        tw_toks = []
        for h in range(NH):
            t1 = kb.sc(lambda e, h=h, WD=WD: e.activation(
                kb.s["a_s2"][:, h * WD:(h + 1) * WD],
                kb.s["a_s1"][:, h * WD:(h + 1) * WD], ACT.Exp,
                accum_out=kb.s["a_suns"][:, h:h + 1]),
                deps=[tmr_last[h], ft.get(("s2e", h))], hazard=True)
            tw_toks.append(t1)
        tS = kb.ve(lambda e: e.tensor_tensor(kb.s["a_srec"], kb.s["a_sall"],
                                             kb.s["a_suns"], ALU.subtract),
                   deps=tw_toks + te_toks + [ft.get(("srec",))])
        tS = kb.ve(lambda e: e.tensor_scalar(kb.s["a_srec"], kb.s["a_srec"],
                                             1e-8, None, ALU.add),
                   deps=[tS], hazard=True)
        tS = kb.ve(lambda e: e.reciprocal(kb.s["a_srec"], kb.s["a_srec"]),
                   deps=[tS], hazard=True)
        ft[("srec",)] = tS
        twn = []
        for h in range(NH):
            tww = kb.ve(lambda e, h=h, WD=WD: e.tensor_tensor(
                kb.s["a_eall"][:, h * WD:(h + 1) * WD],
                kb.s["a_eall"][:, h * WD:(h + 1) * WD],
                kb.s["a_s2"][:, h * WD:(h + 1) * WD], ALU.subtract),
                deps=[tw_toks[h], tS], hazard=True)
            if first:
                tdup = kb.ve(lambda e, h=h, WD=WD: e.reduce_sum(
                    kb.s["a_stt"][:, 0:1],
                    kb.s["a_eall"][:, h * WD + 128: h * WD + 192], axis=AXX),
                    deps=[tww, ft.get(("stt",))], hazard=True)
                tww = kb.ve(lambda e, h=h, WD=WD: e.tensor_tensor(
                    kb.s["a_eall"][:, h * WD + 64: h * WD + 65],
                    kb.s["a_eall"][:, h * WD + 64: h * WD + 65],
                    kb.s["a_stt"][:, 0:1], ALU.add), deps=[tdup], hazard=True)
                ft[("stt",)] = tww
            t2 = kb.ve(lambda e, h=h, WD=WD: e.tensor_scalar(
                hd(kb.s["a_wsel"], h), kb.s["a_eall"][:, h * WD: h * WD + 128],
                kb.s["a_srec"][:, h:h + 1], None, ALU.mult),
                deps=[tww, tS, ft.get(("wsel", h))], hazard=True)
            twn.append(t2)
        ft[("eall",)] = twn[-1]
        ft[("s1",)] = twn[-1]
        u_free = twn[-1]
        twt = []
        for h in range(NH):
            twt.append(kb.dmat(hd(kb.s["a_wst"], h), hd(kb.s["a_wsel"], h),
                               deps=[twn[h], ft.get(("wst", h))], q=f"qt{bi}"))
        tw_all = kb.p.qtot(f"qt{bi}")
        for h in range(NH):
            twt[h] = tw_all
            ft[("wsel", h)] = tw_all
        tos = []
        for h in range(NH):
            pb = pD[h // 2][:, (h % 2) * 128:(h % 2) * 128 + 128]
            tr1 = kb.te(lambda e, h=h: e.matmul(
                pS2[:, 0:128], kb.s["c_ones_row"],
                kb.s["a_crow"][0:1, h * 128:(h + 1) * 128],
                start=True, stop=True), deps=[tdq, ft.get(("pS2",))])
            trh = kb.ve(lambda e, h=h, qf=qf: e.tensor_tensor(
                hd(kb.s["a_os"], h), hd(qf, h), pS2[:, 0:128], ALU.mult),
                deps=[tr1, tq_toks[h], os_free], hazard=True)
            ft[("pS2",)] = trh
            th1 = kb.te(lambda e, h=h, pb=pb: e.matmul(
                pb, hd(kb.s["w_hm"], h), hd(kb.s["a_os"], h),
                start=True, stop=False),
                deps=[trh, tu_toks[h // 2]] +
                ([tos[-1]] if h % 2 == 1 else []))
            ta_ = kb.te(lambda e, h=h, pb=pb: e.matmul(
                pb[:, 0:64], hd(kb.s["a_vmid"], h), hd(kb.s["a_wst"], h)[:, 0:64],
                start=False, stop=False), deps=[twt[h], tvm])
            tb_ = kb.te(lambda e, h=h, vb=vb, pb=pb: e.matmul(
                pb[:, 64:128], hd(vb, h), hd(kb.s["a_wst"], h)[:, 64:128],
                start=False, stop=True), deps=[tv_toks[-1]])
            t2 = kb.ve(lambda e, h=h, pb=pb, qf=qf: e.tensor_tensor(
                hd(kb.s["a_os"], h), hd(qf, h), pb, ALU.mult),
                deps=[ta_, tb_, trh], hazard=True)
            tos.append(t2)
            ctx_free[h] = t2
            ft[("wst", h)] = t2
        os_free = tos[-1]
        tcmb = []
        for half in range(2):
            pb = pdec if half == 0 else pdec2
            tmm = None
            for h in range(NH):
                lhs = hd(kb.s["a_os"], h)
                rhs = kb.s["a_dec"][:, h * 1024 + half * 512: h * 1024 + half * 512 + 512]
                tmm = kb.te(lambda e, pb=pb, l_=lhs, r_=rhs, h=h: e.matmul(
                    pb, l_, r_, start=(h == 0), stop=(h == 7)),
                    deps=tos + [dec_free[half]] if h == 0 else [])
            t1 = kb.ve(lambda e, pb=pb, half=half, xb=xb: e.scalar_tensor_tensor(
                kb.s["a_xc"][:, half * 512:(half + 1) * 512], pb, 1.0,
                xb[:, half * 512:(half + 1) * 512], ALU.mult, ALU.add),
                deps=[tmm, tldx, ft.get(("x1",)), ft.get(("xc",)),
                      ft.get(("x1so",))],
                hazard=True)
            t2 = kb.ve(lambda e, half=half, ffb=ffb: e.tensor_tensor(
                kb.s["a_xc"][:, half * 512:(half + 1) * 512],
                kb.s["a_xc"][:, half * 512:(half + 1) * 512],
                ffb[:, half * 512:(half + 1) * 512], ALU.add),
                deps=[t1, tldf], hazard=True)
            dec_free[half] = t2
            tcmb.append(t2)
        ft[("x", bi)] = tcmb[-1]
        ft[("ff", bi)] = tcmb[-1]
        tsx = kb.dma("sync", d["s_x1"][VHALO + T0: VHALO + T0 + 128, :],
                     kb.s["a_xc"], deps=tcmb, q=f"qsp{bi}")
        if tap_attn is not None:
            kb.dma("sync", tap_attn[T0:T0 + 128, :], kb.s["a_xc"],
                   deps=tcmb, q="qout")
        if tap_x1 is not None:
            kb.dma("sync", tap_x1[T0:T0 + 128, :], kb.s["a_xc"],
                   deps=tcmb, q="qout")
        if tap_x1 is not None or tap_attn is not None:
            ft[("x1so",)] = kb.p.qtot("qout")
        cm1 = kb.s["ln1_mean"][:, ti:ti + 1]
        ci1 = kb.s["ln1_inv"][:, ti:ti + 1]
        t1 = kb.ve(lambda e, cm1=cm1: e.reduce_sum(cm1, kb.s["a_xc"],
                                                   axis=AXX),
                   deps=tcmb + [ft.get(("stt",))], hazard=True)
        tsq = kb.sc(lambda e: e.square(kb.s["a_xn"], kb.s["a_xc"]),
                    deps=tcmb + [ttr[-1], ft.get(("xn",))])
        t2 = kb.ve(lambda e, ci1=ci1: e.reduce_sum(ci1, kb.s["a_xn"],
                                                   axis=AXX), deps=[tsq])
        ft[("xn",)] = t2
        tm_ = kb.ve(lambda e, cm1=cm1: e.tensor_scalar(
            cm1, cm1, 1.0 / 1024.0, None,
            ALU.mult), deps=[t1], hazard=True)
        tm2 = kb.ve(lambda e, cm1=cm1: e.tensor_tensor(
            kb.s["a_stt"][:, 3:4], cm1, cm1,
            ALU.mult), deps=[tm_, ft.get(("stt",))], hazard=True)
        tvv = kb.ve(lambda e, ci1=ci1: e.scalar_tensor_tensor(
            ci1, ci1, 1.0 / 1024.0,
            kb.s["a_stt"][:, 3:4], ALU.mult, ALU.subtract),
            deps=[t2, tm2], hazard=True)
        tl_ = kb.sc(lambda e, ci1=ci1: e.activation(ci1, ci1, ACT.Ln,
                                           bias=1e-5), deps=[tvv], hazard=True)
        tiv = kb.sc(lambda e, ci1=ci1: e.activation(ci1, ci1, ACT.Exp,
                                           scale=-0.5), deps=[tl_], hazard=True)
        txn1 = kb.ve(lambda e, cm1=cm1, ci1=ci1: e.scalar_tensor_tensor(
            kb.s["a_xn1b"], kb.s["a_xc"], cm1,
            ci1.to_broadcast([128, D]),
            ALU.subtract, ALU.mult), deps=[tiv, tm_, ft.get(("xn1b",))],
            hazard=True)
        ft[("stt",)] = txn1
        ft[("x1",)] = txn1
        ttr1 = []
        for kc in range(8):
            ttr1.append(kb.dmat(fm1[:, kc * 128:(kc + 1) * 128],
                                kb.s["a_xn1b"][:, kc * 128:(kc + 1) * 128],
                                deps=[txn1, ft.get(("fm1", bi)),
                                      ft.get(("fm1o",))], q=f"qt{bi}"))
        ft[("xn1b",)] = ttr1[-1]
        dst = bass.AP(d["s_xn1fm"].tensor, 1 + VHALO + T0,
                      [[XNW, 128], [128 * XNW, 8], [1, 128]])
        tso = kb.dma("sync", dst, fm1, deps=ttr1, q=f"qsp{bi}")
        if tap_xn1 is not None:
            dst2 = bass.AP(kb.tap_ap_xn1.tensor if False else tap_xn1.tensor, T0,
                           [[T, 128], [128 * T, 8], [1, 128]])
            kb.dma("sync", dst2, fm1, deps=ttr1, q="qout")
            ft[("fm1o",)] = kb.p.qtot("qout")
        ft[("fm1", bi)] = kb.p.qtot(f"qsp{bi}")
        ft[("xc",)] = ft[("fm1", bi)]
        ft[("fm", bi)] = kb.p.qtot("et")
        ft[("mq2",)] = kb.p.qtot("et")
        ft[("nsq",)] = kb.p.qtot("et")
        ft[("crow",)] = kb.p.qtot("et")
        ft[("vmid",)] = kb.p.qtot("et")
        mq_last = td_toks[-1]
    kb.tok["p3a"] = [kb.p.qtot("qsp0"), kb.p.qtot("qsp1")]





def _kv_groups(l):
    lo = 0 if l == 0 else QLO0
    gs = []
    v = lo
    while v < VRNG:
        w = min(512, VRNG - v)
        gs.append((v, w))
        v += w
    return gs


def emit_p4pre(kb):
    s, d = kb.sb, kb.d
    s("z_fill", (128, 1153), BF16, scoped=True)
    s("bl_a", (128, 3201), BF16, scoped=True)
    s("bl_b", (128, 3201), BF16, scoped=True)
    s("bl_t", (128, 3201), BF16, scoped=True)
    tz = kb.ge(lambda e: e.memset(kb.s["z_fill"], 0.0))
    zt = []
    for c in range(8):
        zt.append(kb.dma("sync", bass.AP(d["s_xn1fm"].tensor, c * 128 * XNW,
                                         [[XNW, 128], [1, 1153]]),
                         kb.s["z_fill"], deps=[tz], q="qsp0"))
        zt.append(kb.dma("sync", bass.AP(d["s_mid"].tensor, c * 128 * VRNG,
                                         [[VRNG, 128], [1, 512]]),
                         kb.s["z_fill"][:, 0:512], deps=[tz], q="qsp0"))
        zt.append(kb.dma("sync", bass.AP(d["s_out"].tensor, c * 128 * VRNG,
                                         [[VRNG, 128], [1, 512]]),
                         kb.s["z_fill"][:, 0:512], deps=[tz], q="qsp0"))
    tzt = kb.p.qtot("qsp0")
    prev = [tzt]
    for c in range(8):
        ta = kb.dma("sync", kb.s["bl_a"],
                    bass.AP(d["s_xn1fm"].tensor, c * 128 * XNW,
                            [[XNW, 128], [1, 3201]]),
                    deps=prev + [tzt], q="qx0")
        tb = kb.dma("sync", kb.s["bl_b"],
                    bass.AP(d["s_xn1fm"].tensor, c * 128 * XNW + 2048,
                            [[XNW, 128], [1, 3201]]),
                    deps=prev, q="qx0")
        tl = kb.p.qtot("qx0")
        t1 = kb.ve(lambda e: e.tensor_tensor(kb.s["bl_t"], kb.s["bl_b"],
                                             kb.s["bl_a"], ALU.subtract),
                   deps=[tl] + prev, hazard=True)
        t2 = kb.ve(lambda e: e.scalar_tensor_tensor(
            kb.s["bl_t"], kb.s["bl_t"], kb.s["c_msel"], kb.s["bl_a"],
            ALU.mult, ALU.add), deps=[t1], hazard=True)
        ts = kb.dma("sync", bass.AP(d["s_xv"].tensor, c * 128 * (1 + VRNG),
                                    [[1 + VRNG, 128], [1, 3201]]),
                    kb.s["bl_t"], deps=[t2], q="qsp0")
        prev = [ts, kb.p.qtot("qsp0")]
    kb.tok["p4pre"] = [kb.p.qtot("qsp0")]


def emit_p4ab(kb, l, taps=()):
    s, d = kb.sb, kb.d
    ve, sc, te = kb.ve, kb.sc, kb.te
    s("w4q", (128, 8 * 1536), BF16, scoped=True)
    s("w4p", (128, 8 * 1024), BF16, scoped=True)
    s("w4o", (128, 4 * 1024), BF16, scoped=True)
    tw = []
    tw.append(kb.dma("gpsimd", bass.AP(kb.s["w4q"].tensor, 0,
                                       [[8 * 1536, 128], [1536, 8], [1, 1536]]),
                     bass.AP(d["fqkv"].tensor, l * D * 1536,
                             [[1536, 128], [128 * 1536, 8], [1, 1536]]),
                     q="qw"))
    tw.append(kb.dma("gpsimd", bass.AP(kb.s["w4p"].tensor, 0,
                                       [[8 * 1024, 128], [1024, 8], [1, 1024]]),
                     bass.AP(d["fqkvp"].tensor, l * D * 1024,
                             [[1024, 128], [128 * 1024, 8], [1, 1024]]),
                     q="qw"))
    tw.append(kb.dma("gpsimd", bass.AP(kb.s["w4o"].tensor, 0,
                                       [[4 * 1024, 128], [1024, 4], [1, 1024]]),
                     bass.AP(d["fwo"].tensor, l * INNER * D,
                             [[1024, 128], [128 * 1024, 4], [1, 1024]]),
                     q="qw"))
    twt = kb.p.qtot("qw")
    s("c_bones", (128, 32), F32, scoped=True)
    s("c_hsel", (8, 4 * 128), F32, scoped=True)
    s("c_ones_rowf", (1, 512), F32, scoped=True)
    tw.append(kb.dma("sync", kb.s["c_bones"], d["bones"], q="q0"))
    tw.append(kb.dma("sync", kb.s["c_hsel"], d["hsel"], q="q0"))
    _corf = kb.s["c_ones_rowf"]
    tonesf = kb.ve(lambda e, _corf=_corf: e.memset(_corf, 1.0))
    s("sc_qs", (128, 1), F32, scoped=True)
    s("sc_qsp", (128, 1), F32, scoped=True)
    s("sc_ks", (128, 1), F32, scoped=True)
    s("sc_ksp", (128, 1), F32, scoped=True)
    tsc0 = []
    for nm, dn in [("sc_qs", "fqscol"), ("sc_qsp", "fqspcol"),
                   ("sc_ks", "fkscol"), ("sc_ksp", "fkspcol")]:
        tsc0.append(kb.dma("sync", kb.s[nm],
                           bass.AP(d[dn].tensor, l * 128,
                                   [[1, 128], [1, 1]]), q="q0"))
    tsct = kb.p.qtot("q0")
    s("q_fm", (128, 4 * VRNG), BF16, scoped=True)
    s("k_fm", (128, 4 * VRNG), BF16, scoped=True)
    s("v_tm", (128, 25 * 520), BF16, scoped=True)
    _vone = bass.AP(kb.s["v_tm"].tensor, 64, [[25 * 520, 128], [65, 200]])
    tone = ve(lambda e, _vone=_vone: e.memset(_vone, 1.0))
    s("hg0", (128, 8 * 512), BF16, scoped=True)
    s("tb0", (128, 4 * 512), F32, scoped=True)
    s("qa_g", (128, 4 * 512), BF16, scoped=True)
    s("ka_g", (128, 4 * 512), BF16, scoped=True)
    s("qb_c", (128, 512), BF16, scoped=True)
    s("sqb", (128, 512), F32, scoped=True)
    s("rn_q", (8, 512), F32, scoped=True)
    s("rn_k", (8, 512), F32, scoped=True)
    s("cu1", (128, 512), F32, scoped=True)
    kb.s["cu2"] = kb.s["sqb"]
    pb = [kb.psum(f"p4b{i}", (128, 512), F32, scoped=True) for i in range(4)]
    pssq = kb.psum("p4ssq", (8, 512), F32, scoped=True)
    pssk = kb.psum("p4ssk", (8, 512), F32, scoped=True)
    pav = kb.psum("p4av", (65, 512), F32, scoped=True)
    pR = kb.psum("p4R", (128, 512), F32, scoped=True)
    s("mb0", (128, 8 * 512), BF16, scoped=True)
    s("xg0", (128, 8 * 512), BF16, scoped=True)
    s("wt_b", (128, 2 * 512), BF16, scoped=True)
    s("wm_b", (128, 2 * 512), BF16, scoped=True)
    s("og", (128, 4 * 512), BF16, scoped=True)
    s("onesf_p", (128, 64), F32, scoped=True)
    s("ot_c", (64, 512), BF16, scoped=True)
    s("rr_c", (64, 512), F32, scoped=True)
    s("midb", (128, 512), BF16, scoped=True)
    tonesp = ve(lambda e, _o=kb.s["onesf_p"]: e.memset(_o, 1.0))
    xsrc = d["s_xv"] if l == 0 else d["s_out"]
    xw = (1 + VRNG) if l == 0 else VRNG
    xoff = 1 if l == 0 else 0

    snap = dict(kb.s)
    S = snap.__getitem__
    ft = {}

    def xread(col0, width, shift):
        out = []
        for c in range(8):
            cc = col0 + xoff - (1 if (shift and c >= 4) else 0)
            out.append(bass.AP(xsrc.tensor, c * 128 * xw + cc,
                               [[xw, 128], [1, width]]))
        return out

    kvg = _kv_groups(l)
    for g, (V0, W) in enumerate(kvg):
        p = g % 2
        hg, tb = S("hg0"), S("tb0")
        tld = []
        srcs = xread(V0, W, True)
        for c in range(8):
            tld.append(kb.dma("sync", hg[:, c * 512:c * 512 + W], srcs[c],
                              deps=[ft.get(("hg",)), ft.get(("hgv",))]
                              + kb.tok["p4pre"], q=f"qx{p}"))
        for i, tn in enumerate(["rt1q", "rt2q", "rt1k", "rt2k"]):
            tld.append(kb.dma("sync", tb[:, i * 512:i * 512 + W],
                              d[tn][:, V0:V0 + W],
                              deps=[ft.get(("tb",))], q=f"qx{p}"))
        tldt = kb.p.qtot(f"qx{p}")
        tts = []
        for i, scn in enumerate(["sc_qs", "sc_qsp", "sc_ks", "sc_ksp"]):
            tts.append(ve(lambda e, i=i, scn=scn, tb=tb, W=W: e.tensor_scalar(
                tb[:, i * 512:i * 512 + W], tb[:, i * 512:i * 512 + W],
                S(scn), None, ALU.mult),
                deps=[tldt, tsct], hazard=True))
        for tt in range(W // 128):
            slot = V0 // 128 + tt
            tmm = None
            for c in range(8):
                tmm = te(lambda e, c=c, tt=tt, hg=hg: e.matmul(
                    pb[3], hg[:, c * 512 + tt * 128: c * 512 + tt * 128 + 128],
                    S("w4q")[:, c * 1536 + 1024: (c + 1) * 1536],
                    start=(c == 0), stop=(c == 7)),
                    deps=[tldt, twt, ft.get(("pb3",))] if c == 0 else [])
            t1 = sc(lambda e, slot=slot: e.activation(
                bass.AP(S("v_tm").tensor, slot * 520,
                        [[25 * 520, 128], [65, 8], [1, 64]]),
                pb[3].rearrange("p (h e) -> p h e", e=64), ACT.Copy),
                deps=[tmm, tone])
            ft[("pb3",)] = t1
        tev = {}
        for oc in range(4):
            for j, (nm, wsel, sq) in enumerate([
                    ("qa_g", ("w4q", 0), "q"),
                    ("ka_g", ("w4q", 512), "k")]):
                wname, wbase = wsel
                wwid = 1536 if wname == "w4q" else 1024
                pbx = pb[j % 3]
                tmm = None
                for c in range(8):
                    lhs = kb.s[wname][:, c * wwid + wbase + oc * 128:
                                      c * wwid + wbase + oc * 128 + 128]
                    tmm = te(lambda e, pbx=pbx, l_=lhs, c=c, hg=hg, W=W: e.matmul(
                        pbx[:, 0:W], l_, hg[:, c * 512:c * 512 + W],
                        start=(c == 0), stop=(c == 7)),
                        deps=[tldt, twt, ft.get(("pb", j % 3))]
                        if c == 0 else [])
                if sq is not None:
                    t1 = sc(lambda e, pbx=pbx, W=W: e.square(
                        S("sqb")[:, 0:W], pbx[:, 0:W]),
                        deps=[tmm, ft.get(("sqb",))])
                    pss = pssq if sq == "q" else pssk
                    t2 = te(lambda e, pss=pss, oc=oc, W=W: e.matmul(
                        pss[:, 0:W], S("c_bones")[:, 8 * oc:8 * oc + 8],
                        S("sqb")[:, 0:W],
                        start=(oc == 0), stop=(oc == 3)),
                        deps=[t1, tsct] +
                        ([ft.get(("pss", sq))] if oc == 0 else []))
                    ft[("sqb",)] = t2
                    ft[("pss", sq + "c")] = t2
                t3 = sc(lambda e, nm=nm, oc=oc, pbx=pbx, W=W: e.activation(
                    S(nm)[:, oc * 512:oc * 512 + W], pbx[:, 0:W], ACT.Copy),
                    deps=[tmm, ft.get((nm,))])
                tev[(nm, oc)] = t3
                ft[("pb", j % 3)] = t3
        rnt = {}
        for sq, pss, rb in [("q", pssq, "rn_q"), ("k", pssk, "rn_k")]:
            r1 = sc(lambda e, pss=pss, rb=rb, W=W: e.activation(
                S(rb)[:, 0:W], pss[:, 0:W], ACT.Ln, bias=1e-24),
                deps=[ft.get(("pss", sq + "c")), ft.get(("rn", sq))])
            r2 = sc(lambda e, rb=rb, W=W: e.activation(
                S(rb)[:, 0:W], S(rb)[:, 0:W], ACT.Exp, scale=0.5),
                deps=[r1], hazard=True)
            r3 = ve(lambda e, rb=rb, W=W: e.tensor_scalar(
                S(rb)[:, 0:W], S(rb)[:, 0:W], 1e-12, None, ALU.max),
                deps=[r2], hazard=True)
            r4 = ve(lambda e, rb=rb, W=W: e.reciprocal(
                S(rb)[:, 0:W], S(rb)[:, 0:W]), deps=[r3], hazard=True)
            rnt[sq] = r4
            ft[("pss", sq)] = r4
        for qk, anm, bbase, rb, ti1, ti2, dst in [
                ("q", "qa_g", 0, "rn_q", 0, 1, "q_fm"),
                ("k", "ka_g", 512, "rn_k", 2, 3, "k_fm")]:
            for oc in range(4):
                pbx = pb[oc % 2]
                tmb = None
                for c in range(8):
                    lhs = S("w4p")[:, c * 1024 + bbase + oc * 128:
                                      c * 1024 + bbase + oc * 128 + 128]
                    tmb = te(lambda e, pbx=pbx, l_=lhs, c=c, hg=hg, W=W:
                             e.matmul(
                                 pbx[:, 0:W], l_, hg[:, c * 512:c * 512 + W],
                                 start=(c == 0), stop=(c == 7)),
                             deps=[ft.get(("pb", oc % 2))] if c == 0 else [])
                tevb = sc(lambda e, pbx=pbx, W=W: e.activation(
                    S("qb_c")[:, 0:W], pbx[:, 0:W], ACT.Copy),
                    deps=[tmb, ft.get(("qb_c",))])
                ft[("pb", oc % 2)] = tevb
                tpr = te(lambda e, rb=rb, oc=oc, W=W: e.matmul(
                    pR[:, 0:W], S("c_hsel")[:, oc * 128:oc * 128 + 128],
                    S(rb)[:, 0:W],
                    start=True, stop=True),
                    deps=[rnt[qk], tsct, ft.get(("pR",))])
                u1 = ve(lambda e, anm=anm, oc=oc, ti1=ti1, tb=tb, W=W: e.tensor_tensor(
                    S("cu1")[:, 0:W], S(anm)[:, oc * 512:oc * 512 + W],
                    tb[:, ti1 * 512:ti1 * 512 + W], ALU.mult),
                    deps=[tev[(anm, oc)], tts[ti1], ft.get(("cu1",))],
                    hazard=True)
                u2 = ve(lambda e, oc=oc, ti2=ti2, tb=tb, W=W: e.tensor_tensor(
                    S("cu2")[:, 0:W], S("qb_c")[:, 0:W],
                    tb[:, ti2 * 512:ti2 * 512 + W], ALU.mult),
                    deps=[tevb, tts[ti2], ft.get(("cu2",))],
                    hazard=True)
                ft[("qb_c",)] = u2
                u3 = ve(lambda e, W=W: e.tensor_tensor(
                    S("cu1")[:, 0:W], S("cu1")[:, 0:W],
                    S("cu2")[:, 0:W], ALU.add), deps=[u1, u2], hazard=True)
                ft[("cu2",)] = u3
                u4 = ve(lambda e, dst=dst, oc=oc, V0=V0, W=W: e.tensor_tensor(
                    S(dst)[:, oc * VRNG + V0: oc * VRNG + V0 + W],
                    S("cu1")[:, 0:W], pR[:, 0:W], ALU.mult),
                    deps=[u3, tpr], hazard=True)
                ft[("cu1",)] = u4
                ft[("pR",)] = u4
                ft[(dst, oc)] = u4
            ft[("rn", qk)] = u4
        ft[("sqb",)] = u4
        ft[("hg",)] = u4
        ft[("hgv",)] = ft[("pb3",)]
        ft[("tb",)] = u4
        for nmx in ("qa_g", "ka_g"):
            ft[(nmx,)] = u4
    tproj = [ft[("q_fm", oc)] for oc in range(4)] + \
            [ft[("k_fm", oc)] for oc in range(4)] + [ft[("pb3",)]]

    qgs = _p4_groups(l)
    tap_mid = kb.tap(f"mid{l}", (D, VRNG), BF16)
    for gi, (Q0, W) in enumerate(qgs):
        p = gi % 2
        mb, xg = S("mb0"), S("xg0")
        nch = (512 + W) // 128
        tld = []
        gext = tproj if gi < 2 else []
        for r in range(nch):
            tld.append(kb.dma("sync", mb[:, r * W:(r + 1) * W],
                              bass.AP(d[f"pmsk{l}_{gi}"].tensor, r * 128 * W,
                                      [[W, 128], [1, W]]),
                              deps=[ft.get(("mb",))] + gext, q=f"qx{p}"))
        srcs = xread(Q0, W, False)
        for c in range(8):
            tld.append(kb.dma("sync", xg[:, c * 512:c * 512 + W], srcs[c],
                              deps=[ft.get(("xg",))] + gext, q=f"qx{p}"))
        tldt = kb.p.qtot(f"qx{p}")
        to_toks = []
        for h in range(8):
            oc, hs = h // 2, (h % 2) * 64
            tav = None
            for r in range(nch):
                C0 = Q0 - 512 + 128 * r
                pscb = pb[r % 2]
                tmm = te(lambda e, pscb=pscb, oc=oc, hs=hs, C0=C0, W=W, Q0=Q0:
                         e.matmul(
                             pscb[:, 0:W],
                             S("k_fm")[hs:hs + 64,
                                          oc * VRNG + C0: oc * VRNG + C0 + 128],
                             S("q_fm")[hs:hs + 64,
                                          oc * VRNG + Q0: oc * VRNG + Q0 + W],
                             start=True, stop=True, tile_position=(hs, 0)),
                         deps=tproj + [ft.get(("psc", r % 2))]
                         if (h == 0 and r < 2) else [ft.get(("psc", r % 2))])
                t1 = sc(lambda e, pscb=pscb, r=r, W=W: e.activation(
                    S("wt_b")[:, (r % 2) * 512:(r % 2) * 512 + W],
                    pscb[:, 0:W], ACT.Exp, bias=-30.0),
                    deps=[tmm, ft.get(("wt", r % 2))])
                ft[("psc", r % 2)] = t1
                t2 = ve(lambda e, r=r, W=W, mb=mb: e.tensor_tensor(
                    S("wm_b")[:, (r % 2) * 512:(r % 2) * 512 + W],
                    S("wt_b")[:, (r % 2) * 512:(r % 2) * 512 + W],
                    mb[:, r * W:(r + 1) * W], ALU.mult),
                    deps=[t1, tldt, ft.get(("wm", r % 2))], hazard=True)
                ft[("wt", r % 2)] = t2
                slot = C0 // 128
                tav = te(lambda e, slot=slot, h=h, r=r, nch=nch, W=W: e.matmul(
                    pav[:, 0:W],
                    S("v_tm")[:, slot * 520 + h * 65: slot * 520 + h * 65 + 65],
                    S("wm_b")[:, (r % 2) * 512:(r % 2) * 512 + W],
                    start=(r == 0), stop=(r == nch - 1)),
                    deps=[t2] + ([ft.get(("pav",))] if r == 0 else []))
                ft[("wm", r % 2)] = tav
            trc = ve(lambda e, W=W: e.reciprocal(S("cu1")[64:65, 0:W],
                                            pav[64:65, 0:W]),
                     deps=[tav, ft.get(("rc",))] + (tproj if h == 0 else []),
                     hazard=True)
            tpr = te(lambda e, W=W: e.matmul(
                pR[0:64, 0:W], S("onesf_p")[64:65, 0:64],
                S("cu1")[64:65, 0:W], start=True, stop=True,
                tile_position=(64, 0)),
                deps=[trc, tonesp, ft.get(("pR",))])
            tpc = sc(lambda e, W=W: e.activation(
                S("rr_c")[:, 0:W], pR[0:64, 0:W], ACT.Copy),
                deps=[tpr, ft.get(("rr",))])
            ft[("pR",)] = tpc
            if hs == 0:
                to = ve(lambda e, oc=oc, W=W: e.tensor_tensor(
                    S("og")[0:64, oc * 512:oc * 512 + W],
                    pav[0:64, 0:W], S("rr_c")[:, 0:W], ALU.mult),
                    deps=[tpc, ft.get(("og", h))], hazard=True)
                tog = to
            else:
                to = ve(lambda e, W=W: e.tensor_tensor(
                    S("ot_c")[:, 0:W],
                    pav[0:64, 0:W], S("rr_c")[:, 0:W], ALU.mult),
                    deps=[tpc, ft.get(("ot",))], hazard=True)
                tog = kb.dma("gpsimd", S("og")[64:128, oc * 512:oc * 512 + W],
                             S("ot_c")[:, 0:W],
                             deps=[to, ft.get(("og", h)), kb.p.qtot("qga")],
                             q="qga")
                tog = kb.p.qtot("qga")
                ft[("ot",)] = tog
            ft[("rc",)] = to
            ft[("pav",)] = to
            ft[("rr",)] = to
            to_toks.append(tog)
        for dt in range(8):
            pwo = pb[2 + dt % 2]
            tmm = None
            for c in range(4):
                tmm = te(lambda e, pwo=pwo, c=c, dt=dt, W=W: e.matmul(
                    pwo[:, 0:W],
                    S("w4o")[:, c * 1024 + dt * 128: c * 1024 + dt * 128 + 128],
                    S("og")[:, c * 512:c * 512 + W],
                    start=(c == 0), stop=(c == 3)),
                    deps=to_toks + [twt, ft.get(("pwo", dt % 2))]
                    if c == 0 else [])
            t1 = ve(lambda e, pwo=pwo, dt=dt, W=W, xg=xg: e.tensor_tensor(
                S("midb")[:, 0:W], pwo[:, 0:W], xg[:, dt * 512:dt * 512 + W],
                ALU.add), deps=[tmm, tldt, ft.get(("midb",)),
                                ft.get(("midbo",))], hazard=True)
            ts_ = kb.dma("sync", bass.AP(d["s_mid"].tensor,
                                         dt * 128 * VRNG + Q0,
                                         [[VRNG, 128], [1, W]]),
                         S("midb")[:, 0:W], deps=[t1], q=f"qsp{p}")
            if tap_mid is not None:
                kb.dma("sync", bass.AP(tap_mid.tensor, dt * 128 * VRNG + Q0,
                                       [[VRNG, 128], [1, W]]),
                       S("midb")[:, 0:W], deps=[t1], q="qout")
            ft[("midb",)] = kb.p.qtot(f"qsp{p}")
            if tap_mid is not None:
                ft[("midbo",)] = kb.p.qtot("qout")
            ft[("pwo", dt % 2)] = t1
        ft[("og", 0)] = t1
        for h in range(8):
            ft[("og", h)] = t1
        ft[("mb",)] = t1
        ft[("xg",)] = t1




def emit_p4c1(kb, l):
    s, d = kb.sb, kb.d
    ve, sc, te = kb.ve, kb.sc, kb.te
    s("w1a_sb", (128, 8 * W1P), BF16, scoped=True)
    s("w1g_sb", (128, 8 * W1P), BF16, scoped=True)
    tw = []
    for nm, dn in [("w1a_sb", "w1a"), ("w1g_sb", "w1g")]:
        tw.append(kb.dma("gpsimd", bass.AP(kb.s[nm].tensor, 0,
                                           [[8 * W1P, 128], [W1P, 8], [1, W1P]]),
                         bass.AP(d[dn].tensor, l * D * W1P,
                                 [[W1P, 128], [128 * W1P, 8], [1, W1P]]),
                         q="qw"))
    s("b1_c", (128, 44), F32, scoped=True)
    s("gb_row", (1, 2048), F32, scoped=True)
    s("c_ones_rowf", (1, 512), F32, scoped=True)
    _corf = kb.s["c_ones_rowf"]
    tonesf = kb.ve(lambda e, _corf=_corf: e.memset(_corf, 1.0))
    tw.append(kb.dma("sync", kb.s["b1_c"],
                     bass.AP(d["fb1col"].tensor, l * 128 * 44,
                             [[44, 128], [1, 44]]), q="q0"))
    tw.append(kb.dma("sync", kb.s["gb_row"][0:1, 0:1024],
                     bass.AP(d["flgrow"].tensor, l * 8 * 128,
                             [[0, 1], [1, 1024]]), q="q0"))
    tw.append(kb.dma("sync", kb.s["gb_row"][0:1, 1024:2048],
                     bass.AP(d["flbrow"].tensor, l * 8 * 128,
                             [[0, 1], [1, 1024]]), q="q0"))
    twt = kb.p.qtot("qw")
    tw0 = kb.p.qtot("q0")
    _gbr = kb.s["gb_row"]
    tnb = ve(lambda e, _gbr=_gbr: e.tensor_scalar(_gbr[0:1, 1024:2048],
                                       _gbr[0:1, 1024:2048], -1.0, None,
                                       ALU.mult), deps=[tw0], hazard=True)
    for p in range(2):
        s(f"mg{p}", (128, 8 * 512), BF16, scoped=True)
    s("sq_c", (128, 512), BF16, scoped=True)
    s("xng", (128, 8 * 512), BF16, scoped=True)
    s("st_r", (1, 4 * 512), F32, scoped=True)
    s("gg_c", (128, 512), BF16, scoped=True)
    s("h3_c", (128, 512), BF16, scoped=True)
    pSa = kb.psum("p4sa", (1, 512), F32, scoped=True)
    pSb = kb.psum("p4sb", (1, 512), F32, scoped=True)
    pM2 = kb.psum("p4m2", (128, 512), F32, scoped=True)
    pM3 = kb.psum("p4m3", (128, 512), F32, scoped=True)
    pa = kb.psum("p4pa", (128, 512), F32, scoped=True)
    pg = kb.psum("p4pg", (128, 512), F32, scoped=True)
    snap = dict(kb.s)
    S = snap.__getitem__
    ft = {}
    for gi, (Q0, W) in enumerate(_p4_groups(l)):
        p = gi % 2
        mg = kb.s[f"mg{p}"]
        tld = []
        for c in range(8):
            cc = Q0 - (1 if c >= 4 else 0)
            tld.append(kb.dma("sync", mg[:, c * 512:c * 512 + W],
                              bass.AP(d["s_mid"].tensor, c * 128 * VRNG + cc,
                                      [[VRNG, 128], [1, W]]),
                              deps=[ft.get(("mg", p))], q=f"qx{p}"))
        tldt = kb.p.qtot(f"qx{p}")
        tsa = tsb = None
        for c in range(8):
            tsa = te(lambda e, c=c, mg=mg, W=W: e.matmul(
                pSa[:, 0:W], S("c_ones_col"), mg[:, c * 512:c * 512 + W],
                start=(c == 0), stop=(c == 7)),
                deps=[tldt, ft.get(("pSa",))] if c == 0 else [])
        for c in range(8):
            t1 = sc(lambda e, c=c, mg=mg, W=W: e.square(
                S("sq_c")[:, 0:W], mg[:, c * 512:c * 512 + W]),
                deps=[tldt, ft.get(("sq",))])
            tsb = te(lambda e, c=c, W=W: e.matmul(
                pSb[:, 0:W], S("c_ones_col"), S("sq_c")[:, 0:W],
                start=(c == 0), stop=(c == 7)),
                deps=[t1] + ([ft.get(("pSb",))] if c == 0 else []))
            ft[("sq",)] = tsb
        st = S("st_r")
        t1 = ve(lambda e, W=W: e.tensor_scalar(st[0:1, 0:0 + W], pSa[:, 0:W],
                                               1.0 / 1024.0, None, ALU.mult),
                deps=[tsa, ft.get(("st",))], hazard=True)
        ft[("pSa",)] = t1
        t2 = ve(lambda e, W=W: e.tensor_scalar(st[0:1, 1536:1536 + W], pSb[:, 0:W],
                                               1.0 / 1024.0, None, ALU.mult),
                deps=[tsb, ft.get(("st",))], hazard=True)
        ft[("pSb",)] = t2
        t3 = ve(lambda e, W=W: e.tensor_tensor(st[0:1, 512:512 + W], st[0:1, 0:0 + W],
                                               st[0:1, 0:0 + W], ALU.mult),
                deps=[t1], hazard=True)
        t4 = ve(lambda e, W=W: e.tensor_tensor(st[0:1, 512:512 + W], st[0:1, 1536:1536 + W],
                                               st[0:1, 512:512 + W], ALU.subtract),
                deps=[t2, t3], hazard=True)
        t5 = sc(lambda e, W=W: e.activation(st[0:1, 512:512 + W], st[0:1, 512:512 + W],
                                            ACT.Ln, bias=1e-5),
                deps=[t4], hazard=True)
        t6 = sc(lambda e, W=W: e.activation(st[0:1, 512:512 + W], st[0:1, 512:512 + W],
                                            ACT.Exp, scale=-0.5),
                deps=[t5], hazard=True)
        t7 = ve(lambda e, W=W: e.tensor_tensor(st[0:1, 1024:1024 + W], st[0:1, 0:0 + W],
                                               st[0:1, 512:512 + W], ALU.mult),
                deps=[t6], hazard=True)
        txn = []
        for c in range(8):
            tm2 = te(lambda e, c=c, W=W: e.matmul(
                pM2[:, 0:W], S("gb_row")[0:1, c * 128:c * 128 + 128], st[0:1, 512:512 + W],
                start=True, stop=True), deps=[t6, ft.get(("pM2",))])
            tm3a = te(lambda e, c=c, W=W: e.matmul(
                pM3[:, 0:W], S("gb_row")[0:1, c * 128:c * 128 + 128], st[0:1, 1024:1024 + W],
                start=True, stop=False), deps=[t7, ft.get(("pM3",)), tnb])
            tm3b = te(lambda e, c=c, W=W: e.matmul(
                pM3[:, 0:W], S("gb_row")[0:1, 1024 + c * 128:1024 + c * 128 + 128],
                S("c_ones_rowf")[:, 0:W],
                start=False, stop=True), deps=[tonesf, tnb])
            tn1 = ve(lambda e, c=c, mg=mg, W=W: e.tensor_tensor(
                S("xng")[:, c * 512:c * 512 + W],
                mg[:, c * 512:c * 512 + W], pM2[:, 0:W], ALU.mult),
                deps=[tm2, tldt, ft.get(("xng", c))], hazard=True)
            tn2 = ve(lambda e, c=c, W=W: e.scalar_tensor_tensor(
                S("xng")[:, c * 512:c * 512 + W],
                S("xng")[:, c * 512:c * 512 + W], 1.0,
                pM3[:, 0:W], ALU.mult, ALU.subtract),
                deps=[tn1, tm3b], hazard=True)
            ft[("pM2",)] = tn1
            ft[("pM3",)] = tn2
            txn.append(tn2)
        ft[("st",)] = tn2
        ft[("mg", p)] = tn2
        for ob in range(22):
            tma = tmg = None
            for c in range(8):
                tma = te(lambda e, c=c, ob=ob, W=W: e.matmul(
                    pa[:, 0:W],
                    S("w1a_sb")[:, c * W1P + ob * 128: c * W1P + ob * 128 + 128],
                    S("xng")[:, c * 512:c * 512 + W],
                    start=(c == 0), stop=(c == 7)),
                    deps=txn + [twt, ft.get(("pa",))] if c == 0 else [])
                tmg = te(lambda e, c=c, ob=ob, W=W: e.matmul(
                    pg[:, 0:W],
                    S("w1g_sb")[:, c * W1P + ob * 128: c * W1P + ob * 128 + 128],
                    S("xng")[:, c * 512:c * 512 + W],
                    start=(c == 0), stop=(c == 7)),
                    deps=[ft.get(("pg",))] if c == 0 else [])
            tg1 = sc(lambda e, ob=ob, W=W: e.activation(
                S("gg_c")[:, 0:W], pg[:, 0:W], ACT.Gelu,
                bias=S("b1_c")[:, 22 + ob:23 + ob]),
                deps=[tmg, tw0, ft.get(("gg",))])
            ft[("pg",)] = tg1
            th3 = ve(lambda e, ob=ob, W=W: e.scalar_tensor_tensor(
                S("h3_c")[:, 0:W], pa[:, 0:W],
                S("b1_c")[:, ob:ob + 1], S("gg_c")[:, 0:W],
                ALU.add, ALU.mult),
                deps=[tma, tg1, tw0, ft.get(("h3",))], hazard=True)
            ft[("pa",)] = th3
            ft[("gg",)] = th3
            tst = kb.dma("sync", bass.AP(d["s_h3"].tensor,
                                         ob * 128 * (VRNG - QLO0) + Q0 - QLO0,
                                         [[VRNG - QLO0, 128], [1, W]]),
                         S("h3_c")[:, 0:W], deps=[th3], q=f"qsp{p}")
            ft[("h3",)] = kb.p.qtot(f"qsp{p}")
        for c in range(8):
            ft[("xng", c)] = th3


def emit_p4c2(kb, l, taps=()):
    s, d = kb.sb, kb.d
    ve, sc, te = kb.ve, kb.sc, kb.te
    s("w2_sb", (128, 22 * 1024), BF16, scoped=True)
    tw = [kb.dma("gpsimd", bass.AP(kb.s["w2_sb"].tensor, 0,
                                   [[22 * 1024, 128], [1024, 22], [1, 1024]]),
                 bass.AP(d["fw2"].tensor, l * W1P * D,
                         [[1024, 128], [128 * 1024, 22], [1, 1024]]),
                 q="qw")]
    s("b2_c", (128, 8), F32, scoped=True)
    tw.append(kb.dma("sync", kb.s["b2_c"],
                     bass.AP(d["fb2col"].tensor, l * 128 * 8,
                             [[8, 128], [1, 8]]), q="q0"))
    twt = kb.p.qtot("qw")
    tw0 = kb.p.qtot("q0")
    for p in range(2):
        s(f"h3g{p}", (128, 22 * 512), BF16, scoped=True)
        s(f"mg2{p}", (128, 8 * 512), BF16, scoped=True)
    s("outb", (128, 512), BF16, scoped=True)
    pd0 = kb.psum("p4d0", (128, 512), F32, scoped=True)
    pd1 = kb.psum("p4d1", (128, 512), F32, scoped=True)
    tap_out = kb.tap(f"out{l}", (D, VRNG), BF16)
    snap = dict(kb.s)
    S = snap.__getitem__
    ft = {}
    for gi, (Q0, W) in enumerate(_p4_groups(l)):
        p = gi % 2
        h3g, mg2 = kb.s[f"h3g{p}"], kb.s[f"mg2{p}"]
        tld = []
        for kc in range(22):
            tld.append(kb.dma("sync", h3g[:, kc * 512:kc * 512 + W],
                              bass.AP(d["s_h3"].tensor,
                                      kc * 128 * (VRNG - QLO0) + Q0 - QLO0,
                                      [[VRNG - QLO0, 128], [1, W]]),
                              deps=[ft.get(("h3g", p))], q=f"qx{p}"))
        for c in range(8):
            tld.append(kb.dma("sync", mg2[:, c * 512:c * 512 + W],
                              bass.AP(d["s_mid"].tensor, c * 128 * VRNG + Q0,
                                      [[VRNG, 128], [1, W]]),
                              deps=[ft.get(("mg2", p))], q=f"qx{p}"))
        tldt = kb.p.qtot(f"qx{p}")
        for dt in range(8):
            pdx = pd0 if dt % 2 == 0 else pd1
            tmm = None
            for kc in range(22):
                tmm = te(lambda e, pdx=pdx, kc=kc, dt=dt, h3g=h3g, W=W:
                         e.matmul(
                             pdx[:, 0:W],
                             S("w2_sb")[:, kc * 1024 + dt * 128:
                                           kc * 1024 + dt * 128 + 128],
                             h3g[:, kc * 512:kc * 512 + W],
                             start=(kc == 0), stop=(kc == 21)),
                         deps=[tldt, twt, ft.get(("pd", dt % 2))]
                         if kc == 0 else [])
            t1 = ve(lambda e, pdx=pdx, dt=dt, mg2=mg2, W=W:
                    e.scalar_tensor_tensor(
                        S("outb")[:, 0:W], pdx[:, 0:W],
                        S("b2_c")[:, dt:dt + 1],
                        mg2[:, dt * 512:dt * 512 + W], ALU.add, ALU.add),
                    deps=[tmm, tldt, tw0, ft.get(("outb",)),
                          ft.get(("outbo",))], hazard=True)
            ft[("pd", dt % 2)] = t1
            ts_ = kb.dma("sync", bass.AP(d["s_out"].tensor,
                                         dt * 128 * VRNG + Q0,
                                         [[VRNG, 128], [1, W]]),
                         S("outb")[:, 0:W], deps=[t1], q=f"qsp{p}")
            if tap_out is not None:
                kb.dma("sync", bass.AP(tap_out.tensor, dt * 128 * VRNG + Q0,
                                       [[VRNG, 128], [1, W]]),
                       S("outb")[:, 0:W], deps=[t1], q="qout")
                ft[("outbo",)] = kb.p.qtot("qout")
            ft[("outb",)] = kb.p.qtot(f"qsp{p}")
        ft[("h3g", p)] = t1
        ft[("mg2", p)] = t1





def emit_p5(kb):
    s, d = kb.sb, kb.d
    ve, sc, te = kb.ve, kb.sc, kb.te
    s("w5o", (128, 8 * 1024), BF16, scoped=True)
    s("b5", (128, 1024), BF16, scoped=True)
    kb.s["w5q"] = kb.s["w5o"][:, 0:4096]
    kb.s["w5k"] = kb.s["w5o"][:, 4096:8192]
    s("w5v", (128, 8 * 512), BF16, scoped=True)
    s("ones_rb", (128, 512), BF16, scoped=True)
    s("onesf_p5", (128, 128), F32, scoped=True)
    s("xn1own", (128, 8 * 2048), BF16, scoped=True)
    s("og5", (128, 8 * 2048), BF16, scoped=True)
    s("q5", (128, 4 * 512), BF16, scoped=True)
    s("k5", (128, 4 * 4096), BF16, scoped=True)
    s("v5", (128, 32 * 520), BF16, scoped=True)
    s("xf5", (128, 8 * 512), BF16, scoped=True)
    kb.s["blB"] = kb.s["xf5"][:, 0:2048]
    s("wt5", (128, 4 * 512), BF16, scoped=True)
    s("ot5", (64, 512), BF16, scoped=True)
    s("rr5", (64, 512), F32, scoped=True)
    s("st2", (128, 32), F32, scoped=True)
    s("stT", (32, 128), F32, scoped=True)
    s("strow", (1, 1024), F32, scoped=True)
    s("u5", (128, 512), F32, scoped=True)
    s("x2c", (128, 512), F32, scoped=True)
    pq = [kb.psum(f"p5q{i}", (128, 512), F32, scoped=True) for i in range(2)]
    pav = kb.psum("p5av", (65, 512), F32, scoped=True)
    pR = kb.psum("p5R", (128, 512), F32, scoped=True)
    pRm = kb.psum("p5Rm", (128, 512), F32, scoped=True)
    pRs = kb.psum("p5Rs", (128, 512), F32, scoped=True)
    pT = kb.psum("p5T", (32, 128), F32, scoped=True)
    snap = dict(kb.s)
    S = snap.__getitem__
    ft = {}
    tw = []
    for bn, row in [("bq1", 0), ("bk1", 32), ("bv1", 64), ("bo1", 96)]:
        tw.append(kb.dma("gpsimd", S("b5")[row:row + 1, 0:1024],
                         d[bn], q="qw"))
    twt0 = kb.p.qtot("qw")
    t_onr = ve(lambda e: e.memset(S("ones_rb"), 1.0))
    t_onp = ve(lambda e: e.memset(S("onesf_p5"), 1.0))
    tvone = ve(lambda e: e.memset(
        bass.AP(S("v5").tensor, 64, [[32 * 520, 128], [65, 256]]), 1.0))
    tblend = []
    prevb = []
    for c in range(8):
        t1 = kb.dma("sync", S("xn1own")[:, c * 2048:(c + 1) * 2048],
                    bass.AP(d["s_xn1fm"].tensor, c * 128 * XNW + 1153,
                            [[XNW, 128], [1, 2048]]),
                    deps=prevb, q="qx0")
        t2 = kb.dma("sync", S("blB"),
                    bass.AP(d["s_xn1fm"].tensor, c * 128 * XNW + 3201,
                            [[XNW, 128], [1, 2048]]),
                    deps=prevb, q="qx0")
        tl = kb.p.qtot("qx0")
        t3 = ve(lambda e, c=c: e.tensor_tensor(
            S("blB"), S("blB"), S("xn1own")[:, c * 2048:(c + 1) * 2048],
            ALU.subtract), deps=[tl], hazard=True)
        t4 = ve(lambda e, c=c: e.scalar_tensor_tensor(
            S("xn1own")[:, c * 2048:(c + 1) * 2048], S("blB"),
            kb.s["c_msel"], S("xn1own")[:, c * 2048:(c + 1) * 2048],
            ALU.mult, ALU.add), deps=[t3], hazard=True)
        tblend.append(t4)
        prevb = [t4]
    ft[("xf5",)] = tblend[-1]
    tm1 = ve(lambda e: e.tensor_tensor(
        S("st2")[:, 0:16], kb.s["ln1_mean"][:, 16:32],
        kb.s["ln1_mean"][:, 0:16], ALU.subtract), deps=[])
    tm2 = ve(lambda e: e.scalar_tensor_tensor(
        S("st2")[:, 0:16], S("st2")[:, 0:16], kb.s["c_msel"],
        kb.s["ln1_mean"][:, 0:16], ALU.mult, ALU.add),
        deps=[tm1], hazard=True)
    ts1 = ve(lambda e: e.tensor_tensor(
        S("st2")[:, 16:32], kb.s["ln1_inv"][:, 16:32],
        kb.s["ln1_inv"][:, 0:16], ALU.subtract), deps=[])
    ts2 = ve(lambda e: e.scalar_tensor_tensor(
        S("st2")[:, 16:32], S("st2")[:, 16:32], kb.s["c_msel"],
        kb.s["ln1_inv"][:, 0:16], ALU.mult, ALU.add),
        deps=[ts1], hazard=True)
    ts3 = ve(lambda e: e.reciprocal(S("st2")[:, 16:32], S("st2")[:, 16:32]),
             deps=[ts2], hazard=True)
    tT1 = te(lambda e: e.transpose(pT[:, 0:128], S("st2"), kb.s["c_identf"]),
             deps=[tm2, ts3])
    tT2 = ve(lambda e: e.tensor_copy(S("stT"), pT[:, 0:128]), deps=[tT1])
    tT2b = kb.p.qtot("qga")

    to_all = []
    for hg in range(2):
        twg = []
        for dn, dstt, doff, dw in [("wqT", "w5o", 0, 8192),
                                   ("wkT", "w5o", 4096, 8192),
                                   ("wvT", "w5v", 0, 4096)]:
            twg.append(kb.dma("gpsimd",
                              bass.AP(S(dstt).tensor, doff,
                                      [[dw, 128], [512, 8], [1, 512]]),
                              bass.AP(d[dn].tensor, hg * 512,
                                      [[1024, 128], [128 * 1024, 8],
                                       [1, 512]]),
                              deps=[ft.get(("w5g",))], q="qw"))
        twt = kb.p.qtot("qw")
        tk_toks = []
        for tkg in range(8):
            tld = []
            for dc in range(8):
                tld.append(kb.dma("sync", S("xf5")[:, dc * 512:(dc + 1) * 512],
                                  bass.AP(d["s_xn1fm"].tensor,
                                          dc * 128 * XNW + 1153 + tkg * 512,
                                          [[XNW, 128], [1, 512]]),
                                  deps=[ft.get(("xf5",))], q="qx1"))
            tldt = kb.p.qtot("qx1")
            for oc in range(4):
                pbx = pq[oc % 2]
                tmm = None
                for dc in range(8):
                    tmm = te(lambda e, pbx=pbx, oc=oc, dc=dc: e.matmul(
                        pbx, S("w5k")[:, dc * 512 + oc * 128:
                                      dc * 512 + oc * 128 + 128],
                        S("xf5")[:, dc * 512:(dc + 1) * 512],
                        start=(dc == 0), stop=False),
                        deps=[twt, tldt, ft.get(("pq", oc % 2))]
                        if dc == 0 else [])
                tmb = te(lambda e, pbx=pbx, oc=oc, hg=hg: e.matmul(
                    pbx, S("b5")[32:33, hg * 512 + oc * 128:
                                 hg * 512 + oc * 128 + 128],
                    S("ones_rb")[32:33, :], start=False, stop=True,
                    tile_position=(32, 0)),
                    deps=[twt0, t_onr])
                t1 = sc(lambda e, pbx=pbx, oc=oc, tkg=tkg: e.activation(
                    S("k5")[:, oc * 4096 + tkg * 512:
                            oc * 4096 + tkg * 512 + 512],
                    pbx, ACT.Copy), deps=[tmb, ft.get(("k5",))])
                ft[("pq", oc % 2)] = t1
                tk_toks.append(t1)
            for tt in range(4):
                slot = tkg * 4 + tt
                pbx = pq[tt % 2]
                tmm = None
                for dc in range(8):
                    tmm = te(lambda e, pbx=pbx, tt=tt, dc=dc: e.matmul(
                        pbx, S("xf5")[:, dc * 512 + tt * 128:
                                      dc * 512 + tt * 128 + 128],
                        S("w5v")[:, dc * 512:(dc + 1) * 512],
                        start=(dc == 0), stop=False),
                        deps=[twt, tldt, ft.get(("pq", tt % 2))]
                        if dc == 0 else [])
                tmb = te(lambda e, pbx=pbx, hg=hg: e.matmul(
                    pbx, S("ones_rb")[64:65, 0:128],
                    S("b5")[64:65, hg * 512: hg * 512 + 512],
                    start=False, stop=True, tile_position=(64, 0)),
                    deps=[twt0, t_onr])
                t1 = sc(lambda e, pbx=pbx, slot=slot: e.activation(
                    bass.AP(S("v5").tensor, slot * 520,
                            [[32 * 520, 128], [65, 8], [1, 64]]),
                    pbx.rearrange("p (h e) -> p h e", e=64), ACT.Copy),
                    deps=[tmb, tvone, ft.get(("v5",))])
                ft[("pq", tt % 2)] = t1
                tk_toks.append(t1)
            ft[("xf5",)] = tk_toks[-1]
        ft[("w5g",)] = tk_toks[-1]
        tprj = list(tk_toks)
        for tqg in range(4):
            tq_toks = []
            for oc in range(4):
                pbx = pq[oc % 2]
                tmm = None
                for dc in range(8):
                    tmm = te(lambda e, pbx=pbx, oc=oc, tqg=tqg, dc=dc: e.matmul(
                        pbx, S("w5q")[:, dc * 512 + oc * 128:
                                      dc * 512 + oc * 128 + 128],
                        S("xn1own")[:, dc * 2048 + tqg * 512:
                                    dc * 2048 + tqg * 512 + 512],
                        start=(dc == 0), stop=False),
                        deps=[twt, tblend[dc], ft.get(("pq", oc % 2))]
                        if dc == 0 else [])
                tmb = te(lambda e, pbx=pbx, oc=oc, hg=hg: e.matmul(
                    pbx, S("b5")[0:1, hg * 512 + oc * 128:
                                 hg * 512 + oc * 128 + 128],
                    S("ones_rb")[0:1, :], start=False, stop=True),
                    deps=[twt0, t_onr])
                t1 = sc(lambda e, pbx=pbx, oc=oc: e.activation(
                    S("q5")[:, oc * 512:oc * 512 + 512],
                    pbx, ACT.Copy), deps=[tmb, ft.get(("q5",))])
                ft[("pq", oc % 2)] = t1
                tq_toks.append(t1)
            for h8 in range(8):
                oc, hs = h8 // 2, (h8 % 2) * 64
                gc = hg * 4 + h8 // 2
                tav = None
                for ck in range(32):
                    C0 = ck * 128
                    pscb = [pq[0], pq[1], pRm, pRs][ck % 4]
                    tmm = te(lambda e, pscb=pscb, oc=oc, hs=hs, C0=C0,
                             tqg=tqg: e.matmul(
                                 pscb[:, 0:512],
                                 S("k5")[hs:hs + 64,
                                         oc * 4096 + C0: oc * 4096 + C0 + 128],
                                 S("q5")[hs:hs + 64,
                                         oc * 512:oc * 512 + 512],
                                 start=True, stop=True, tile_position=(hs, 0)),
                             deps=tprj + tq_toks + [ft.get(("psc", ck % 4))]
                             if (h8 == 0 and ck < 4)
                             else [ft.get(("psc", ck % 4))])
                    t1 = sc(lambda e, pscb=pscb, ck=ck: e.activation(
                        S("wt5")[:, (ck % 4) * 512:(ck % 4) * 512 + 512],
                        pscb[:, 0:512], ACT.Exp, bias=-30.0, scale=0.125),
                        deps=[tmm, ft.get(("wt", ck % 4))])
                    ft[("psc", ck % 4)] = t1
                    slot = C0 // 128
                    tav = te(lambda e, slot=slot, h8=h8, ck=ck: e.matmul(
                        pav, S("v5")[:, slot * 520 + h8 * 65:
                                     slot * 520 + h8 * 65 + 65],
                        S("wt5")[:, (ck % 4) * 512:(ck % 4) * 512 + 512],
                        start=(ck == 0), stop=(ck == 31)),
                        deps=[t1] + ([ft.get(("pav",))] if ck == 0 else []))
                    ft[("wt", ck % 4)] = tav
                trc = ve(lambda e: e.reciprocal(S("u5")[64:65, 0:512],
                                                pav[64:65, :]),
                         deps=[tav, ft.get(("rc",))], hazard=True)
                tpr = te(lambda e: e.matmul(
                    pR[0:64, :], S("onesf_p5")[64:65, 0:64],
                    S("u5")[64:65, 0:512], start=True, stop=True,
                    tile_position=(64, 0)),
                    deps=[trc, t_onp, ft.get(("pR",))])
                tpc = sc(lambda e: e.activation(
                    S("rr5"), pR[0:64, :], ACT.Copy),
                    deps=[tpr, ft.get(("rr",))])
                ft[("pR",)] = tpc
                if hs == 0:
                    to = ve(lambda e, gc=gc, tqg=tqg: e.tensor_tensor(
                        S("og5")[0:64, gc * 2048 + tqg * 512:
                                 gc * 2048 + tqg * 512 + 512],
                        pav[0:64, :], S("rr5"), ALU.mult),
                        deps=[tpc, ft.get(("og5", gc, tqg))], hazard=True)
                    tog = to
                else:
                    to = ve(lambda e: e.tensor_tensor(
                        S("ot5"), pav[0:64, :], S("rr5"), ALU.mult),
                        deps=[tpc, ft.get(("ot",))], hazard=True)
                    tog = kb.dma("gpsimd",
                                 S("og5")[64:128, gc * 2048 + tqg * 512:
                                          gc * 2048 + tqg * 512 + 512],
                                 S("ot5"),
                                 deps=[to, ft.get(("og5", gc, tqg)),
                                       kb.p.qtot("qga")], q="qga")
                    tog = kb.p.qtot("qga")
                    ft[("ot",)] = tog
                ft[("rc",)] = to
                ft[("pav",)] = to
                ft[("rr",)] = to
                ft[("og5", gc, tqg)] = tog
                to_all.append(tog)
            ft[("pq", 0)] = ft.get(("psc", 0))
            ft[("pq", 1)] = ft.get(("psc", 1))
            ft[("pRm",)] = ft.get(("psc", 2))
            ft[("pRs",)] = ft.get(("psc", 3))
            ft[("q5",)] = to_all[-1]
        ft[("k5",)] = to_all[-1]
        ft[("v5",)] = to_all[-1]

    two = kb.dma("gpsimd", bass.AP(S("w5o").tensor, 0,
                                   [[8 * 1024, 128], [1024, 8], [1, 1024]]),
                 bass.AP(d["wo1"].tensor, 0,
                         [[1024, 128], [128 * 1024, 8], [1, 1024]]),
                 deps=to_all + [ft.get(("w5g",))], q="qw")
    twot = kb.p.qtot("qw")
    tap_x2f = kb.tap("x2f", (D, 2048))
    for tg in range(4):
        tldf = []
        for dc in range(8):
            tldf.append(kb.dma("sync", S("xf5")[:, dc * 512:(dc + 1) * 512],
                               bass.AP(d["s_out"].tensor,
                                       dc * 128 * VRNG + VHALO + tg * 512,
                                       [[VRNG, 128], [1, 512]]),
                               deps=[ft.get(("xf5",))] + (to_all if tg == 0
                                                          else []),
                               q="qx1"))
        tldft = kb.p.qtot("qx1")
        tr1 = kb.dma("gpsimd", S("strow")[0:1, 0:512],
                     S("stT")[tg * 4:tg * 4 + 4, :],
                     deps=[tT2b, ft.get(("strow",)), kb.p.qtot("qga")],
                     q="qga")
        tr2 = kb.dma("gpsimd", S("strow")[0:1, 512:1024],
                     S("stT")[16 + tg * 4:16 + tg * 4 + 4, :],
                     deps=[tT2b, ft.get(("strow",)), kb.p.qtot("qga")],
                     q="qga")
        trowt = kb.p.qtot("qga")
        trm = trs = None
        for j in range(4):
            trm = te(lambda e, j=j: e.matmul(
                pRm[:, j * 128:(j + 1) * 128],
                S("onesf_p5")[0:1, 0:128],
                S("strow")[0:1, j * 128:j * 128 + 128],
                start=True, stop=True),
                deps=[trowt, t_onp, ft.get(("pRm",))])
            trs = te(lambda e, j=j: e.matmul(
                pRs[:, j * 128:(j + 1) * 128],
                S("onesf_p5")[0:1, 0:128],
                S("strow")[0:1, 512 + j * 128:512 + j * 128 + 128],
                start=True, stop=True),
                deps=[trowt, t_onp, ft.get(("pRs",))])
        for dt in range(8):
            pwo = pq[dt % 2]
            tmm = None
            for ic in range(8):
                tmm = te(lambda e, pwo=pwo, ic=ic, dt=dt, tg=tg: e.matmul(
                    pwo, S("w5o")[:, ic * 1024 + dt * 128:
                                  ic * 1024 + dt * 128 + 128],
                    S("og5")[:, ic * 2048 + tg * 512:
                             ic * 2048 + tg * 512 + 512],
                    start=(ic == 0), stop=False),
                    deps=to_all + [twot, ft.get(("pq", dt % 2))]
                    if ic == 0 else [])
            tmb = te(lambda e, pwo=pwo, dt=dt: e.matmul(
                pwo, S("b5")[96:97, dt * 128: dt * 128 + 128],
                S("ones_rb")[96:97, :], start=False, stop=True,
                tile_position=(96, 0)),
                deps=[twt0, t_onr])
            u1 = ve(lambda e, dt=dt, tg=tg: e.tensor_tensor(
                S("u5"), S("xn1own")[:, dt * 2048 + tg * 512:
                                     dt * 2048 + tg * 512 + 512],
                pRs[:, 0:512], ALU.mult),
                deps=[trs, tblend[dt], ft.get(("u5",))], hazard=True)
            u2 = ve(lambda e: e.scalar_tensor_tensor(
                S("u5"), S("u5"), 1.0, pRm[:, 0:512], ALU.mult, ALU.add),
                deps=[u1, trm], hazard=True)
            u3 = ve(lambda e, pwo=pwo: e.scalar_tensor_tensor(
                S("x2c"), pwo, 1.0, S("u5"), ALU.mult, ALU.add),
                deps=[tmb, u2, ft.get(("x2c",))], hazard=True)
            u4 = ve(lambda e, dt=dt: e.tensor_tensor(
                S("x2c"), S("x2c"), S("xf5")[:, dt * 512:(dt + 1) * 512],
                ALU.add), deps=[u3, tldft], hazard=True)
            ft[("u5",)] = u3
            ft[("pq", dt % 2)] = u3
            tst = kb.dma("sync", d["x2"][dt * 128:(dt + 1) * 128,
                                         tg * 512:(tg + 1) * 512],
                         S("x2c"), deps=[u4], q="qout")
            if tap_x2f is not None:
                kb.dma("sync", tap_x2f[dt * 128:(dt + 1) * 128,
                                       tg * 512:(tg + 1) * 512],
                       S("x2c"), deps=[u4], q="qout")
            ft[("x2c",)] = kb.p.qtot("qout")
        ft[("pRm",)] = u2
        ft[("pRs",)] = u1
        ft[("strow",)] = trs
        ft[("xf5",)] = u4
    kb.tok["p5"] = [kb.p.qtot("qout")]


def build(taps=()):
    nc = bass.Bass("TRN2", target_bir_lowering=False)
    stack = ExitStack()
    with stack:
        kb = KB(nc, stack, taps)
        with nc.Block() as block:
            emit_p0(kb)
            kb.open_scope()
            emit_p1(kb)
            kb.close_scope()
            kb.open_scope()
            emit_p2(kb)
            kb.close_scope()
            kb.open_scope()
            emit_p3f(kb)
            kb.close_scope()
            kb.open_scope()
            emit_p3fb(kb)
            kb.close_scope()
            kb.open_scope()
            emit_p3a(kb)
            kb.close_scope()
            kb.open_scope()
            emit_p4pre(kb)
            kb.close_scope()
            for l in range(2):
                kb.open_scope()
                emit_p4ab(kb, l)
                kb.close_scope()
                kb.open_scope()
                emit_p4c1(kb, l)
                kb.close_scope()
                kb.open_scope()
                emit_p4c2(kb, l)
                kb.close_scope()
            kb.open_scope()
            emit_p5(kb)
            kb.close_scope()
            finalize(kb)
            kb.p.check()
            kb.p.replay(block)
    return nc, kb


def finalize(kb):
    kb.p.emit("sync", lambda e: e.nop(),
              deps=[kb.p.qtot("qout"), kb.p.qtot("qsp0"), kb.p.qtot("qsp1"),
                    kb.p.qtot("qga")])


def kernel(**inputs):
    sh, percore = prep_inputs(inputs)
    taps = tuple(globals().get("_TAPS", ()))
    in_maps = []
    for c in range(8):
        m = dict(sh)
        m.update(percore[c])
        in_maps.append(m)
    nc, kb = build(taps=taps)
    res = run_bass_kernel_spmd(nc, in_maps, core_ids=list(range(8)))
    out = np.zeros((B, T, D), np.float32)
    for c in range(8):
        b, half = c // 2, c % 2
        out[b, half * 2048:(half + 1) * 2048] = res.results[c]["x2"].T
    globals()["_LAST_RES"] = res
    return out


# version 04 (final)
# speedup vs baseline: 1.1535x; 1.1535x over previous
import math
from contextlib import ExitStack

import numpy as np
import ml_dtypes

import concourse.bass as bass
import concourse.mybir as mybir
from concourse.bass_utils import run_bass_kernel_spmd

F32 = mybir.dt.float32
BF16 = mybir.dt.bfloat16
F16 = mybir.dt.float16
U8 = mybir.dt.uint8
I32 = mybir.dt.int32
ALU = mybir.AluOpType
ACT = mybir.ActivationFunctionType
AXX = mybir.AxisListType.X
BF = ml_dtypes.bfloat16

B, T, D = 4, 4096, 1024
NT = T // 128
NH, HD = 8, 128
WIN, TOPK = 64, 32
IFF = 2816
H1, HD1 = 16, 64
INNER, IH, HID = 512, 64, 2730
VHALO = 1152
VRNG = VHALO + 2048
QLO0 = 512
QLO1 = 1024
W1P = 2816
XNW = 1 + VHALO + T
LBIG = 30000.0
SENT = -1.0e30
NEGINF = -3.0e38

ENGINES = ("sync", "gpsimd", "vector", "scalar", "tensor")

CONSTS = [0.0, 1.0, -1.0, 0.5, -0.5, 0.125, -0.125, 1e-8, 1e-5, -20.0, -30.0,
          1.0 / 1024.0, 8.0, 2.0, 1e-24, 0.04, 64.0]


class Prog:
    DMA_SEMS = ("q0", "qg0", "qx0", "qx1", "qt", "qt0", "qt1", "qw", "qw2", "qsp0", "qsp1", "qout", "qaux", "qga")

    def __init__(self, nc, stack):
        self.nc = nc
        self.sems = {}
        self.cnt = {}
        self.ops = {e: [] for e in ENGINES}
        for s in ("ev", "es", "et", "eg", "esp") + self.DMA_SEMS:
            self.sems[s] = stack.enter_context(nc.semaphore(s))
            self.cnt[s] = 0
        self.base = []

    def emit(self, eng, fn, deps=(), dma=None, hazard=False):
        if dma:
            sem, step = dma, 16
        else:
            sem = {"vector": "ev", "scalar": "es", "tensor": "et",
                   "gpsimd": "eg", "sync": "esp"}[eng]
            step = 1
        seen = {}
        for dp in list(deps) + self.base:
            if dp is None:
                continue
            s, v = dp
            if v <= 0:
                continue
            if seen.get(s, 0) < v:
                seen[s] = v
        waits = sorted(seen.items())
        self.cnt[sem] += step
        tok = (sem, self.cnt[sem])
        self.ops[eng].append((waits, fn, sem, step))
        return tok

    def qtot(self, q):
        return (q, self.cnt[q])

    def check(self):
        pos = {e: 0 for e in ENGINES}
        val = {s: 0 for s in self.sems}
        progress = True
        while progress:
            progress = False
            for e in ENGINES:
                while pos[e] < len(self.ops[e]):
                    waits, fn, sem, step = self.ops[e][pos[e]]
                    if all(val[s] >= v for s, v in waits):
                        val[sem] += step
                        pos[e] += 1
                        progress = True
                    else:
                        break
        stuck = {e: pos[e] for e in ENGINES if pos[e] < len(self.ops[e])}
        if stuck:
            msg = []
            for e, pp in stuck.items():
                msg.append(f"{e}@{pp}/{len(self.ops[e])} waits={self.ops[e][pp][0]}")
            raise RuntimeError("deadlock: " + "; ".join(msg) + f" state={val}")

    def replay(self, block):
        prog = self

        def mk(eng_name):
            def body(eng):
                wm = {}
                for waits, fn, sem, step in prog.ops[eng_name]:
                    for s, v in waits:
                        if wm.get(s, 0) >= v:
                            continue
                        eng.wait_ge(prog.sems[s], v)
                        wm[s] = v
                    try:
                        inst = fn(eng)
                    except Exception:
                        import traceback
                        print(f"FAILED op: engine={eng_name} "
                              f"index={prog.ops[eng_name].index((waits, fn, sem, step))}")
                        raise
                    inst.then_inc(prog.sems[sem], step)
            return body

        block.sync(mk("sync"))
        block.gpsimd(mk("gpsimd"))
        block.vector(mk("vector"))
        block.scalar(mk("scalar"))
        block.tensor(mk("tensor"))


class TB:

    __slots__ = ("w", "r")

    def __init__(self, tok=None):
        self.w = tok
        self.r = []

    def rd(self):
        return [self.w] if self.w else []

    def wr(self):
        d = list(self.r)
        if self.w:
            d.append(self.w)
        return d

    def did_rd(self, tok):
        self.r.append(tok)
        if len(self.r) > 6:
            self.r = self.r[-6:] + [max(self.r, key=lambda t: t[1])] if False else self.r[-6:]
        return tok

    def did_wr(self, tok):
        self.w = tok
        self.r = []
        return tok



def _rotary_tables():
    n, dh = T, 64
    half = np.arange(0, dh, 2, dtype=np.float64)
    inv_freq = 1.0 / 10000.0 ** (half / dh)
    t = np.arange(n, dtype=np.float64)
    freqs = t[:, None] * inv_freq[None, :]
    freqs = np.concatenate([freqs, freqs], axis=-1)
    base = (half + 0.4 * dh) / (1.4 * dh)
    power = (t - n // 2) / 256.0
    scale = base[None, :] ** power[:, None]
    scale = np.concatenate([scale, scale], axis=-1)
    cos = np.cos(freqs)
    sin = np.sin(freqs)
    q1 = (cos * scale).astype(np.float32)
    q2 = (sin * scale).astype(np.float32)
    q2[:, :32] = -q2[:, :32]
    k1 = (cos / scale).astype(np.float32)
    k2 = (sin / scale).astype(np.float32)
    k2[:, :32] = -k2[:, :32]
    return q1, q2, k1, k2


def _l0_masks():
    m1 = np.zeros((128, 128), np.float32)
    m2 = np.full((128, 128), -LBIG, np.float32)
    m1_t0 = np.zeros((128, 128), np.float32)
    m2_t0 = np.full((128, 128), -LBIG, np.float32)
    for p in range(128):
        for c in range(128):
            pos_rel = (c - 64) if p < 64 else c
            dd = p - pos_rel
            if 0 <= dd <= 63:
                m2[p, c] = 0.0
                if dd != 0:
                    m1[p, c] = 1.0
                if pos_rel >= 0:
                    m2_t0[p, c] = 0.0
                    if dd != 0:
                        m1_t0[p, c] = 1.0
    dup = np.full((128, 64), -LBIG, np.float32)
    for p in range(64):
        if p <= 62:
            dup[p, 0:(63 - p)] = 0.0
    return m1, m2, m1_t0, m2_t0, dup


def _p4_groups(l):
    qlo = QLO1 if l == 1 else QLO0
    gs = []
    q = qlo
    while q < VRNG:
        w = min(512, VRNG - q)
        gs.append((q, w))
        q += w
    return gs


def _l1_inner_masks(half):
    off = 2048 * half - VHALO
    masks = {}
    for l in range(2):
        for gi, (Q0, W) in enumerate(_p4_groups(l)):
            nch = (512 + W) // 128
            arr = np.zeros((nch, 128, W), np.float32)
            vq = Q0 + np.arange(W)[None, :]
            rq = vq + off
            qbad = (rq < 0) | (rq >= T)
            for r in range(nch):
                C0 = Q0 - 512 + 128 * r
                vk = C0 + np.arange(128)[:, None]
                rk = vk + off
                dd = vq - vk
                ok = (dd >= 0) & (dd <= 512) & (rk >= 0) & (rk < T)
                arr[r] = np.where(qbad | ok, 1.0, 0.0)
            masks[(l, gi)] = arr.astype(ml_dtypes.bfloat16)
    return masks


def prep_inputs(ins):
    sh = {}
    f32 = lambda a: np.ascontiguousarray(a, dtype=np.float32)
    sh["encq"] = f32(np.transpose(ins["L0_attn_encoder_query"], (1, 0, 2)).reshape(D, NH * HD))
    sh["encv"] = f32(np.transpose(ins["L0_attn_encoder_value"], (1, 0, 2)).reshape(D, NH * HD))
    sh["mnw1"] = f32(ins["L0_attn_mn_w1"])
    sh["mnb1"] = f32(ins["L0_attn_mn_b1"]).reshape(1, 512)
    sh["mnw2"] = f32(ins["L0_attn_mn_w2"])
    sh["mnb2"] = f32(ins["L0_attn_mn_b2"]).reshape(1, D)
    sh["basem"] = f32(ins["L0_attn_base_metric"]).reshape(1, NH * HD)
    sh["hebut"] = f32(np.transpose(ins["L0_attn_hebbian_U"], (0, 2, 1)))
    sh["hebv"] = f32(ins["L0_attn_hebbian_V"])
    sh["dec"] = f32(ins["L0_attn_decoder"])
    sh["wg_nat"] = f32(ins["L0_ffn_w_gate"])
    sh["wu_nat"] = f32(ins["L0_ffn_w_up"])
    sh["wd_nat"] = f32(ins["L0_ffn_w_down"])
    Wqkv = ins["L1_attn_Wqkv"].reshape(H1, 3 * HD1, D)
    sh["wqT"] = f32(np.transpose(Wqkv[:, 0:HD1, :], (2, 0, 1)).reshape(D, H1 * HD1))
    sh["wkT"] = f32(np.transpose(Wqkv[:, HD1:2 * HD1, :], (2, 0, 1)).reshape(D, H1 * HD1))
    sh["wvT"] = f32(np.transpose(Wqkv[:, 2 * HD1:, :], (2, 0, 1)).reshape(D, H1 * HD1))
    bqkv = ins["L1_attn_bqkv"].reshape(H1, 3 * HD1)
    sh["bq1"] = f32(bqkv[:, 0:HD1].reshape(1, H1 * HD1))
    sh["bk1"] = f32(bqkv[:, HD1:2 * HD1].reshape(1, H1 * HD1))
    sh["bv1"] = f32(bqkv[:, 2 * HD1:].reshape(1, H1 * HD1))
    sh["wo1"] = f32(ins["L1_attn_Wo"].T)
    sh["bo1"] = f32(ins["L1_attn_bo"]).reshape(1, D)
    sh["fqkv"] = f32(ins["L1_ffn_wqkv"])
    sh["fwo"] = f32(ins["L1_ffn_wo"])
    w2p = np.zeros((2, W1P, D), np.float32)
    w2p[:, 0:HID, :] = ins["L1_ffn_w2"]
    sh["fw2"] = f32(w2p)
    perm = (np.arange(INNER) // IH) * IH + (np.arange(INNER) % IH + 32) % IH
    fqkv = f32(ins["L1_ffn_wqkv"])
    sh["fqkvp"] = np.ascontiguousarray(
        np.concatenate([fqkv[:, :, perm], fqkv[:, :, INNER + perm]], axis=2))
    w1 = f32(ins["L1_ffn_w1"])
    w1a = np.zeros((2, D, W1P), np.float32)
    w1g = np.zeros((2, D, W1P), np.float32)
    w1a[:, :, 0:HID] = w1[:, :, 0:HID]
    w1g[:, :, 0:HID] = w1[:, :, HID:2 * HID]
    sh["w1a"], sh["w1g"] = w1a, w1g
    b1 = f32(ins["L1_ffn_b1"])
    b1c = np.zeros((2, 2 * W1P), np.float32)
    b1c[:, 0:HID] = b1[:, 0:HID]
    b1c[:, W1P:W1P + HID] = b1[:, HID:2 * HID]
    sh["fb1col"] = np.ascontiguousarray(
        b1c.reshape(2, 44, 128).transpose(0, 2, 1))
    sh["flgrow"] = np.ascontiguousarray(
        f32(ins["L1_ffn_ln_g"]).reshape(2, 8, 128))
    sh["flbrow"] = np.ascontiguousarray(
        f32(ins["L1_ffn_ln_b"]).reshape(2, 8, 128))
    bones = np.zeros((128, 32), np.float32)
    for oc in range(4):
        bones[0:64, 8 * oc + 2 * oc] = 1.0
        bones[64:128, 8 * oc + 2 * oc + 1] = 1.0
    sh["bones"] = bones
    hsel8 = np.zeros((8, 4 * 128), np.float32)
    for oc in range(4):
        hsel8[2 * oc, oc * 128: oc * 128 + 64] = 1.0
        hsel8[2 * oc + 1, oc * 128 + 64: oc * 128 + 128] = 1.0
    sh["hsel"] = hsel8
    sh["fb2col"] = np.ascontiguousarray(
        f32(ins["L1_ffn_b2"]).reshape(2, 8, 128).transpose(0, 2, 1))
    qs = f32(ins["L1_ffn_q_scale"])
    ks = f32(ins["L1_ffn_k_scale"])
    pm64 = (np.arange(IH) + 32) % IH
    sh["fqscol"] = np.ascontiguousarray(np.tile(qs, (1, 2))[:, :, None])
    sh["fqspcol"] = np.ascontiguousarray(np.tile(qs[:, pm64], (1, 2))[:, :, None])
    sh["fkscol"] = np.ascontiguousarray(np.tile(ks, (1, 2))[:, :, None])
    sh["fkspcol"] = np.ascontiguousarray(np.tile(ks[:, pm64], (1, 2))[:, :, None])
    m1, m2, m1t0, m2t0, dup = _l0_masks()
    sh["dm1"], sh["dm2"], sh["dm1t0"], sh["dm2t0"], sh["ddup"] = m1, m2, m1t0, m2t0, dup
    sh["ident"] = np.eye(128, dtype=np.float32)
    sh["cpool"] = np.tile(np.array(CONSTS, np.float32), (128, 1))
    q1, q2, k1, k2 = _rotary_tables()
    x = ins["L0_attn_x"]
    percore = []
    for c in range(8):
        b, half = c // 2, c % 2
        S0 = 2048 * half
        off = S0 - VHALO
        rc = np.clip(np.arange(VRNG) + off, 0, T - 1)

        def fm(tab, scale8):
            out = tab[rc].T
            if scale8:
                out = out * np.float32(8.0)
            return np.ascontiguousarray(np.tile(out, (2, 1)), np.float32)
        mk = _l1_inner_masks(half)
        pc = {
            "x0": f32(x[b]),
            "rt1q": fm(q1, True), "rt2q": fm(q2, True),
            "rt1k": fm(k1, False), "rt2k": fm(k2, False),
            "msel": np.full((128, 1), float(half), np.float32),
        }
        for l in range(2):
            for gi in range(len(_p4_groups(l))):
                pc[f"pmsk{l}_{gi}"] = mk[(l, gi)]
        percore.append(pc)
    return sh, percore



class KB:
    def __init__(self, nc, stack, taps=()):
        self.nc = nc
        self.p = Prog(nc, stack)
        self.taps = set(taps)
        self.d = {}
        self.s = {}
        self.psd = {}
        self.tok = {}
        self.scope = None
        self._declare()

    def open_scope(self):
        assert self.scope is None
        self.scope = ExitStack()

    def close_scope(self):
        self.scope.close()
        self.scope = None
        self.p.base = [(s_, v) for s_, v in self.p.cnt.items() if v > 0]

    def dram(self, name, shape, dtype=F32, kind="Internal"):
        self.d[name] = self.nc.dram_tensor(name, list(shape), dtype, kind=kind).ap()
        return self.d[name]

    def _uniq(self, name):
        self._nmc = getattr(self, "_nmc", {})
        n = self._nmc.get(name, 0)
        self._nmc[name] = n + 1
        return name if n == 0 else f"{name}__{n}"

    def sb(self, name, shape, dtype=F32, scoped=False, **kw):
        un = self._uniq(name)
        if scoped:
            h = self.scope.enter_context(
                self.nc.sbuf_tensor(un, list(shape), dtype, **kw))
            self.s[name] = h.ap() if hasattr(h, "ap") else h[:]
        else:
            self.s[name] = self.nc.alloc_sbuf_tensor(
                un, list(shape), dtype, **kw).ap()
        return self.s[name]

    def psum(self, name, shape, dtype=F32, scoped=False):
        un = self._uniq(name)
        if scoped:
            h = self.scope.enter_context(
                self.nc.psum_tensor(un, list(shape), dtype))
            self.psd[name] = h.ap() if hasattr(h, "ap") else h[:]
        else:
            self.psd[name] = self.nc.alloc_psum_tensor(un, list(shape), dtype).ap()
        return self.psd[name]

    def _declare(self):
        d = self.dram
        d("x0", (T, D), kind="ExternalInput")
        for nm, shape, dt in [
            ("encq", (D, NH * HD), F32), ("encv", (D, NH * HD), F32),
            ("mnw1", (D, 512), F32), ("mnb1", (1, 512), F32),
            ("mnw2", (512, D), F32), ("mnb2", (1, D), F32),
            ("basem", (1, NH * HD), F32),
            ("hebut", (NH, 32, HD), F32), ("hebv", (NH, 32, HD), F32),
            ("dec", (D, D), F32),
            ("wg_nat", (IFF, D), F32), ("wu_nat", (IFF, D), F32),
            ("wd_nat", (D, IFF), F32),
            ("wqT", (D, H1 * HD1), F32), ("wkT", (D, H1 * HD1), F32),
            ("wvT", (D, H1 * HD1), F32),
            ("bq1", (1, H1 * HD1), F32), ("bk1", (1, H1 * HD1), F32),
            ("bv1", (1, H1 * HD1), F32),
            ("wo1", (D, D), F32), ("bo1", (1, D), F32),
            ("fqkv", (2, D, 3 * INNER), F32),
            ("fqkvp", (2, D, 2 * INNER), F32), ("fwo", (2, INNER, D), F32),
            ("w1a", (2, D, W1P), F32), ("w1g", (2, D, W1P), F32),
            ("fb1col", (2, 128, 44), F32),
            ("flgrow", (2, 8, 128), F32), ("flbrow", (2, 8, 128), F32),
            ("fb2col", (2, 128, 8), F32),
            ("bones", (128, 32), F32), ("hsel", (8, 4 * 128), F32),
            ("fw2", (2, W1P, D), F32),
            ("fqscol", (2, 128, 1), F32), ("fqspcol", (2, 128, 1), F32),
            ("fkscol", (2, 128, 1), F32), ("fkspcol", (2, 128, 1), F32),
            ("dm1", (128, 128), F32), ("dm2", (128, 128), F32),
            ("dm1t0", (128, 128), F32), ("dm2t0", (128, 128), F32),
            ("ddup", (128, 64), F32),
            ("ident", (128, 128), F32), ("cpool", (128, len(CONSTS)), F32),
            ("rt1q", (128, VRNG), F32), ("rt2q", (128, VRNG), F32),
            ("rt1k", (128, VRNG), F32), ("rt2k", (128, VRNG), F32),
            ("msel", (128, 1), F32),
        ]:
            d(nm, shape, dt, kind="ExternalInput")
        for l in range(2):
            for gi, (Q0, W) in enumerate(_p4_groups(l)):
                d(f"pmsk{l}_{gi}", ((512 + W) // 128, 128, W), BF16,
                  kind="ExternalInput")
        d("x2", (D, 2048), kind="ExternalOutput")
        d("s_ffn0", (T, D))
        d("s_hq", (T, IFF), BF16)
        d("s_x1", (VHALO + T, D))
        d("s_xn1fm", (D, XNW), BF16)
        d("s_xv", (D, 1 + VRNG), BF16)
        d("s_mid", (D, VRNG), BF16)
        d("s_out", (D, VRNG), BF16)
        d("s_midf", (D, VRNG), F32)
        d("s_outf", (D, VRNG), F32)
        d("s_h3", (W1P, VRNG - QLO0), BF16)
        cp = self.sb("c_pool", (128, len(CONSTS)), F32)
        for i, v in enumerate(CONSTS):
            self.nc.const_aps.register(F32, float(v), cp[:, i:i + 1])

    def tap(self, name, shape, dtype=F32):
        if name not in self.taps:
            return None
        return self.nc.dram_tensor("tap_" + name, list(shape), dtype,
                                   kind="ExternalOutput").ap()

    def dma(self, eng, out, in_, deps=(), q="qaux"):
        if q == "qout":
            deps = list(deps) + [self.p.qtot("qout")]
        return self.p.emit(eng, lambda e, o=out, i=in_: e.dma_start(o, i),
                           deps=deps, dma=q)

    def dmat(self, out, in_, deps=(), q="qt"):
        return self.p.emit("sync", lambda e, o=out, i=in_: e.dma_start_transpose(o, i),
                           deps=deps, dma=q)

    def ve(self, fn, deps=(), hazard=False):
        return self.p.emit("vector", fn, deps=deps, hazard=hazard)

    def sc(self, fn, deps=(), hazard=False):
        return self.p.emit("scalar", fn, deps=deps, hazard=hazard)

    def te(self, fn, deps=(), hazard=False):
        return self.p.emit("tensor", fn, deps=deps, hazard=hazard)

    def ge(self, fn, deps=(), hazard=False):
        return self.p.emit("gpsimd", fn, deps=deps, hazard=hazard)


def wload_chunked(kb, dst_name, src_name, nchunk, width, q="qw", deps=()):
    src = bass.AP(kb.d[src_name].tensor, 0,
                  [[width, 128], [128 * width, nchunk], [1, width]])
    dst = bass.AP(kb.s[dst_name].tensor, 0,
                  [[nchunk * width, 128], [width, nchunk], [1, width]])
    return kb.dma("gpsimd", dst, src, q=q, deps=deps)



def emit_p0(kb):
    s, d = kb.sb, kb.d
    s("c_ident", (128, 128), BF16)
    s("c_identf", (128, 128), F32)
    s("c_ones_col", (128, 1), BF16)
    s("c_ones_row", (1, 128), BF16)
    s("r_mnb1", (1, 512), F32)
    s("r_mnb2", (1, 1024), F32)
    s("r_basem", (1, 1024), F32)
    s("m_dm1", (128, 128), F32)
    s("m_dm2", (128, 128), F32)
    s("m_dm1t0", (128, 128), F32)
    s("m_dm2t0", (128, 128), F32)
    s("m_ddup", (128, 64), F32)
    toks = []
    toks.append(kb.dma("sync", kb.s["c_pool"], d["cpool"], q="q0"))
    toks.append(kb.dma("gpsimd", kb.s["c_ident"], d["ident"], q="qg0"))
    toks.append(kb.dma("sync", kb.s["c_identf"], d["ident"], q="q0"))

    def chunked(dst_name, src_name, nchunk, width):
        src = bass.AP(d[src_name].tensor, 0,
                      [[width, 128], [128 * width, nchunk], [1, width]])
        dstt = kb.s[dst_name]
        dst = bass.AP(dstt.tensor, 0,
                      [[nchunk * width, 128], [width, nchunk], [1, width]])
        toks.append(kb.dma("gpsimd", dst, src, q="qg0"))

    for nm, dn in [("r_mnb1", "mnb1"), ("r_mnb2", "mnb2"), ("r_basem", "basem")]:
        toks.append(kb.dma("sync", kb.s[nm], d[dn], q="q0"))
    for nm, dn in [("m_dm1", "dm1"), ("m_dm2", "dm2"), ("m_dm1t0", "dm1t0"),
                   ("m_dm2t0", "dm2t0"), ("m_ddup", "ddup")]:
        toks.append(kb.dma("sync", kb.s[nm], d[dn], q="q0"))
    s("ln_mean", (128, NT), F32)
    s("ln_inv", (128, NT), F32)
    s("ln1_mean", (128, NT), F32)
    s("ln1_inv", (128, NT), F32)
    s("st_sum", (128, NT), F32)
    s("st_s2", (128, NT), F32)
    s("st_m2", (128, NT), F32)
    s("sig_acc", (128, NH), F32)
    s("sig8_acc", (128, NH), F32)
    s("metcol2", (128, NH), F32)
    s("w_hm", (128, 8 * 128), BF16)
    s("c_msel", (128, 1), F32)
    toks.append(kb.ge(lambda e: e.memset(kb.s["c_ones_col"], 1.0)))
    toks.append(kb.ge(lambda e: e.memset(kb.s["c_ones_row"], 1.0)))
    toks.append(kb.dma("sync", kb.s["c_msel"], d["msel"], q="q0"))
    kb.tok["p0"] = [t for t in toks if t]
    kb.p.base = list(kb.tok["p0"])



def emit_p1(kb):
    s, d = kb.sb, kb.d
    s("w_encq", (128, 8 * 1024), BF16, scoped=True)
    t_encq = wload_chunked(kb, "w_encq", "encq", 8, 1024)
    kb.p.base = kb.p.base + [t_encq]
    for i in range(2):
        s(f"p1_x{i}", (128, 4 * D), F32, scoped=True)
        s(f"p1_fm{i}", (128, 8 * 512), BF16, scoped=True, is_transpose_dest=True)
        s(f"p1_xn{i}", (128, D), BF16, scoped=True)
        s(f"p1_sq{i}", (128, D), F32, scoped=True)
    s("p1_rl", (128, 512), F32, scoped=True)
    s("p1_a1", (128, 1), F32, scoped=True)
    s("p1_a2", (128, 1), F32, scoped=True)
    kb.psum("pq0", (128, 512), scoped=True)
    kb.psum("pq1", (128, 512), scoped=True)
    z1 = kb.ve(lambda e: e.memset(kb.s["sig_acc"], 0.0))
    z2 = kb.ve(lambda e: e.memset(kb.s["sig8_acc"], 0.0))
    p0 = kb.tok["p0"]
    x0 = d["x0"]
    buf_free = [None, None]
    ps_free = [None, None]
    sig_last = (z1, z2)
    relu_free = None
    for g in range(8):
        bi = g % 2
        xg, fmg = kb.s[f"p1_x{bi}"], kb.s[f"p1_fm{bi}"]
        xnb, sqb = kb.s[f"p1_xn{bi}"], kb.s[f"p1_sq{bi}"]
        src = bass.AP(x0.tensor, g * 512 * D, [[D, 128], [128 * D, 4], [1, D]])
        dst = bass.AP(xg.tensor, 0, [[4 * D, 128], [D, 4], [1, D]])
        tld = kb.dma("sync", dst, src, deps=[buf_free[bi]], q=f"qx{bi}")
        tr_toks = []
        t_xn_prev = None
        for j in range(4):
            ti = g * 4 + j
            xt = xg[:, j * D:(j + 1) * D]
            c_sum = kb.s["st_sum"][:, ti:ti + 1]
            c_s2 = kb.s["st_s2"][:, ti:ti + 1]
            c_m2 = kb.s["st_m2"][:, ti:ti + 1]
            c_mean = kb.s["ln_mean"][:, ti:ti + 1]
            c_inv = kb.s["ln_inv"][:, ti:ti + 1]
            t1 = kb.ve(lambda e, xt=xt, c=c_sum: e.reduce_sum(c, xt, axis=AXX),
                       deps=[tld])
            tq = kb.sc(lambda e, xt=xt, sq=sqb: e.square(sq, xt),
                       deps=[tld, t_xn_prev])
            t2 = kb.ve(lambda e, sq=sqb, c=c_s2: e.reduce_sum(c, sq, axis=AXX),
                       deps=[tq])
            tm = kb.ve(lambda e, a=c_sum, o=c_mean: e.tensor_scalar(
                o, a, 1.0 / 1024.0, None, ALU.mult), deps=[t1])
            tm2 = kb.ve(lambda e, a=c_mean, o=c_m2: e.tensor_tensor(
                o, a, a, ALU.mult), deps=[tm], hazard=True)
            tv = kb.ve(lambda e, a=c_s2, b2=c_m2: e.scalar_tensor_tensor(
                a, a, 1.0 / 1024.0, b2, ALU.mult, ALU.subtract),
                deps=[t2, tm2], hazard=True)
            tl = kb.sc(lambda e, a=c_s2: e.activation(a, a, ACT.Ln, bias=1e-5),
                       deps=[tv])
            tiv = kb.sc(lambda e, a=c_s2, o=c_inv: e.activation(
                o, a, ACT.Exp, scale=-0.5), deps=[tl], hazard=True)
            t_xn = kb.ve(lambda e, xt=xt, m=c_mean, iv=c_inv, o=xnb:
                         e.scalar_tensor_tensor(
                             o, xt, m, iv.to_broadcast([128, D]),
                             ALU.subtract, ALU.mult),
                         deps=[tiv, tm] + tr_toks[-8:], hazard=True)
            t_xn_prev = t_xn
            for kc in range(8):
                tt = kb.dmat(fmg[:, kc * 512 + j * 128: kc * 512 + (j + 1) * 128],
                             xnb[:, kc * 128:(kc + 1) * 128],
                             deps=[t_xn, ps_free[bi]])
                tr_toks.append(tt)
        mm_last = {}
        for h in range(NH):
            pb = kb.psd[f"pq{h % 2}"]
            deps0 = tr_toks + p0 + [mm_last.get(h % 2), relu_free]
            tmm = None
            for kc in range(8):
                lhs = kb.s["w_encq"][:, kc * 1024 + h * 128: kc * 1024 + (h + 1) * 128]
                rhs = fmg[:, kc * 512:(kc + 1) * 512]
                tmm = kb.te(lambda e, pb=pb, lhs=lhs, rhs=rhs, kc=kc: e.matmul(
                    pb, lhs, rhs, start=(kc == 0), stop=(kc == 7)),
                    deps=deps0 if kc == 0 else ())
            t_r = kb.sc(lambda e, pb=pb: e.activation(
                kb.s["p1_rl"], pb, ACT.Relu, accum_out=kb.s["p1_a1"]),
                deps=[tmm, sig_last[0]])
            v8 = kb.s["p1_rl"].rearrange("p (a b) -> p b a", b=8)[:, 0:1, :]
            t_r8 = kb.ve(lambda e, v8=v8: e.reduce_sum(
                kb.s["p1_a2"], v8, axis=AXX), deps=[t_r, sig_last[1]])
            ta = kb.ve(lambda e, h=h: e.tensor_tensor(
                kb.s["sig_acc"][:, h:h + 1], kb.s["sig_acc"][:, h:h + 1],
                kb.s["p1_a1"], ALU.add), deps=[t_r, sig_last[0]], hazard=True)
            tb = kb.ve(lambda e, h=h: e.tensor_tensor(
                kb.s["sig8_acc"][:, h:h + 1], kb.s["sig8_acc"][:, h:h + 1],
                kb.s["p1_a2"], ALU.add), deps=[t_r8, sig_last[1]], hazard=True)
            sig_last = (ta, tb)
            relu_free = tb
            mm_last[h % 2] = t_r8
        buf_free[bi] = sig_last[1]
        ps_free[bi] = sig_last[1]
    kb.tok["p1"] = list(sig_last)
    kb.tok["p1_taps"] = []
    tp = kb.tap("sig", (128, NH))
    if tp is not None:
        kb.dma("sync", tp, kb.s["sig_acc"], deps=list(sig_last), q="qout")
        for nm, buf in [("sig8", "sig8_acc"), ("lnmean", "ln_mean"),
                        ("lninv", "ln_inv")]:
            ap = kb.tap(nm, (128, NH if nm == "sig8" else NT))
            if ap is not None:
                kb.dma("sync", ap, kb.s[buf], deps=list(sig_last), q="qout")
        kb.tok["p1_taps"] = [kb.p.qtot("qout")]



def emit_p2(kb):
    s = kb.sb
    p1 = kb.tok["p1"]
    p0 = kb.tok["p0"]
    s("w_mnw1", (128, 8 * 512), BF16, scoped=True)
    s("w_mnw2", (128, 4 * 1024), BF16, scoped=True)
    s("w_hebut", (32, 8 * 128), BF16, scoped=True)
    s("w_hebv", (32, 8 * 128), BF16, scoped=True)
    wload_chunked(kb, "w_mnw1", "mnw1", 8, 512)
    wload_chunked(kb, "w_mnw2", "mnw2", 4, 1024)
    for nm, dn in [("w_hebut", "hebut"), ("w_hebv", "hebv")]:
        src = bass.AP(kb.d[dn].tensor, 0, [[128, 32], [32 * 128, 8], [1, 128]])
        dst = bass.AP(kb.s[nm].tensor, 0, [[8 * 128, 32], [128, 8], [1, 128]])
        kb.dma("gpsimd", dst, src, q="qw")
    kb.p.base = kb.p.base + [kb.p.qtot("qw")]
    s("sigb", (128, NH), BF16, scoped=True)
    s("r_hid", (1, 512), F32, scoped=True)
    s("r_hidb", (1, 512), BF16, scoped=True)
    s("hidcol", (128, 4), BF16, scoped=True)
    s("r_met", (1, 1024), F32, scoped=True)
    ps_mn = kb.psum("ps_mn", (1, 512), scoped=True)
    ps_mod = kb.psum("ps_mod", (1, 1024), scoped=True)
    ps_tc = kb.psum("ps_tc", (128, 8), BF16, scoped=True)
    ps_hm = kb.psum("ps_hm", (128, 128), scoped=True)
    ps_mc = kb.psum("ps_mc", (128, 8), scoped=True)
    t0 = kb.ve(lambda e: e.tensor_scalar(
        kb.s["sig8_acc"], kb.s["sig8_acc"], 0.5 / 512.0, None, ALU.mult),
        deps=p1 + kb.tok.get("p1_taps", []), hazard=True)
    t0 = kb.ve(lambda e: e.scalar_tensor_tensor(
        kb.s["sigb"], kb.s["sig_acc"], 1.0 / 4096.0, kb.s["sig8_acc"],
        ALU.mult, ALU.add), deps=[t0], hazard=True)
    tmm = None
    for kc in range(8):
        h = kc
        lhs = kb.s["sigb"][:, kc:kc + 1]
        rhs = kb.s["w_mnw1"][:, kc * 512:(kc + 1) * 512]
        tmm = kb.te(lambda e, lhs=lhs, rhs=rhs, kc=kc: e.matmul(
            ps_mn, lhs, rhs, start=(kc == 0), stop=(kc == 7)),
            deps=[t0] + p0 if kc == 0 else ())
    th = kb.ve(lambda e: e.tensor_tensor(kb.s["r_hid"], ps_mn, kb.s["r_mnb1"],
                                         ALU.add), deps=[tmm])
    th = kb.sc(lambda e: e.activation(kb.s["r_hid"], kb.s["r_hid"], ACT.Gelu),
               deps=[th], hazard=True)
    th = kb.ve(lambda e: e.tensor_copy(kb.s["r_hidb"], kb.s["r_hid"]), deps=[th])
    tt = []
    for kc in range(4):
        t = kb.te(lambda e, kc=kc: e.transpose(
            ps_tc[:, 2 * kc:2 * kc + 1], kb.s["r_hidb"][0:1, kc * 128:(kc + 1) * 128],
            kb.s["c_ident"][0:1, 0:1]), deps=[th])
        tt.append(t)
    pv = ps_tc.rearrange("p (a b) -> p a b", b=2)[:, :, 0:1]
    tev = kb.ve(lambda e: e.tensor_copy(
        kb.s["hidcol"].rearrange("p a -> p a 1" if False else "p (a b) -> p a b", b=1), pv), deps=tt)
    tmm2 = None
    for kc in range(4):
        for nh_ in range(2):
            lhs = kb.s["hidcol"][:, kc:kc + 1]
            rhs = kb.s["w_mnw2"][:, kc * 1024 + nh_ * 512: kc * 1024 + nh_ * 512 + 512]
            tmm2 = kb.te(lambda e, lhs=lhs, rhs=rhs, kc=kc, nh_=nh_: e.matmul(
                ps_mod[:, nh_ * 512:(nh_ + 1) * 512], lhs, rhs,
                start=(kc == 0), stop=(kc == 3)), deps=[tev] if kc == 0 else ())
    tz = kb.ve(lambda e: e.tensor_tensor(kb.s["r_met"], ps_mod, kb.s["r_mnb2"],
                                         ALU.add), deps=[tmm2])
    tz = kb.ve(lambda e: e.scalar_tensor_tensor(
        kb.s["r_met"], kb.s["r_met"], 0.1, kb.s["r_basem"], ALU.mult, ALU.add),
        deps=[tz], hazard=True)
    tz = kb.sc(lambda e: e.activation(kb.s["r_met"], kb.s["r_met"], ACT.Exp),
               deps=[tz])
    tz = kb.sc(lambda e: e.activation(kb.s["r_met"], kb.s["r_met"], ACT.Ln,
                                      bias=1.0), deps=[tz], hazard=True)
    tz = kb.ve(lambda e: e.tensor_scalar(kb.s["r_met"], kb.s["r_met"], -2.0,
                                         None, ALU.mult), deps=[tz])
    tp = kb.tap("metric", (1, 1024))
    if tp is not None:
        kb.dma("sync", tp, kb.s["r_met"], deps=[tz], q="qout")
    tt2 = []
    for h in range(NH):
        t = kb.te(lambda e, h=h: e.transpose(
            ps_mc[:, h:h + 1], kb.s["r_met"][0:1, h * 128:(h + 1) * 128],
            kb.s["c_identf"][0:1, 0:1]), deps=[tz])
        tt2.append(t)
    tmc = kb.ve(lambda e: e.tensor_copy(kb.s["metcol2"], ps_mc), deps=tt2)
    hm_toks = []
    tprev = None
    for h in range(NH):
        t = kb.te(lambda e, h=h: e.matmul(
            ps_hm, kb.s["w_hebut"][:, h * 128:(h + 1) * 128],
            kb.s["w_hebv"][:, h * 128:(h + 1) * 128], start=True, stop=True),
            deps=p0 + tt2 + [tprev])
        t2 = kb.ve(lambda e, h=h: e.tensor_copy(
            kb.s["w_hm"][:, h * 128:(h + 1) * 128], ps_hm), deps=[t])
        tprev = t2
        hm_toks.append(t2)
    kb.tok["p2"] = [tmc] + hm_toks





def emit_quant(kb, src, src_is_f32, W, out, scr, deps):
    nb = W // 16
    bm, scf, rcp = scr["bm"][:, 0:nb], scr["scf"][:, 0:nb], scr["rcp"][:, 0:nb]
    rcpb, scb = scr["rcpb"][:, 0:nb], scr["scb"][:, 0:nb]
    u = scr["u"][:, 0:W]
    tA, tB = scr["tA"][:, 0:W], scr["tB"][:, 0:W]
    tC, tD = scr["tC"][:, 0:W], scr["tD"][:, 0:W]
    tE = scr["tE"][:, 0:W]
    src3 = src.rearrange("p (a b) -> p a b", b=16)

    def bc16(ap):
        return bass.AP(ap.tensor, 0, [[ap.tensor.shape[1], 128], [1, nb], [0, 16]])

    t = kb.ve(lambda e: e.reduce_max(bm, src3, axis=AXX,
                                     apply_absolute_value=True), deps=deps)
    t = kb.ve(lambda e: e.tensor_scalar(scf, bm, 6e-12, 1.0 / 6.0,
                                        ALU.max, ALU.mult), deps=[t])
    t = kb.ve(lambda e: e.reciprocal(rcp, scf), deps=[t])
    tsc = kb.ve(lambda e: e.tensor_copy(scb, scf), deps=[t])
    if src_is_f32:
        rr, trc = rcp, t
    else:
        trc = kb.ve(lambda e: e.tensor_copy(rcpb, rcp), deps=[t])
        rr = rcpb
    u3 = u.rearrange("p (a b) -> p a b", b=16)
    t = kb.ve(lambda e, rr=rr: e.tensor_tensor(u3, src3, bc16(rr), ALU.mult),
              deps=[trc, tsc] + deps)
    tsg = kb.ve(lambda e: e.tensor_scalar(tE, u, 0.0, 2.0, ALU.is_gt, ALU.mult),
                deps=[t])
    tau = kb.sc(lambda e: e.activation(u, u, ACT.Abs), deps=[tsg])
    masks = []
    for buf, thr, dg in [(tA, 0.25, 0.5), (tB, 0.75, 0.5), (tC, 1.25, 0.5),
                         (tD, 1.75, 0.5)]:
        masks.append(kb.ve(lambda e, b_=buf, th=thr, dg=dg: e.tensor_scalar(
            b_, u, th, dg, ALU.is_gt, ALU.mult), deps=[tau]))
    t1 = kb.ve(lambda e: e.tensor_tensor(tA, tA, tB, ALU.add), deps=masks)
    t2 = kb.ve(lambda e: e.tensor_tensor(tC, tC, tD, ALU.add), deps=masks[2:])
    t3 = kb.ve(lambda e: e.tensor_tensor(tA, tA, tC, ALU.add), deps=[t1, t2])
    m5 = kb.ve(lambda e: e.tensor_scalar(tB, u, 2.5, 1.0, ALU.is_gt, ALU.mult),
               deps=[t1, t3])
    m6 = kb.ve(lambda e: e.tensor_scalar(tD, u, 3.5, 1.0, ALU.is_gt, ALU.mult),
               deps=[t2, t3])
    m7 = kb.ve(lambda e: e.tensor_scalar(tC, u, 5.0, 2.0, ALU.is_gt, ALU.mult),
               deps=[t3])
    t4 = kb.ve(lambda e: e.tensor_tensor(tB, tB, tD, ALU.add), deps=[m5, m6])
    t5 = kb.ve(lambda e: e.tensor_tensor(tA, tA, tC, ALU.add), deps=[t3, m7])
    tf = kb.ve(lambda e: e.tensor_tensor(tA, tA, tB, ALU.add), deps=[t4, t5])
    tA3 = tA.rearrange("p (a b) -> p a b", b=16)
    tqa = kb.ve(lambda e: e.tensor_tensor(tA3, tA3, bc16(scb), ALU.mult),
                deps=[tf])
    tw = kb.ve(lambda e: e.tensor_tensor(tC, tA, tE, ALU.mult), deps=[tqa])
    return kb.ve(lambda e: e.tensor_tensor(out, tC, tA, ALU.subtract),
                 deps=[tw])


def mk_qscr(kb, pfx, width=IFF):
    scr = {}
    for nm, w_, dt_ in [("bm", width // 16, F32), ("scf", width // 16, F32),
                        ("rcp", width // 16, F32), ("rcpb", width // 16, BF16),
                        ("scb", width // 16, mybir.dt.float16),
                        ("u", width, F32),
                        ("tA", width, mybir.dt.float16),
                        ("tB", width, mybir.dt.float16),
                        ("tC", width, mybir.dt.float16),
                        ("tD", width, mybir.dt.float16),
                        ("tE", width, mybir.dt.float16)]:
        scr[nm] = kb.sb(pfx + "_q_" + nm, (128, w_), dt_, scoped=True)
    return scr


def emit_wquant(kb, kb_s_dst, src_dram, nrows, kwidth, scr, ldbuf, qbuf, deps):
    ntile = nrows // 128
    nkc = kwidth // 128
    toks = []
    prev_q = deps
    prev_tr = kb.p.qtot("qt")
    for ni in range(ntile):
        tld = kb.dma("sync", ldbuf[:, 0:kwidth],
                     src_dram[ni * 128:(ni + 1) * 128, :],
                     deps=prev_q, q="qaux")
        tq = emit_quant(kb, ldbuf[:, 0:kwidth], True, kwidth, qbuf[:, 0:kwidth],
                        scr, [tld, prev_tr])
        trs = []
        for kc in range(nkc):
            trs.append(kb.dmat(
                kb.s[kb_s_dst][:, kc * nrows + ni * 128: kc * nrows + (ni + 1) * 128],
                qbuf[:, kc * 128:(kc + 1) * 128], deps=[tq], q="qt"))
        prev_tr = kb.p.qtot("qt")
        prev_q = [tq]
        toks = trs
    return [kb.p.qtot("qt")]



def emit_p3f(kb):
    s, d = kb.sb, kb.d
    NCH = [512] * 5 + [256]
    s("w_wg", (128, 8 * IFF), BF16, scoped=True)
    s("w_wu", (128, 8 * IFF), BF16, scoped=True)
    for i in range(2):
        s(f"f_x{i}", (128, D), F32, scoped=True)
        kb.s[f"f_xn{i}"] = kb.s[f"f_x{i}"]
        s(f"f_xq{i}", (128, D), BF16, scoped=True)
        s(f"f_xqfm{i}", (128, 8 * 128), BF16, scoped=True, is_transpose_dest=True)
        s(f"f_h{i}", (128, IFF), BF16, scoped=True)
        s(f"f_hq{i}", (128, IFF), BF16, scoped=True)
    s("f_sl", (128, 512), F32, scoped=True)
    s("f_wnat", (128, D), F32, scoped=True)
    scr = mk_qscr(kb, "f")
    scrx = mk_qscr(kb, "fx", width=D)
    pg = kb.psum("pg", (128, 512), scoped=True)
    pu = kb.psum("pu", (128, 512), scoped=True)
    emit_wquant(kb, "w_wg", kb.d["wg_nat"], IFF, D, scr, kb.s["f_wnat"],
                kb.s["f_hq0"], [])
    tw_end = emit_wquant(kb, "w_wu", kb.d["wu_nat"], IFF, D, scr,
                         kb.s["f_wnat"], kb.s["f_hq0"], [])
    twq = kb.p.qtot("qt")
    tap_xq = kb.tap("xq", (T, D), BF16)
    tap_hq = kb.tap("hq", (128, IFF), BF16)

    x0 = d["x0"]
    free_tok = {}
    free_tok[("hqv", 0)] = twq
    quant_free = kb.p.qtot("ev")
    qf_x = quant_free
    sl_free = None
    pgpu_free = None
    for ti in range(NT):
        bi = ti % 2
        xb = kb.s[f"f_x{bi}"]
        xn = kb.s[f"f_xn{bi}"]
        xq = kb.s[f"f_xq{bi}"]
        xqfm = kb.s[f"f_xqfm{bi}"]
        hh = kb.s[f"f_h{bi}"]
        hq = kb.s[f"f_hq{bi}"]
        tld = kb.dma("sync", xb, x0[ti * 128:(ti + 1) * 128, :],
                     deps=[free_tok.get(("x", bi))], q=f"qx{bi}")
        c_mean = kb.s["ln_mean"][:, ti:ti + 1]
        c_inv = kb.s["ln_inv"][:, ti:ti + 1]
        txn = kb.ve(lambda e, xb=xb, xn=xn, m=c_mean, iv=c_inv:
                    e.scalar_tensor_tensor(xn, xb, m, iv.to_broadcast([128, D]),
                                           ALU.subtract, ALU.mult),
                    deps=[tld, free_tok.get(("xn", bi))])
        tq = emit_quant(kb, xn, True, D, xq, scrx,
                        [txn, qf_x, free_tok.get(("xq", bi)),
                         free_tok.get(("xq_t", bi))])
        qf_x = tq
        free_tok[("x", bi)] = tq
        free_tok[("xn", bi)] = tq
        ttr = []
        for kc in range(8):
            ttr.append(kb.dmat(xqfm[:, kc * 128:(kc + 1) * 128],
                               xq[:, kc * 128:(kc + 1) * 128],
                               deps=[tq, free_tok.get(("xqfm", bi))],
                               q=f"qt{bi}"))
        free_tok[("xq", bi)] = ttr[-1]
        th_toks = []
        hq_free = free_tok.get(("hq", bi))
        for ci, cw in enumerate(NCH):
            co = sum(NCH[:ci])
            tmmg = tmmu = None
            for kc in range(8):
                lhs = xqfm[:, kc * 128:(kc + 1) * 128]
                rg = kb.s["w_wg"][:, kc * IFF + co: kc * IFF + co + cw]
                ru = kb.s["w_wu"][:, kc * IFF + co: kc * IFF + co + cw]
                dep0 = ttr + [twq, pgpu_free] if kc == 0 else []
                tmmg = kb.te(lambda e, l_=lhs, r_=rg, kc=kc, cw=cw: e.matmul(
                    pg[:, 0:cw], l_, r_, start=(kc == 0), stop=(kc == 7)),
                    deps=dep0)
                tmmu = kb.te(lambda e, l_=lhs, r_=ru, kc=kc, cw=cw: e.matmul(
                    pu[:, 0:cw], l_, r_, start=(kc == 0), stop=(kc == 7)),
                    deps=[])
            tsl = kb.sc(lambda e, cw=cw: e.activation(
                kb.s["f_sl"][:, 0:cw], pg[:, 0:cw], ACT.Silu),
                deps=[tmmg, tmmu, sl_free])
            th = kb.ve(lambda e, hh=hh, co=co, cw=cw: e.tensor_tensor(
                hh[:, co:co + cw], pu[:, 0:cw], kb.s["f_sl"][:, 0:cw],
                ALU.mult), deps=[tsl, hq_free])
            sl_free = th
            pgpu_free = th
            th_toks.append(th)
        tqh = emit_quant(kb, hh, False, IFF, hq, scr,
                         th_toks + [quant_free, free_tok.get(("hqv", bi)),
                                    free_tok.get(("hqv_t", bi))])
        quant_free = tqh
        tsp = kb.dma("sync", d["s_hq"][ti * 128:(ti + 1) * 128, :], hq,
                     deps=[tqh], q=f"qsp{bi}")
        if tap_xq is not None:
            kb.dma("sync", tap_xq[ti * 128:(ti + 1) * 128, :], xq,
                   deps=[tq], q="qout")
            free_tok[("xq_t", bi)] = kb.p.qtot("qout")
        if tap_hq is not None and ti == 3:
            kb.dma("sync", tap_hq, hq, deps=[tqh], q="qout")
        free_tok[("hqv", bi)] = kb.p.qtot(f"qsp{bi}")
        free_tok[("hqv_t", bi)] = (kb.p.qtot("qout") if tap_hq is not None
                                   else None)
        free_tok[("xqfm", bi)] = kb.p.qtot("et")
    kb.tok["p3f"] = [kb.p.qtot("qsp0"), kb.p.qtot("qsp1")]


def emit_p3fb(kb):
    s, d = kb.sb, kb.d
    s("w_wd", (128, 22 * 1024), BF16, scoped=True)
    for i in range(2):
        s(f"fb_hqfm{i}", (128, 22 * 128), BF16, scoped=True,
          is_transpose_dest=True)
        s(f"fb_out{i}", (128, D), F32, scoped=True)
    s("fb_wnat", (128, IFF), F32, scoped=True)
    s("fb_wq", (128, IFF), BF16, scoped=True)
    pd = [kb.psum(f"pd{i}", (128, 512), scoped=True) for i in range(2)]
    scr = mk_qscr(kb, "fb")
    emit_wquant(kb, "w_wd", kb.d["wd_nat"], D, IFF, scr, kb.s["fb_wnat"],
                kb.s["fb_wq"], [])
    twq = kb.p.qtot("qt")
    tap_ffn = kb.tap("ffn0", (T, D))
    free_tok = {}
    pd_free = [None, None]
    for ti in range(NT):
        bi = ti % 2
        hqfm = kb.s[f"fb_hqfm{bi}"]
        fout = kb.s[f"fb_out{bi}"]
        ttr2 = []
        for kc in range(22):
            ttr2.append(kb.dmat(
                hqfm[:, kc * 128:(kc + 1) * 128],
                d["s_hq"][ti * 128:(ti + 1) * 128, kc * 128:(kc + 1) * 128],
                deps=[free_tok.get(("hqfm", bi))] + kb.tok["p3f"],
                q=f"qt{bi}"))
        ev_toks = []
        for half in range(2):
            tmm = None
            for kc in range(22):
                lhs = hqfm[:, kc * 128:(kc + 1) * 128]
                rd_ = kb.s["w_wd"][:, kc * 1024 + half * 512:
                                   kc * 1024 + half * 512 + 512]
                tmm = kb.te(lambda e, l_=lhs, r_=rd_, kc=kc, half=half: e.matmul(
                    pd[half], l_, r_, start=(kc == 0), stop=(kc == 21)),
                    deps=ttr2 + [twq, pd_free[half]] if kc == 0 else [])
            tev = kb.ve(lambda e, fout=fout, half=half: e.tensor_copy(
                fout[:, half * 512:(half + 1) * 512], pd[half]),
                deps=[tmm, free_tok.get(("fout", bi)),
                      free_tok.get(("fout_t", bi))])
            pd_free[half] = tev
            ev_toks.append(tev)
        free_tok[("hqfm", bi)] = kb.p.qtot("et")
        kb.dma("sync", d["s_ffn0"][ti * 128:(ti + 1) * 128, :], fout,
               deps=ev_toks, q=f"qsp{bi}")
        if tap_ffn is not None:
            kb.dma("sync", tap_ffn[ti * 128:(ti + 1) * 128, :], fout,
                   deps=ev_toks, q="qout")
        free_tok[("fout", bi)] = kb.p.qtot(f"qsp{bi}")
        free_tok[("fout_t", bi)] = (kb.p.qtot("qout") if tap_ffn is not None
                                    else None)
    kb.tok["p3fb"] = [kb.p.qtot("qsp0"), kb.p.qtot("qsp1")]






def emit_p3a(kb):
    s, d = kb.sb, kb.d
    s("a_encq", (128, 8 * 1024), BF16, scoped=True)
    s("a_encv", (128, 8 * 1024), BF16, scoped=True)
    s("a_dec", (128, 8 * 1024), BF16, scoped=True)
    wload_chunked(kb, "a_encq", "encq", 8, 1024)
    wload_chunked(kb, "a_encv", "encv", 8, 1024)
    wload_chunked(kb, "a_dec", "dec", 8, 1024)
    kb.p.base = kb.p.base + [kb.p.qtot("qw")]
    for i in range(2):
        s(f"a_x{i}", (128, D), F32, scoped=True)
        s(f"a_ff{i}", (128, D), F32, scoped=True)
        s(f"a_fm{i}", (128, 8 * 128), BF16, scoped=True, is_transpose_dest=True)
        s(f"a_q{i}", (128, 8 * 128), BF16, scoped=True)
        s(f"a_v{i}", (128, 8 * 128), BF16, scoped=True)
        s(f"a_fm1{i}", (128, 8 * 128), BF16, scoped=True, is_transpose_dest=True)
    s("a_vmid", (128, 8 * 128), BF16, scoped=True)
    s("a_xn", (128, D), F32, scoped=True)
    s("a_xnb", (128, D), BF16, scoped=True)
    s("a_xn1b", (128, D), BF16, scoped=True)
    s("a_mqn", (128, 8 * 192), BF16, scoped=True)
    s("a_mq2", (128, 8 * 128), BF16, scoped=True)
    s("a_nsq", (128, 8 * 128), BF16, scoped=True)
    s("a_aring", (1, 8 * 192), BF16, scoped=True)
    s("a_crow", (1, 8 * 128), BF16, scoped=True)
    s("a_scol", (128, 16), F32, scoped=True)
    s("a_nsqc", (128, 8), F32, scoped=True)
    s("a_sct", (16, 128), BF16, scoped=True)
    s("a_u", (128, 8 * 192), F32, scoped=True)
    s("a_s1", (128, 8 * 192), F32, scoped=True)
    s("a_s2", (128, 8 * 192), F32, scoped=True)
    s("a_m8", (128, 8), F32, scoped=True)
    s("a_eall", (128, 8 * 192), F32, scoped=True)
    s("a_sall", (128, NH), F32, scoped=True)
    s("a_suns", (128, NH), F32, scoped=True)
    s("a_srec", (128, NH), F32, scoped=True)
    s("a_wsel", (128, 8 * 128), BF16, scoped=True)
    s("a_wst", (128, 8 * 128), BF16, scoped=True, is_transpose_dest=True)
    s("a_os", (128, 8 * 128), BF16, scoped=True)
    s("a_xc", (128, D), F32, scoped=True)
    s("a_stt", (128, 8), F32, scoped=True)
    pD = [kb.psum(f"pD{i}", (128, 256), F32, scoped=True) for i in range(4)]
    pS = kb.psum("pS", (128, 16), F32, scoped=True)
    pS2 = kb.psum("pS2", (128, 128), F32, scoped=True)
    pdec = kb.psum("pdec", (128, 512), F32, scoped=True)
    pdec2 = kb.psum("pdec2", (128, 512), F32, scoped=True)
    x0 = d["x0"]
    tap_attn = kb.tap("attn0", (T, D))
    tap_x1 = kb.tap("x1", (T, D))
    tap_xn1 = kb.tap("xn1fm", (D, T), BF16)

    zs = []
    zs.append(kb.ve(lambda e: e.memset(kb.s["a_mqn"], 0.0)))
    zs.append(kb.ve(lambda e: e.memset(kb.s["a_aring"], 0.0)))
    zs.append(kb.ve(lambda e: e.memset(kb.s["a_v0"], 0.0)))
    zs.append(kb.ve(lambda e: e.memset(kb.s["a_v1"], 0.0)))
    zs.append(kb.ve(lambda e: e.memset(kb.s["a_u"], 0.0)))
    zq = zs[-1]

    def hd(ap, h, w=128):
        return ap[:, h * w:(h + 1) * w]

    def bch(ap, width, nrep=8, off=0):
        return bass.AP(ap.tensor, off, [[ap.tensor.shape[1], 128],
                                        [0, nrep], [1, width]])

    ft = {}
    mq_last = None
    ar_last = zq
    ctx_free = [None] * 8
    dec_free = [None, None]
    sc_free = None
    u_free = None
    w_free = None
    os_free = None
    for ti in range(NT):
        bi = ti % 2
        T0 = ti * 128
        first = (ti == 0)
        WD = 192 if first else 128
        m1c = kb.s["m_dm1t0" if first else "m_dm1"]
        m2c = kb.s["m_dm2t0" if first else "m_dm2"]
        xb, ffb = kb.s[f"a_x{bi}"], kb.s[f"a_ff{bi}"]
        fmb = kb.s[f"a_fm{bi}"]
        qf, vb = kb.s[f"a_q{bi}"], kb.s[f"a_v{bi}"]
        vprev = kb.s[f"a_v{1 - bi}"]
        fm1 = kb.s[f"a_fm1{bi}"]
        tldx = kb.dma("sync", xb, x0[T0:T0 + 128, :],
                      deps=[ft.get(("x", bi)), zq], q=f"qx{bi}")
        tldf = kb.dma("sync", ffb, d["s_ffn0"][T0:T0 + 128, :],
                      deps=[ft.get(("ff", bi)), zq] + kb.tok["p3fb"], q=f"qx{bi}")
        tldx = tldf = kb.p.qtot(f"qx{bi}")
        c_mean = kb.s["ln_mean"][:, ti:ti + 1]
        c_inv = kb.s["ln_inv"][:, ti:ti + 1]
        txn = kb.ve(lambda e, xb=xb, m=c_mean, iv=c_inv: e.scalar_tensor_tensor(
            kb.s["a_xn"], xb, m, iv.to_broadcast([128, D]),
            ALU.subtract, ALU.mult), deps=[tldx, ft.get(("xn",))])
        txc = kb.ve(lambda e: e.tensor_copy(kb.s["a_xnb"], kb.s["a_xn"]),
                    deps=[txn, ft.get(("xnb",))])
        ft[("xn",)] = txc
        ttr = []
        for kc in range(8):
            ttr.append(kb.dmat(fmb[:, kc * 128:(kc + 1) * 128],
                               kb.s["a_xnb"][:, kc * 128:(kc + 1) * 128],
                               deps=[txc, ft.get(("fm", bi))], q=f"qt{bi}"))
        ft[("xnb",)] = ttr[-1]
        tq_toks = []
        for h in range(NH):
            pb = pD[h // 2][:, (h % 2) * 128:(h % 2) * 128 + 128]
            tmm = None
            for kc in range(8):
                lhs = kb.s["a_encq"][:, kc * 1024 + h * 128: kc * 1024 + (h + 1) * 128]
                rhs = fmb[:, kc * 128:(kc + 1) * 128]
                tmm = kb.te(lambda e, pb=pb, l_=lhs, r_=rhs, kc=kc: e.matmul(
                    pb, l_, r_, start=(kc == 0), stop=(kc == 7)),
                    deps=ttr + [ctx_free[h], os_free] +
                    ([tq_toks[-1]] if h % 2 == 1 else []) if kc == 0 else [])
            t1 = kb.sc(lambda e, pb=pb, h=h, qf=qf: e.activation(
                hd(qf, h), pb, ACT.Relu), deps=[tmm, ft.get(("qf", bi))])
            tq_toks.append(t1)
        ft[("qf", bi)] = None
        tv_toks = []
        for half in range(2):
            pb = pdec if half == 0 else pdec2
            tmm = None
            for kc in range(8):
                lhs = fmb[:, kc * 128:(kc + 1) * 128]
                rhs = kb.s["a_encv"][:, kc * 1024 + half * 512:
                                     kc * 1024 + half * 512 + 512]
                tmm = kb.te(lambda e, pb=pb, l_=lhs, r_=rhs, kc=kc: e.matmul(
                    pb, l_, r_, start=(kc == 0), stop=(kc == 7)),
                    deps=ttr + [dec_free[half]] if kc == 0 else [])
            t1 = kb.sc(lambda e, pb=pb, half=half, vb=vb: e.activation(
                vb[:, half * 512:(half + 1) * 512], pb, ACT.Relu),
                deps=[tmm, ft.get(("vb", bi))])
            tv_toks.append(t1)
        ft[("vb", bi)] = None
        tvm1 = kb.dma("gpsimd", kb.s["a_vmid"][0:64, :], vprev[64:128, :],
                      deps=[ft.get(("vmid",)), zq, kb.p.qtot("qga")], q="qga")
        tvm2 = kb.dma("gpsimd", kb.s["a_vmid"][64:128, :], vb[0:64, :],
                      deps=tv_toks + [kb.p.qtot("qga")], q="qga")
        tvm = kb.p.qtot("qga")
        ttail = kb.ve(lambda e: e.tensor_copy(
            bass.AP(kb.s["a_mqn"].tensor, 0, [[8 * 192, 128], [192, 8], [1, 64]]),
            bass.AP(kb.s["a_mqn"].tensor, 128, [[8 * 192, 128], [192, 8], [1, 64]])),
            deps=[mq_last, zq])
        tmq = []
        for h in range(NH):
            t1 = kb.ve(lambda e, h=h, qf=qf: e.tensor_scalar(
                kb.s["a_mqn"][:, h * 192 + 64: h * 192 + 192], hd(qf, h),
                kb.s["metcol2"][:, h:h + 1], None, ALU.mult),
                deps=[tq_toks[h], ttail] + kb.tok["p2"], hazard=True)
            tmq.append(t1)
        mqcur = bass.AP(kb.s["a_mqn"].tensor, 64, [[8 * 192, 128], [192, 8], [1, 128]])
        qf8 = bass.AP(qf.tensor, 0, [[8 * 128, 128], [128, 8], [1, 128]])
        tmq2 = kb.ve(lambda e, qf8=qf8, mqcur=mqcur: e.tensor_tensor(
            kb.s["a_mq2"].rearrange("p (h c) -> p h c", c=128), qf8, mqcur,
            ALU.mult), deps=tmq + [ft.get(("mq2",))])
        tnsq = kb.ve(lambda e, qf=qf: e.tensor_tensor(kb.s["a_nsq"], qf, qf, ALU.mult),
                     deps=tq_toks + [ft.get(("nsq",))])
        tac = []
        tprev = sc_free
        for h in range(NH):
            t1 = kb.te(lambda e, h=h: e.matmul(
                pS[:, h:h + 1], hd(kb.s["a_mq2"], h), kb.s["c_ones_col"],
                start=True, stop=True),
                deps=[tmq2, tprev] if h == 0 else [tmq2])
            t2 = kb.te(lambda e, h=h: e.matmul(
                pS[:, 8 + h:9 + h], hd(kb.s["a_nsq"], h), kb.s["c_ones_col"],
                start=True, stop=True), deps=[tnsq] if h == 0 else [])
            tac += [t1, t2]
        tsc1 = kb.ve(lambda e: e.tensor_scalar(
            kb.s["a_scol"][:, 0:8], pS[:, 0:8], -0.5, None, ALU.mult),
            deps=tac + [sc_free])
        tsc2 = kb.ve(lambda e: e.tensor_copy(kb.s["a_nsqc"], pS[:, 8:16]),
                     deps=tac + [sc_free])
        sc_free = tsc2
        tc1 = kb.sc(lambda e: e.activation(kb.s["a_stt"], kb.s["a_nsqc"],
                                           ACT.Ln, bias=1e-24),
                    deps=[tsc2, ft.get(("stt",))])
        tc2 = kb.sc(lambda e: e.activation(kb.s["a_stt"], kb.s["a_stt"],
                                           ACT.Exp, scale=0.5),
                    deps=[tc1], hazard=True)
        tc3 = kb.ve(lambda e: e.tensor_scalar(kb.s["a_stt"], kb.s["a_stt"],
                                              1e-12, None, ALU.max), deps=[tc2])
        tc4 = kb.ve(lambda e: e.reciprocal(kb.s["a_stt"], kb.s["a_stt"]),
                    deps=[tc3], hazard=True)
        gate_c = float(np.float32(0.2) * np.float32(0.2))
        tc5 = kb.ve(lambda e: e.tensor_scalar(
            kb.s["a_scol"][:, 8:16], kb.s["a_nsqc"], gate_c, 0.1,
            ALU.is_gt, ALU.mult), deps=[tsc2, tsc1])
        tc6 = kb.ve(lambda e: e.tensor_tensor(
            kb.s["a_scol"][:, 8:16], kb.s["a_scol"][:, 8:16], kb.s["a_stt"],
            ALU.mult), deps=[tc4, tc5], hazard=True)
        ft[("stt",)] = tc6
        tT = kb.te(lambda e: e.transpose(pS2[0:16, 0:128], kb.s["a_scol"],
                                         kb.s["c_identf"]),
                   deps=[tsc1, tc6, ft.get(("pS2",))])
        tev = kb.ve(lambda e: e.tensor_copy(kb.s["a_sct"], pS2[0:16, 0:128]),
                    deps=[tT, ft.get(("sct",))])
        ft[("pS2",)] = tev
        tat = kb.ve(lambda e: e.tensor_copy(
            bass.AP(kb.s["a_aring"].tensor, 0, [[8 * 192, 1], [192, 8], [1, 64]]),
            bass.AP(kb.s["a_aring"].tensor, 128, [[8 * 192, 1], [192, 8], [1, 64]])),
            deps=[ar_last])
        tda = kb.dma("gpsimd", bass.AP(kb.s["a_aring"].tensor, 64,
                                       [[8 * 192, 1], [192, 8], [1, 128]]),
                     kb.s["a_sct"][0:8, :], deps=[tev, tat, kb.p.qtot("qga")],
                     q="qga")
        tdc = kb.dma("gpsimd", kb.s["a_crow"].rearrange("p (h c) -> p h c", c=128),
                     kb.s["a_sct"][8:16, :],
                     deps=[tev, ft.get(("crow",)), kb.p.qtot("qga")], q="qga")
        tdq = kb.p.qtot("qga")
        ft[("sct",)] = tdq
        ar_last = tdq
        td_toks = []
        for h in range(NH):
            pb = pD[h // 2][:, (h % 2) * 128:(h % 2) * 128 + 128]
            mq_w = kb.s["a_mqn"][:, h * 192: h * 192 + 192]
            ar_w = kb.s["a_aring"][0:1, h * 192: h * 192 + 192]
            deps0 = [tmq[h], tdq, tq_toks[h | 1], u_free, ctx_free[h]]
            t1 = kb.te(lambda e, pb=pb, h=h, mq_w=mq_w, qf=qf: e.matmul(
                pb[0:64, :], hd(qf, h)[:, 0:64], mq_w[:, 0:128],
                start=True, stop=False), deps=deps0)
            kb.te(lambda e, pb=pb, h=h, ar_w=ar_w: e.matmul(
                pb[0:64, :], ar_w[:, 64:128], kb.s["c_ones_row"],
                start=False, stop=False), deps=[])
            kb.te(lambda e, pb=pb, h=h, ar_w=ar_w: e.matmul(
                pb[0:64, :], kb.s["c_ones_row"][:, 0:64], ar_w[:, 0:128],
                start=False, stop=True), deps=[])
            kb.te(lambda e, pb=pb, h=h, mq_w=mq_w, qf=qf: e.matmul(
                pb[64:128, :], hd(qf, h)[:, 64:128], mq_w[:, 64:192],
                start=True, stop=False, tile_position=(0, 64)), deps=[])
            kb.te(lambda e, pb=pb, h=h, ar_w=ar_w: e.matmul(
                pb[64:128, :], ar_w[:, 128:192], kb.s["c_ones_row"],
                start=False, stop=False, tile_position=(0, 64)), deps=[])
            t6 = kb.te(lambda e, pb=pb, h=h, ar_w=ar_w: e.matmul(
                pb[64:128, :], kb.s["c_ones_row"][:, 0:64], ar_w[:, 64:192],
                start=False, stop=True, tile_position=(0, 64)), deps=[])
            td_toks.append(t6)
        tu_toks = []
        for b_ in range(4):
            srcv = pD[b_].rearrange("p (h c) -> p h c", c=128)
            dstv = bass.AP(kb.s["a_u"].tensor, (2 * b_) * WD,
                           [[8 * 192, 128], [WD, 2], [1, 128]])
            t1 = kb.ve(lambda e, b_=b_, srcv=srcv, dstv=dstv, m1c=m1c: e.tensor_tensor(
                dstv, srcv, bch(m1c, 128, 2), ALU.mult),
                deps=td_toks[2 * b_:2 * b_ + 2] + [u_free], hazard=True)
            tu_toks.append(t1)
        UW = 8 * WD
        tln = kb.sc(lambda e, UW=UW: e.activation(kb.s["a_u"][:, 0:UW],
                                           kb.s["a_u"][:, 0:UW], ACT.Ln,
                                           bias=1e-8), deps=tu_toks)
        texp = kb.sc(lambda e, UW=UW: e.activation(kb.s["a_u"][:, 0:UW],
                                            kb.s["a_u"][:, 0:UW], ACT.Exp,
                                            scale=0.5), deps=[tln], hazard=True)
        sv = bass.AP(kb.s["a_s1"].tensor, 0, [[8 * 192, 128], [WD, 8], [1, 128]])
        uv = bass.AP(kb.s["a_u"].tensor, 0, [[8 * 192, 128], [WD, 8], [1, 128]])
        ts1 = kb.ve(lambda e, sv=sv, uv=uv, m2c=m2c: e.tensor_tensor(sv, bch(m2c, 128), uv,
                                              ALU.subtract),
                    deps=[texp, ft.get(("s1",))])
        sdeps = [ts1]
        if first:
            for h in range(NH):
                d0 = kb.s["a_u"][:, h * WD + 64: h * WD + 65]
                t1 = kb.ve(lambda e, h=h, d0=d0, WD=WD: e.tensor_tensor(
                    kb.s["a_s1"][:, h * WD + 128: h * WD + 192],
                    kb.s["m_ddup"], d0.to_broadcast([128, 64]), ALU.subtract),
                    deps=[texp, ts1])
                sdeps.append(t1)
        te_toks = []
        for h in range(NH):
            t1 = kb.sc(lambda e, h=h, WD=WD: e.activation(
                kb.s["a_eall"][:, h * WD:(h + 1) * WD],
                kb.s["a_s1"][:, h * WD:(h + 1) * WD], ACT.Exp,
                accum_out=kb.s["a_sall"][:, h:h + 1]),
                deps=sdeps + [ft.get(("eall",))])
            te_toks.append(t1)
        tmr_last = []
        for h in range(NH):
            cur, nxt = kb.s["a_s1"], kb.s["a_s2"]
            tprev = te_toks[h]
            for rnd in range(4):
                tm8 = kb.ve(lambda e, h=h, cur=cur, WD=WD: e.max(
                    out=kb.s["a_m8"], in_=cur[:, h * WD:(h + 1) * WD]),
                    deps=[tprev, ft.get(("m8",))], hazard=True)
                tmr = kb.ve(lambda e, h=h, cur=cur, nxt=nxt, WD=WD: e.match_replace(
                    out=nxt[:, h * WD:(h + 1) * WD], in_to_replace=kb.s["a_m8"],
                    in_values=cur[:, h * WD:(h + 1) * WD], imm_value=SENT),
                    deps=[tm8, ft.get(("s2", h)) if rnd == 0 else None],
                    hazard=True)
                ft[("m8",)] = tmr
                tprev = tmr
                cur, nxt = nxt, cur
            tmr_last.append(tprev)
            ft[("s2", h)] = tprev
        tw_toks = []
        for h in range(NH):
            t1 = kb.sc(lambda e, h=h, WD=WD: e.activation(
                kb.s["a_s2"][:, h * WD:(h + 1) * WD],
                kb.s["a_s1"][:, h * WD:(h + 1) * WD], ACT.Exp,
                accum_out=kb.s["a_suns"][:, h:h + 1]),
                deps=[tmr_last[h], ft.get(("s2e", h))], hazard=True)
            tw_toks.append(t1)
        tS = kb.ve(lambda e: e.tensor_tensor(kb.s["a_srec"], kb.s["a_sall"],
                                             kb.s["a_suns"], ALU.subtract),
                   deps=tw_toks + te_toks + [ft.get(("srec",))])
        tS = kb.ve(lambda e: e.tensor_scalar(kb.s["a_srec"], kb.s["a_srec"],
                                             1e-8, None, ALU.add),
                   deps=[tS], hazard=True)
        tS = kb.ve(lambda e: e.reciprocal(kb.s["a_srec"], kb.s["a_srec"]),
                   deps=[tS], hazard=True)
        ft[("srec",)] = tS
        twn = []
        for h in range(NH):
            tww = kb.ve(lambda e, h=h, WD=WD: e.tensor_tensor(
                kb.s["a_eall"][:, h * WD:(h + 1) * WD],
                kb.s["a_eall"][:, h * WD:(h + 1) * WD],
                kb.s["a_s2"][:, h * WD:(h + 1) * WD], ALU.subtract),
                deps=[tw_toks[h], tS], hazard=True)
            if first:
                tdup = kb.ve(lambda e, h=h, WD=WD: e.reduce_sum(
                    kb.s["a_stt"][:, 0:1],
                    kb.s["a_eall"][:, h * WD + 128: h * WD + 192], axis=AXX),
                    deps=[tww, ft.get(("stt",))], hazard=True)
                tww = kb.ve(lambda e, h=h, WD=WD: e.tensor_tensor(
                    kb.s["a_eall"][:, h * WD + 64: h * WD + 65],
                    kb.s["a_eall"][:, h * WD + 64: h * WD + 65],
                    kb.s["a_stt"][:, 0:1], ALU.add), deps=[tdup], hazard=True)
                ft[("stt",)] = tww
            t2 = kb.ve(lambda e, h=h, WD=WD: e.tensor_scalar(
                hd(kb.s["a_wsel"], h), kb.s["a_eall"][:, h * WD: h * WD + 128],
                kb.s["a_srec"][:, h:h + 1], None, ALU.mult),
                deps=[tww, tS, ft.get(("wsel", h))], hazard=True)
            twn.append(t2)
        ft[("eall",)] = twn[-1]
        ft[("s1",)] = twn[-1]
        u_free = twn[-1]
        twt = []
        for h in range(NH):
            twt.append(kb.dmat(hd(kb.s["a_wst"], h), hd(kb.s["a_wsel"], h),
                               deps=[twn[h], ft.get(("wst", h))], q=f"qt{bi}"))
        tw_all = kb.p.qtot(f"qt{bi}")
        for h in range(NH):
            twt[h] = tw_all
            ft[("wsel", h)] = tw_all
        tos = []
        for h in range(NH):
            pb = pD[h // 2][:, (h % 2) * 128:(h % 2) * 128 + 128]
            tr1 = kb.te(lambda e, h=h: e.matmul(
                pS2[:, 0:128], kb.s["c_ones_row"],
                kb.s["a_crow"][0:1, h * 128:(h + 1) * 128],
                start=True, stop=True), deps=[tdq, ft.get(("pS2",))])
            trh = kb.ve(lambda e, h=h, qf=qf: e.tensor_tensor(
                hd(kb.s["a_os"], h), hd(qf, h), pS2[:, 0:128], ALU.mult),
                deps=[tr1, tq_toks[h], os_free], hazard=True)
            ft[("pS2",)] = trh
            th1 = kb.te(lambda e, h=h, pb=pb: e.matmul(
                pb, hd(kb.s["w_hm"], h), hd(kb.s["a_os"], h),
                start=True, stop=False),
                deps=[trh, tu_toks[h // 2]] +
                ([tos[-1]] if h % 2 == 1 else []))
            ta_ = kb.te(lambda e, h=h, pb=pb: e.matmul(
                pb[:, 0:64], hd(kb.s["a_vmid"], h), hd(kb.s["a_wst"], h)[:, 0:64],
                start=False, stop=False), deps=[twt[h], tvm])
            tb_ = kb.te(lambda e, h=h, vb=vb, pb=pb: e.matmul(
                pb[:, 64:128], hd(vb, h), hd(kb.s["a_wst"], h)[:, 64:128],
                start=False, stop=True), deps=[tv_toks[-1]])
            t2 = kb.ve(lambda e, h=h, pb=pb, qf=qf: e.tensor_tensor(
                hd(kb.s["a_os"], h), hd(qf, h), pb, ALU.mult),
                deps=[ta_, tb_, trh], hazard=True)
            tos.append(t2)
            ctx_free[h] = t2
            ft[("wst", h)] = t2
        os_free = tos[-1]
        tcmb = []
        for half in range(2):
            pb = pdec if half == 0 else pdec2
            tmm = None
            for h in range(NH):
                lhs = hd(kb.s["a_os"], h)
                rhs = kb.s["a_dec"][:, h * 1024 + half * 512: h * 1024 + half * 512 + 512]
                tmm = kb.te(lambda e, pb=pb, l_=lhs, r_=rhs, h=h: e.matmul(
                    pb, l_, r_, start=(h == 0), stop=(h == 7)),
                    deps=tos + [dec_free[half]] if h == 0 else [])
            t1 = kb.ve(lambda e, pb=pb, half=half, xb=xb: e.scalar_tensor_tensor(
                kb.s["a_xc"][:, half * 512:(half + 1) * 512], pb, 1.0,
                xb[:, half * 512:(half + 1) * 512], ALU.mult, ALU.add),
                deps=[tmm, tldx, ft.get(("x1",)), ft.get(("xc",)),
                      ft.get(("x1so",))],
                hazard=True)
            t2 = kb.ve(lambda e, half=half, ffb=ffb: e.tensor_tensor(
                kb.s["a_xc"][:, half * 512:(half + 1) * 512],
                kb.s["a_xc"][:, half * 512:(half + 1) * 512],
                ffb[:, half * 512:(half + 1) * 512], ALU.add),
                deps=[t1, tldf], hazard=True)
            dec_free[half] = t2
            tcmb.append(t2)
        ft[("x", bi)] = tcmb[-1]
        ft[("ff", bi)] = tcmb[-1]
        tsx = kb.dma("sync", d["s_x1"][VHALO + T0: VHALO + T0 + 128, :],
                     kb.s["a_xc"], deps=tcmb, q=f"qsp{bi}")
        if tap_attn is not None:
            kb.dma("sync", tap_attn[T0:T0 + 128, :], kb.s["a_xc"],
                   deps=tcmb, q="qout")
        if tap_x1 is not None:
            kb.dma("sync", tap_x1[T0:T0 + 128, :], kb.s["a_xc"],
                   deps=tcmb, q="qout")
        if tap_x1 is not None or tap_attn is not None:
            ft[("x1so",)] = kb.p.qtot("qout")
        cm1 = kb.s["ln1_mean"][:, ti:ti + 1]
        ci1 = kb.s["ln1_inv"][:, ti:ti + 1]
        t1 = kb.ve(lambda e, cm1=cm1: e.reduce_sum(cm1, kb.s["a_xc"],
                                                   axis=AXX),
                   deps=tcmb + [ft.get(("stt",))], hazard=True)
        tsq = kb.sc(lambda e: e.square(kb.s["a_xn"], kb.s["a_xc"]),
                    deps=tcmb + [ttr[-1], ft.get(("xn",))])
        t2 = kb.ve(lambda e, ci1=ci1: e.reduce_sum(ci1, kb.s["a_xn"],
                                                   axis=AXX), deps=[tsq])
        ft[("xn",)] = t2
        tm_ = kb.ve(lambda e, cm1=cm1: e.tensor_scalar(
            cm1, cm1, 1.0 / 1024.0, None,
            ALU.mult), deps=[t1], hazard=True)
        tm2 = kb.ve(lambda e, cm1=cm1: e.tensor_tensor(
            kb.s["a_stt"][:, 3:4], cm1, cm1,
            ALU.mult), deps=[tm_, ft.get(("stt",))], hazard=True)
        tvv = kb.ve(lambda e, ci1=ci1: e.scalar_tensor_tensor(
            ci1, ci1, 1.0 / 1024.0,
            kb.s["a_stt"][:, 3:4], ALU.mult, ALU.subtract),
            deps=[t2, tm2], hazard=True)
        tl_ = kb.sc(lambda e, ci1=ci1: e.activation(ci1, ci1, ACT.Ln,
                                           bias=1e-5), deps=[tvv], hazard=True)
        tiv = kb.sc(lambda e, ci1=ci1: e.activation(ci1, ci1, ACT.Exp,
                                           scale=-0.5), deps=[tl_], hazard=True)
        txn1 = kb.ve(lambda e, cm1=cm1, ci1=ci1: e.scalar_tensor_tensor(
            kb.s["a_xn1b"], kb.s["a_xc"], cm1,
            ci1.to_broadcast([128, D]),
            ALU.subtract, ALU.mult), deps=[tiv, tm_, ft.get(("xn1b",))],
            hazard=True)
        ft[("stt",)] = txn1
        ft[("x1",)] = txn1
        ttr1 = []
        for kc in range(8):
            ttr1.append(kb.dmat(fm1[:, kc * 128:(kc + 1) * 128],
                                kb.s["a_xn1b"][:, kc * 128:(kc + 1) * 128],
                                deps=[txn1, ft.get(("fm1", bi)),
                                      ft.get(("fm1o",))], q=f"qt{bi}"))
        ft[("xn1b",)] = ttr1[-1]
        dst = bass.AP(d["s_xn1fm"].tensor, 1 + VHALO + T0,
                      [[XNW, 128], [128 * XNW, 8], [1, 128]])
        tso = kb.dma("sync", dst, fm1, deps=ttr1, q=f"qsp{bi}")
        if tap_xn1 is not None:
            dst2 = bass.AP(kb.tap_ap_xn1.tensor if False else tap_xn1.tensor, T0,
                           [[T, 128], [128 * T, 8], [1, 128]])
            kb.dma("sync", dst2, fm1, deps=ttr1, q="qout")
            ft[("fm1o",)] = kb.p.qtot("qout")
        ft[("fm1", bi)] = kb.p.qtot(f"qsp{bi}")
        ft[("xc",)] = ft[("fm1", bi)]
        ft[("fm", bi)] = kb.p.qtot("et")
        ft[("mq2",)] = kb.p.qtot("et")
        ft[("nsq",)] = kb.p.qtot("et")
        ft[("crow",)] = kb.p.qtot("et")
        ft[("vmid",)] = kb.p.qtot("et")
        mq_last = td_toks[-1]
    kb.tok["p3a"] = [kb.p.qtot("qsp0"), kb.p.qtot("qsp1")]





def _kv_groups(l):
    lo = 0 if l == 0 else QLO0
    gs = []
    v = lo
    while v < VRNG:
        w = min(512, VRNG - v)
        gs.append((v, w))
        v += w
    return gs


def emit_p4pre(kb):
    s, d = kb.sb, kb.d
    s("z_fill", (128, 1153), BF16, scoped=True)
    s("bl_a", (128, 3201), BF16, scoped=True)
    s("bl_b", (128, 3201), BF16, scoped=True)
    s("bl_t", (128, 3201), BF16, scoped=True)
    tz = kb.ge(lambda e: e.memset(kb.s["z_fill"], 0.0))
    zt = []
    for c in range(8):
        zt.append(kb.dma("sync", bass.AP(d["s_xn1fm"].tensor, c * 128 * XNW,
                                         [[XNW, 128], [1, 1153]]),
                         kb.s["z_fill"], deps=[tz], q="qsp0"))
        zt.append(kb.dma("sync", bass.AP(d["s_mid"].tensor, c * 128 * VRNG,
                                         [[VRNG, 128], [1, 512]]),
                         kb.s["z_fill"][:, 0:512], deps=[tz], q="qsp0"))
        zt.append(kb.dma("sync", bass.AP(d["s_out"].tensor, c * 128 * VRNG,
                                         [[VRNG, 128], [1, 512]]),
                         kb.s["z_fill"][:, 0:512], deps=[tz], q="qsp0"))
    tzt = kb.p.qtot("qsp0")
    prev = [tzt]
    for c in range(8):
        ta = kb.dma("sync", kb.s["bl_a"],
                    bass.AP(d["s_xn1fm"].tensor, c * 128 * XNW,
                            [[XNW, 128], [1, 3201]]),
                    deps=prev + [tzt], q="qx0")
        tb = kb.dma("sync", kb.s["bl_b"],
                    bass.AP(d["s_xn1fm"].tensor, c * 128 * XNW + 2048,
                            [[XNW, 128], [1, 3201]]),
                    deps=prev, q="qx0")
        tl = kb.p.qtot("qx0")
        t1 = kb.ve(lambda e: e.tensor_tensor(kb.s["bl_t"], kb.s["bl_b"],
                                             kb.s["bl_a"], ALU.subtract),
                   deps=[tl] + prev, hazard=True)
        t2 = kb.ve(lambda e: e.scalar_tensor_tensor(
            kb.s["bl_t"], kb.s["bl_t"], kb.s["c_msel"], kb.s["bl_a"],
            ALU.mult, ALU.add), deps=[t1], hazard=True)
        ts = kb.dma("sync", bass.AP(d["s_xv"].tensor, c * 128 * (1 + VRNG),
                                    [[1 + VRNG, 128], [1, 3201]]),
                    kb.s["bl_t"], deps=[t2], q="qsp0")
        prev = [ts, kb.p.qtot("qsp0")]
    kb.tok["p4pre"] = [kb.p.qtot("qsp0")]




def emit_colfix(kb, dname, tok_deps=()):
    d = kb.d
    kb.sb("cfx", (128, 8), BF16, scoped=True)
    cfx = kb.s["cfx"]
    src = bass.AP(d[dname].tensor, VHALO - 1,
                  [[VRNG, 128], [128 * VRNG, 8], [1, 1]])
    tl = kb.p.emit("sync", lambda e, o=cfx, i=src: e.dma_start(
        o, i, allow_slow_non_contiguous=True),
        deps=list(tok_deps), dma="qx0")
    tlt = kb.p.qtot("qx0")
    t1 = kb.ve(lambda e, cfx=cfx: e.tensor_scalar(
        cfx, cfx, kb.s["c_msel"], None, ALU.mult),
        deps=[tlt], hazard=True)
    ts = kb.p.emit("sync", lambda e, o=src, i=cfx: e.dma_start(
        o, i, allow_slow_non_contiguous=True),
        deps=[t1], dma="qsp0")
    return kb.p.qtot("qsp0")

def emit_p4ab(kb, l, taps=()):
    s, d = kb.sb, kb.d
    ve, sc, te = kb.ve, kb.sc, kb.te
    s("w4q", (128, 8 * 1536), BF16, scoped=True)
    s("w4p", (128, 8 * 1024), BF16, scoped=True)
    s("w4o", (128, 4 * 1024), BF16, scoped=True)
    tw = []
    tw.append(kb.dma("gpsimd", bass.AP(kb.s["w4q"].tensor, 0,
                                       [[8 * 1536, 128], [1536, 8], [1, 1536]]),
                     bass.AP(d["fqkv"].tensor, l * D * 1536,
                             [[1536, 128], [128 * 1536, 8], [1, 1536]]),
                     q="qw"))
    tw.append(kb.dma("gpsimd", bass.AP(kb.s["w4p"].tensor, 0,
                                       [[8 * 1024, 128], [1024, 8], [1, 1024]]),
                     bass.AP(d["fqkvp"].tensor, l * D * 1024,
                             [[1024, 128], [128 * 1024, 8], [1, 1024]]),
                     q="qw"))
    tw.append(kb.dma("gpsimd", bass.AP(kb.s["w4o"].tensor, 0,
                                       [[4 * 1024, 128], [1024, 4], [1, 1024]]),
                     bass.AP(d["fwo"].tensor, l * INNER * D,
                             [[1024, 128], [128 * 1024, 4], [1, 1024]]),
                     q="qw"))
    twt = kb.p.qtot("qw")
    s("c_bones", (128, 32), F32, scoped=True)
    s("c_hsel", (8, 4 * 128), F32, scoped=True)
    s("c_ones_rowf", (1, 512), F32, scoped=True)
    tw.append(kb.dma("sync", kb.s["c_bones"], d["bones"], q="q0"))
    tw.append(kb.dma("sync", kb.s["c_hsel"], d["hsel"], q="q0"))
    _corf = kb.s["c_ones_rowf"]
    tonesf = kb.ve(lambda e, _corf=_corf: e.memset(_corf, 1.0))
    s("sc_qs", (128, 1), F32, scoped=True)
    s("sc_qsp", (128, 1), F32, scoped=True)
    s("sc_ks", (128, 1), F32, scoped=True)
    s("sc_ksp", (128, 1), F32, scoped=True)
    tsc0 = []
    for nm, dn in [("sc_qs", "fqscol"), ("sc_qsp", "fqspcol"),
                   ("sc_ks", "fkscol"), ("sc_ksp", "fkspcol")]:
        tsc0.append(kb.dma("sync", kb.s[nm],
                           bass.AP(d[dn].tensor, l * 128,
                                   [[1, 128], [1, 1]]), q="q0"))
    tsct = kb.p.qtot("q0")
    s("q_fm", (128, 4 * VRNG), BF16, scoped=True)
    s("k_fm", (128, 4 * VRNG), BF16, scoped=True)
    s("v_tm", (128, 25 * 520), BF16, scoped=True)
    _vone = bass.AP(kb.s["v_tm"].tensor, 64, [[25 * 520, 128], [65, 200]])
    tone = ve(lambda e, _vone=_vone: e.memset(_vone, 1.0))
    s("hg0", (128, 8 * 512), BF16, scoped=True)
    s("tb0", (128, 4 * 512), F32, scoped=True)
    s("qa_g", (128, 4 * 512), BF16, scoped=True)
    s("ka_g", (128, 4 * 512), BF16, scoped=True)
    s("qb_c", (128, 512), BF16, scoped=True)
    s("sqb", (128, 512), F32, scoped=True)
    s("rn_q", (8, 512), F32, scoped=True)
    s("rn_k", (8, 512), F32, scoped=True)
    s("cu1", (128, 512), F32, scoped=True)
    kb.s["cu2"] = kb.s["sqb"]
    pb = [kb.psum(f"p4b{i}", (128, 512), F32, scoped=True) for i in range(4)]
    pssq = kb.psum("p4ssq", (8, 512), F32, scoped=True)
    pssk = kb.psum("p4ssk", (8, 512), F32, scoped=True)
    pav = kb.psum("p4av", (65, 512), F32, scoped=True)
    pR = kb.psum("p4R", (128, 512), F32, scoped=True)
    s("mb0", (128, 8 * 512), BF16, scoped=True)
    s("xg0", (128, 8 * 512), BF16, scoped=True)
    s("wt_b", (128, 2 * 512), BF16, scoped=True)
    s("wm_b", (128, 2 * 512), BF16, scoped=True)
    s("og", (128, 4 * 512), BF16, scoped=True)
    s("onesf_p", (128, 64), F32, scoped=True)
    s("ot_c", (64, 512), BF16, scoped=True)
    s("rr_c", (64, 512), F32, scoped=True)
    s("midb", (128, 512), F32, scoped=True)
    tonesp = ve(lambda e, _o=kb.s["onesf_p"]: e.memset(_o, 1.0))
    xsrc = d["s_xv"] if l == 0 else d["s_out"]
    xw = (1 + VRNG) if l == 0 else VRNG
    xoff = 1 if l == 0 else 0
    tfix = emit_colfix(kb, "s_out") if l == 1 else None

    snap = dict(kb.s)
    S = snap.__getitem__
    ft = {}
    if tfix is not None:
        ft[("hg",)] = tfix
        ft[("xg",)] = tfix

    def xread(col0, width, shift):
        out = []
        for c in range(8):
            cc = col0 + xoff - (1 if (shift and c >= 4) else 0)
            out.append(bass.AP(xsrc.tensor, c * 128 * xw + cc,
                               [[xw, 128], [1, width]]))
        return out

    kvg = _kv_groups(l)
    for g, (V0, W) in enumerate(kvg):
        p = g % 2
        hg, tb = S("hg0"), S("tb0")
        tld = []
        srcs = xread(V0, W, True)
        for c in range(8):
            tld.append(kb.dma("sync", hg[:, c * 512:c * 512 + W], srcs[c],
                              deps=[ft.get(("hg",)), ft.get(("hgv",))]
                              + kb.tok["p4pre"], q=f"qx{p}"))
        for i, tn in enumerate(["rt1q", "rt2q", "rt1k", "rt2k"]):
            tld.append(kb.dma("sync", tb[:, i * 512:i * 512 + W],
                              d[tn][:, V0:V0 + W],
                              deps=[ft.get(("tb",))], q=f"qx{p}"))
        tldt = kb.p.qtot(f"qx{p}")
        tts = []
        for i, scn in enumerate(["sc_qs", "sc_qsp", "sc_ks", "sc_ksp"]):
            tts.append(ve(lambda e, i=i, scn=scn, tb=tb, W=W: e.tensor_scalar(
                tb[:, i * 512:i * 512 + W], tb[:, i * 512:i * 512 + W],
                S(scn), None, ALU.mult),
                deps=[tldt, tsct], hazard=True))
        for tt in range(W // 128):
            slot = V0 // 128 + tt
            tmm = None
            for c in range(8):
                tmm = te(lambda e, c=c, tt=tt, hg=hg: e.matmul(
                    pb[3], hg[:, c * 512 + tt * 128: c * 512 + tt * 128 + 128],
                    S("w4q")[:, c * 1536 + 1024: (c + 1) * 1536],
                    start=(c == 0), stop=(c == 7)),
                    deps=[tldt, twt, ft.get(("pb3",))] if c == 0 else [])
            t1 = sc(lambda e, slot=slot: e.activation(
                bass.AP(S("v_tm").tensor, slot * 520,
                        [[25 * 520, 128], [65, 8], [1, 64]]),
                pb[3].rearrange("p (h e) -> p h e", e=64), ACT.Copy),
                deps=[tmm, tone])
            ft[("pb3",)] = t1
        tev = {}
        for oc in range(4):
            for j, (nm, wsel, sq) in enumerate([
                    ("qa_g", ("w4q", 0), "q"),
                    ("ka_g", ("w4q", 512), "k")]):
                wname, wbase = wsel
                wwid = 1536 if wname == "w4q" else 1024
                pbx = pb[j % 3]
                tmm = None
                for c in range(8):
                    lhs = kb.s[wname][:, c * wwid + wbase + oc * 128:
                                      c * wwid + wbase + oc * 128 + 128]
                    tmm = te(lambda e, pbx=pbx, l_=lhs, c=c, hg=hg, W=W: e.matmul(
                        pbx[:, 0:W], l_, hg[:, c * 512:c * 512 + W],
                        start=(c == 0), stop=(c == 7)),
                        deps=[tldt, twt, ft.get(("pb", j % 3))]
                        if c == 0 else [])
                if sq is not None:
                    t1 = sc(lambda e, pbx=pbx, W=W: e.square(
                        S("sqb")[:, 0:W], pbx[:, 0:W]),
                        deps=[tmm, ft.get(("sqb",))])
                    pss = pssq if sq == "q" else pssk
                    t2 = te(lambda e, pss=pss, oc=oc, W=W: e.matmul(
                        pss[:, 0:W], S("c_bones")[:, 8 * oc:8 * oc + 8],
                        S("sqb")[:, 0:W],
                        start=(oc == 0), stop=(oc == 3)),
                        deps=[t1, tsct] +
                        ([ft.get(("pss", sq))] if oc == 0 else []))
                    ft[("sqb",)] = t2
                    ft[("pss", sq + "c")] = t2
                t3 = sc(lambda e, nm=nm, oc=oc, pbx=pbx, W=W: e.activation(
                    S(nm)[:, oc * 512:oc * 512 + W], pbx[:, 0:W], ACT.Copy),
                    deps=[tmm, ft.get((nm,))])
                tev[(nm, oc)] = t3
                ft[("pb", j % 3)] = t3
        rnt = {}
        for sq, pss, rb in [("q", pssq, "rn_q"), ("k", pssk, "rn_k")]:
            r1 = sc(lambda e, pss=pss, rb=rb, W=W: e.activation(
                S(rb)[:, 0:W], pss[:, 0:W], ACT.Ln, bias=1e-24),
                deps=[ft.get(("pss", sq + "c")), ft.get(("rn", sq))])
            r2 = sc(lambda e, rb=rb, W=W: e.activation(
                S(rb)[:, 0:W], S(rb)[:, 0:W], ACT.Exp, scale=0.5),
                deps=[r1], hazard=True)
            r3 = ve(lambda e, rb=rb, W=W: e.tensor_scalar(
                S(rb)[:, 0:W], S(rb)[:, 0:W], 1e-12, None, ALU.max),
                deps=[r2], hazard=True)
            r4 = ve(lambda e, rb=rb, W=W: e.reciprocal(
                S(rb)[:, 0:W], S(rb)[:, 0:W]), deps=[r3], hazard=True)
            rnt[sq] = r4
            ft[("pss", sq)] = r4
        for qk, anm, bbase, rb, ti1, ti2, dst in [
                ("q", "qa_g", 0, "rn_q", 0, 1, "q_fm"),
                ("k", "ka_g", 512, "rn_k", 2, 3, "k_fm")]:
            for oc in range(4):
                pbx = pb[oc % 2]
                tmb = None
                for c in range(8):
                    lhs = S("w4p")[:, c * 1024 + bbase + oc * 128:
                                      c * 1024 + bbase + oc * 128 + 128]
                    tmb = te(lambda e, pbx=pbx, l_=lhs, c=c, hg=hg, W=W:
                             e.matmul(
                                 pbx[:, 0:W], l_, hg[:, c * 512:c * 512 + W],
                                 start=(c == 0), stop=(c == 7)),
                             deps=[ft.get(("pb", oc % 2))] if c == 0 else [])
                tevb = sc(lambda e, pbx=pbx, W=W: e.activation(
                    S("qb_c")[:, 0:W], pbx[:, 0:W], ACT.Copy),
                    deps=[tmb, ft.get(("qb_c",))])
                ft[("pb", oc % 2)] = tevb
                tpr = te(lambda e, rb=rb, oc=oc, W=W: e.matmul(
                    pR[:, 0:W], S("c_hsel")[:, oc * 128:oc * 128 + 128],
                    S(rb)[:, 0:W],
                    start=True, stop=True),
                    deps=[rnt[qk], tsct, ft.get(("pR",))])
                u1 = ve(lambda e, anm=anm, oc=oc, ti1=ti1, tb=tb, W=W: e.tensor_tensor(
                    S("cu1")[:, 0:W], S(anm)[:, oc * 512:oc * 512 + W],
                    tb[:, ti1 * 512:ti1 * 512 + W], ALU.mult),
                    deps=[tev[(anm, oc)], tts[ti1], ft.get(("cu1",))],
                    hazard=True)
                u2 = ve(lambda e, oc=oc, ti2=ti2, tb=tb, W=W: e.tensor_tensor(
                    S("cu2")[:, 0:W], S("qb_c")[:, 0:W],
                    tb[:, ti2 * 512:ti2 * 512 + W], ALU.mult),
                    deps=[tevb, tts[ti2], ft.get(("cu2",))],
                    hazard=True)
                ft[("qb_c",)] = u2
                u3 = ve(lambda e, W=W: e.tensor_tensor(
                    S("cu1")[:, 0:W], S("cu1")[:, 0:W],
                    S("cu2")[:, 0:W], ALU.add), deps=[u1, u2], hazard=True)
                ft[("cu2",)] = u3
                u4 = ve(lambda e, dst=dst, oc=oc, V0=V0, W=W: e.tensor_tensor(
                    S(dst)[:, oc * VRNG + V0: oc * VRNG + V0 + W],
                    S("cu1")[:, 0:W], pR[:, 0:W], ALU.mult),
                    deps=[u3, tpr], hazard=True)
                ft[("cu1",)] = u4
                ft[("pR",)] = u4
                ft[(dst, oc)] = u4
            ft[("rn", qk)] = u4
        ft[("sqb",)] = u4
        ft[("hg",)] = u4
        ft[("hgv",)] = ft[("pb3",)]
        ft[("tb",)] = u4
        for nmx in ("qa_g", "ka_g"):
            ft[(nmx,)] = u4
    tproj = [ft[("q_fm", oc)] for oc in range(4)] + \
            [ft[("k_fm", oc)] for oc in range(4)] + [ft[("pb3",)]]

    qgs = _p4_groups(l)
    tap_mid = kb.tap(f"mid{l}", (D, VRNG))
    for gi, (Q0, W) in enumerate(qgs):
        p = gi % 2
        mb, xg = S("mb0"), S("xg0")
        nch = (512 + W) // 128
        tld = []
        gext = tproj if gi < 2 else []
        for r in range(nch):
            tld.append(kb.dma("sync", mb[:, r * W:(r + 1) * W],
                              bass.AP(d[f"pmsk{l}_{gi}"].tensor, r * 128 * W,
                                      [[W, 128], [1, W]]),
                              deps=[ft.get(("mb",))] + gext, q=f"qx{p}"))
        srcs = xread(Q0, W, False)
        for c in range(8):
            tld.append(kb.dma("sync", xg[:, c * 512:c * 512 + W], srcs[c],
                              deps=[ft.get(("xg",))] + gext, q=f"qx{p}"))
        tldt = kb.p.qtot(f"qx{p}")
        to_toks = []
        for h in range(8):
            oc, hs = h // 2, (h % 2) * 64
            tav = None
            for r in range(nch):
                C0 = Q0 - 512 + 128 * r
                pscb = pb[r % 2]
                tmm = te(lambda e, pscb=pscb, oc=oc, hs=hs, C0=C0, W=W, Q0=Q0:
                         e.matmul(
                             pscb[:, 0:W],
                             S("k_fm")[hs:hs + 64,
                                          oc * VRNG + C0: oc * VRNG + C0 + 128],
                             S("q_fm")[hs:hs + 64,
                                          oc * VRNG + Q0: oc * VRNG + Q0 + W],
                             start=True, stop=True, tile_position=(hs, 0)),
                         deps=tproj + [ft.get(("psc", r % 2))]
                         if (h == 0 and r < 2) else [ft.get(("psc", r % 2))])
                t1 = sc(lambda e, pscb=pscb, r=r, W=W: e.activation(
                    S("wt_b")[:, (r % 2) * 512:(r % 2) * 512 + W],
                    pscb[:, 0:W], ACT.Exp, bias=-30.0),
                    deps=[tmm, ft.get(("wt", r % 2))])
                ft[("psc", r % 2)] = t1
                t2 = ve(lambda e, r=r, W=W, mb=mb: e.tensor_tensor(
                    S("wm_b")[:, (r % 2) * 512:(r % 2) * 512 + W],
                    S("wt_b")[:, (r % 2) * 512:(r % 2) * 512 + W],
                    mb[:, r * W:(r + 1) * W], ALU.mult),
                    deps=[t1, tldt, ft.get(("wm", r % 2))], hazard=True)
                ft[("wt", r % 2)] = t2
                slot = C0 // 128
                tav = te(lambda e, slot=slot, h=h, r=r, nch=nch, W=W: e.matmul(
                    pav[:, 0:W],
                    S("v_tm")[:, slot * 520 + h * 65: slot * 520 + h * 65 + 65],
                    S("wm_b")[:, (r % 2) * 512:(r % 2) * 512 + W],
                    start=(r == 0), stop=(r == nch - 1)),
                    deps=[t2] + ([ft.get(("pav",))] if r == 0 else []))
                ft[("wm", r % 2)] = tav
            trc = ve(lambda e, W=W: e.reciprocal(S("cu1")[64:65, 0:W],
                                            pav[64:65, 0:W]),
                     deps=[tav, ft.get(("rc",))] + (tproj if h == 0 else []),
                     hazard=True)
            tpr = te(lambda e, W=W: e.matmul(
                pR[0:64, 0:W], S("onesf_p")[64:65, 0:64],
                S("cu1")[64:65, 0:W], start=True, stop=True,
                tile_position=(64, 0)),
                deps=[trc, tonesp, ft.get(("pR",))])
            tpc = sc(lambda e, W=W: e.activation(
                S("rr_c")[:, 0:W], pR[0:64, 0:W], ACT.Copy),
                deps=[tpr, ft.get(("rr",))])
            ft[("pR",)] = tpc
            if hs == 0:
                to = ve(lambda e, oc=oc, W=W: e.tensor_tensor(
                    S("og")[0:64, oc * 512:oc * 512 + W],
                    pav[0:64, 0:W], S("rr_c")[:, 0:W], ALU.mult),
                    deps=[tpc, ft.get(("og", h))], hazard=True)
                tog = to
            else:
                to = ve(lambda e, W=W: e.tensor_tensor(
                    S("ot_c")[:, 0:W],
                    pav[0:64, 0:W], S("rr_c")[:, 0:W], ALU.mult),
                    deps=[tpc, ft.get(("ot",))], hazard=True)
                tog = kb.dma("gpsimd", S("og")[64:128, oc * 512:oc * 512 + W],
                             S("ot_c")[:, 0:W],
                             deps=[to, ft.get(("og", h)), kb.p.qtot("qga")],
                             q="qga")
                tog = kb.p.qtot("qga")
                ft[("ot",)] = tog
            ft[("rc",)] = to
            ft[("pav",)] = to
            ft[("rr",)] = to
            to_toks.append(tog)
        for dt in range(8):
            pwo = pb[2 + dt % 2]
            tmm = None
            for c in range(4):
                tmm = te(lambda e, pwo=pwo, c=c, dt=dt, W=W: e.matmul(
                    pwo[:, 0:W],
                    S("w4o")[:, c * 1024 + dt * 128: c * 1024 + dt * 128 + 128],
                    S("og")[:, c * 512:c * 512 + W],
                    start=(c == 0), stop=(c == 3)),
                    deps=to_toks + [twt, ft.get(("pwo", dt % 2))]
                    if c == 0 else [])
            t1 = ve(lambda e, pwo=pwo, dt=dt, W=W, xg=xg: e.tensor_tensor(
                S("midb")[:, 0:W], pwo[:, 0:W], xg[:, dt * 512:dt * 512 + W],
                ALU.add), deps=[tmm, tldt, ft.get(("midb",)),
                                ft.get(("midbg",)), ft.get(("midbo",))],
                hazard=True)
            ts_ = kb.dma("sync", bass.AP(d["s_midf"].tensor,
                                         dt * 128 * VRNG + Q0,
                                         [[VRNG, 128], [1, W]]),
                         S("midb")[:, 0:W], deps=[t1], q=f"qsp{p}")
            tsb = kb.dma("gpsimd", bass.AP(d["s_mid"].tensor,
                                           dt * 128 * VRNG + Q0,
                                           [[VRNG, 128], [1, W]]),
                         S("midb")[:, 0:W],
                         deps=[t1, kb.p.qtot("qg0")], q="qg0")
            if tap_mid is not None:
                kb.dma("sync", bass.AP(tap_mid.tensor, dt * 128 * VRNG + Q0,
                                       [[VRNG, 128], [1, W]]),
                       S("midb")[:, 0:W], deps=[t1], q="qout")
            ft[("midb",)] = kb.p.qtot(f"qsp{p}")
            ft[("midbg",)] = kb.p.qtot("qg0")
            if tap_mid is not None:
                ft[("midbo",)] = kb.p.qtot("qout")
            ft[("pwo", dt % 2)] = t1
        ft[("og", 0)] = t1
        for h in range(8):
            ft[("og", h)] = t1
        ft[("mb",)] = t1
        ft[("xg",)] = t1




def emit_p4c1(kb, l):
    s, d = kb.sb, kb.d
    ve, sc, te = kb.ve, kb.sc, kb.te
    s("w1a_sb", (128, 8 * W1P), BF16, scoped=True)
    s("w1g_sb", (128, 8 * W1P), BF16, scoped=True)
    tw = []
    for nm, dn in [("w1a_sb", "w1a"), ("w1g_sb", "w1g")]:
        tw.append(kb.dma("gpsimd", bass.AP(kb.s[nm].tensor, 0,
                                           [[8 * W1P, 128], [W1P, 8], [1, W1P]]),
                         bass.AP(d[dn].tensor, l * D * W1P,
                                 [[W1P, 128], [128 * W1P, 8], [1, W1P]]),
                         q="qw"))
    s("b1_c", (128, 44), F32, scoped=True)
    s("gb_row", (1, 2048), F32, scoped=True)
    s("c_ones_rowf", (1, 512), F32, scoped=True)
    _corf = kb.s["c_ones_rowf"]
    tonesf = kb.ve(lambda e, _corf=_corf: e.memset(_corf, 1.0))
    tw.append(kb.dma("sync", kb.s["b1_c"],
                     bass.AP(d["fb1col"].tensor, l * 128 * 44,
                             [[44, 128], [1, 44]]), q="q0"))
    tw.append(kb.dma("sync", kb.s["gb_row"][0:1, 0:1024],
                     bass.AP(d["flgrow"].tensor, l * 8 * 128,
                             [[0, 1], [1, 1024]]), q="q0"))
    tw.append(kb.dma("sync", kb.s["gb_row"][0:1, 1024:2048],
                     bass.AP(d["flbrow"].tensor, l * 8 * 128,
                             [[0, 1], [1, 1024]]), q="q0"))
    twt = kb.p.qtot("qw")
    tw0 = kb.p.qtot("q0")
    _gbr = kb.s["gb_row"]
    tnb = ve(lambda e, _gbr=_gbr: e.tensor_scalar(_gbr[0:1, 1024:2048],
                                       _gbr[0:1, 1024:2048], -1.0, None,
                                       ALU.mult), deps=[tw0], hazard=True)
    for p in range(2):
        s(f"mg{p}", (128, 8 * 512), BF16, scoped=True)
    s("sq_c", (128, 512), BF16, scoped=True)
    s("xng", (128, 8 * 512), BF16, scoped=True)
    s("st_r", (1, 4 * 512), F32, scoped=True)
    s("gg_c", (128, 512), BF16, scoped=True)
    s("h3_c", (128, 512), BF16, scoped=True)
    tfix = emit_colfix(kb, "s_mid")
    pSa = kb.psum("p4sa", (1, 512), F32, scoped=True)
    pSb = kb.psum("p4sb", (1, 512), F32, scoped=True)
    pM2 = kb.psum("p4m2", (128, 512), F32, scoped=True)
    pM3 = kb.psum("p4m3", (128, 512), F32, scoped=True)
    pa = kb.psum("p4pa", (128, 512), F32, scoped=True)
    pg = kb.psum("p4pg", (128, 512), F32, scoped=True)
    snap = dict(kb.s)
    S = snap.__getitem__
    ft = {}
    ft[("mg", 0)] = tfix
    ft[("mg", 1)] = tfix
    for gi, (Q0, W) in enumerate(_p4_groups(l)):
        p = gi % 2
        mg = kb.s[f"mg{p}"]
        tld = []
        for c in range(8):
            cc = Q0 - (1 if c >= 4 else 0)
            tld.append(kb.dma("sync", mg[:, c * 512:c * 512 + W],
                              bass.AP(d["s_mid"].tensor, c * 128 * VRNG + cc,
                                      [[VRNG, 128], [1, W]]),
                              deps=[ft.get(("mg", p))], q=f"qx{p}"))
        tldt = kb.p.qtot(f"qx{p}")
        tsa = tsb = None
        for c in range(8):
            tsa = te(lambda e, c=c, mg=mg, W=W: e.matmul(
                pSa[:, 0:W], S("c_ones_col"), mg[:, c * 512:c * 512 + W],
                start=(c == 0), stop=(c == 7)),
                deps=[tldt, ft.get(("pSa",))] if c == 0 else [])
        for c in range(8):
            t1 = sc(lambda e, c=c, mg=mg, W=W: e.square(
                S("sq_c")[:, 0:W], mg[:, c * 512:c * 512 + W]),
                deps=[tldt, ft.get(("sq",))])
            tsb = te(lambda e, c=c, W=W: e.matmul(
                pSb[:, 0:W], S("c_ones_col"), S("sq_c")[:, 0:W],
                start=(c == 0), stop=(c == 7)),
                deps=[t1] + ([ft.get(("pSb",))] if c == 0 else []))
            ft[("sq",)] = tsb
        st = S("st_r")
        t1 = ve(lambda e, W=W: e.tensor_scalar(st[0:1, 0:0 + W], pSa[:, 0:W],
                                               1.0 / 1024.0, None, ALU.mult),
                deps=[tsa, ft.get(("st",))], hazard=True)
        ft[("pSa",)] = t1
        t2 = ve(lambda e, W=W: e.tensor_scalar(st[0:1, 1536:1536 + W], pSb[:, 0:W],
                                               1.0 / 1024.0, None, ALU.mult),
                deps=[tsb, ft.get(("st",))], hazard=True)
        ft[("pSb",)] = t2
        t3 = ve(lambda e, W=W: e.tensor_tensor(st[0:1, 512:512 + W], st[0:1, 0:0 + W],
                                               st[0:1, 0:0 + W], ALU.mult),
                deps=[t1], hazard=True)
        t4 = ve(lambda e, W=W: e.tensor_tensor(st[0:1, 512:512 + W], st[0:1, 1536:1536 + W],
                                               st[0:1, 512:512 + W], ALU.subtract),
                deps=[t2, t3], hazard=True)
        t5 = sc(lambda e, W=W: e.activation(st[0:1, 512:512 + W], st[0:1, 512:512 + W],
                                            ACT.Ln, bias=1e-5),
                deps=[t4], hazard=True)
        t6 = sc(lambda e, W=W: e.activation(st[0:1, 512:512 + W], st[0:1, 512:512 + W],
                                            ACT.Exp, scale=-0.5),
                deps=[t5], hazard=True)
        t7 = ve(lambda e, W=W: e.tensor_tensor(st[0:1, 1024:1024 + W], st[0:1, 0:0 + W],
                                               st[0:1, 512:512 + W], ALU.mult),
                deps=[t6], hazard=True)
        txn = []
        for c in range(8):
            tm2 = te(lambda e, c=c, W=W: e.matmul(
                pM2[:, 0:W], S("gb_row")[0:1, c * 128:c * 128 + 128], st[0:1, 512:512 + W],
                start=True, stop=True), deps=[t6, ft.get(("pM2",))])
            tm3a = te(lambda e, c=c, W=W: e.matmul(
                pM3[:, 0:W], S("gb_row")[0:1, c * 128:c * 128 + 128], st[0:1, 1024:1024 + W],
                start=True, stop=False), deps=[t7, ft.get(("pM3",)), tnb])
            tm3b = te(lambda e, c=c, W=W: e.matmul(
                pM3[:, 0:W], S("gb_row")[0:1, 1024 + c * 128:1024 + c * 128 + 128],
                S("c_ones_rowf")[:, 0:W],
                start=False, stop=True), deps=[tonesf, tnb])
            tn1 = ve(lambda e, c=c, mg=mg, W=W: e.tensor_tensor(
                S("xng")[:, c * 512:c * 512 + W],
                mg[:, c * 512:c * 512 + W], pM2[:, 0:W], ALU.mult),
                deps=[tm2, tldt, ft.get(("xng", c))], hazard=True)
            tn2 = ve(lambda e, c=c, W=W: e.scalar_tensor_tensor(
                S("xng")[:, c * 512:c * 512 + W],
                S("xng")[:, c * 512:c * 512 + W], 1.0,
                pM3[:, 0:W], ALU.mult, ALU.subtract),
                deps=[tn1, tm3b], hazard=True)
            ft[("pM2",)] = tn1
            ft[("pM3",)] = tn2
            txn.append(tn2)
        ft[("st",)] = tn2
        ft[("mg", p)] = tn2
        for ob in range(22):
            tma = tmg = None
            for c in range(8):
                tma = te(lambda e, c=c, ob=ob, W=W: e.matmul(
                    pa[:, 0:W],
                    S("w1a_sb")[:, c * W1P + ob * 128: c * W1P + ob * 128 + 128],
                    S("xng")[:, c * 512:c * 512 + W],
                    start=(c == 0), stop=(c == 7)),
                    deps=txn + [twt, ft.get(("pa",))] if c == 0 else [])
                tmg = te(lambda e, c=c, ob=ob, W=W: e.matmul(
                    pg[:, 0:W],
                    S("w1g_sb")[:, c * W1P + ob * 128: c * W1P + ob * 128 + 128],
                    S("xng")[:, c * 512:c * 512 + W],
                    start=(c == 0), stop=(c == 7)),
                    deps=[ft.get(("pg",))] if c == 0 else [])
            tg1 = sc(lambda e, ob=ob, W=W: e.activation(
                S("gg_c")[:, 0:W], pg[:, 0:W], ACT.Gelu,
                bias=S("b1_c")[:, 22 + ob:23 + ob]),
                deps=[tmg, tw0, ft.get(("gg",))])
            ft[("pg",)] = tg1
            th3 = ve(lambda e, ob=ob, W=W: e.scalar_tensor_tensor(
                S("h3_c")[:, 0:W], pa[:, 0:W],
                S("b1_c")[:, ob:ob + 1], S("gg_c")[:, 0:W],
                ALU.add, ALU.mult),
                deps=[tma, tg1, tw0, ft.get(("h3",))], hazard=True)
            ft[("pa",)] = th3
            ft[("gg",)] = th3
            tst = kb.dma("sync", bass.AP(d["s_h3"].tensor,
                                         ob * 128 * (VRNG - QLO0) + Q0 - QLO0,
                                         [[VRNG - QLO0, 128], [1, W]]),
                         S("h3_c")[:, 0:W], deps=[th3], q=f"qsp{p}")
            ft[("h3",)] = kb.p.qtot(f"qsp{p}")
        for c in range(8):
            ft[("xng", c)] = th3


def emit_p4c2(kb, l, taps=()):
    s, d = kb.sb, kb.d
    ve, sc, te = kb.ve, kb.sc, kb.te
    s("w2_sb", (128, 22 * 1024), BF16, scoped=True)
    tw = [kb.dma("gpsimd", bass.AP(kb.s["w2_sb"].tensor, 0,
                                   [[22 * 1024, 128], [1024, 22], [1, 1024]]),
                 bass.AP(d["fw2"].tensor, l * W1P * D,
                         [[1024, 128], [128 * 1024, 22], [1, 1024]]),
                 q="qw")]
    s("b2_c", (128, 8), F32, scoped=True)
    tw.append(kb.dma("sync", kb.s["b2_c"],
                     bass.AP(d["fb2col"].tensor, l * 128 * 8,
                             [[8, 128], [1, 8]]), q="q0"))
    twt = kb.p.qtot("qw")
    tw0 = kb.p.qtot("q0")
    for p in range(2):
        s(f"h3g{p}", (128, 22 * 512), BF16, scoped=True)
        s(f"mg2{p}", (128, 8 * 512), F32, scoped=True)
    s("outb", (128, 512), F32, scoped=True)
    pd0 = kb.psum("p4d0", (128, 512), F32, scoped=True)
    pd1 = kb.psum("p4d1", (128, 512), F32, scoped=True)
    tap_out = kb.tap(f"out{l}", (D, VRNG))
    snap = dict(kb.s)
    S = snap.__getitem__
    ft = {}
    for gi, (Q0, W) in enumerate(_p4_groups(l)):
        p = gi % 2
        h3g, mg2 = kb.s[f"h3g{p}"], kb.s[f"mg2{p}"]
        tld = []
        for kc in range(22):
            tld.append(kb.dma("sync", h3g[:, kc * 512:kc * 512 + W],
                              bass.AP(d["s_h3"].tensor,
                                      kc * 128 * (VRNG - QLO0) + Q0 - QLO0,
                                      [[VRNG - QLO0, 128], [1, W]]),
                              deps=[ft.get(("h3g", p))], q=f"qx{p}"))
        for c in range(8):
            tld.append(kb.dma("sync", mg2[:, c * 512:c * 512 + W],
                              bass.AP(d["s_midf"].tensor, c * 128 * VRNG + Q0,
                                      [[VRNG, 128], [1, W]]),
                              deps=[ft.get(("mg2", p))], q=f"qx{p}"))
        tldt = kb.p.qtot(f"qx{p}")
        for dt in range(8):
            pdx = pd0 if dt % 2 == 0 else pd1
            tmm = None
            for kc in range(22):
                tmm = te(lambda e, pdx=pdx, kc=kc, dt=dt, h3g=h3g, W=W:
                         e.matmul(
                             pdx[:, 0:W],
                             S("w2_sb")[:, kc * 1024 + dt * 128:
                                           kc * 1024 + dt * 128 + 128],
                             h3g[:, kc * 512:kc * 512 + W],
                             start=(kc == 0), stop=(kc == 21)),
                         deps=[tldt, twt, ft.get(("pd", dt % 2))]
                         if kc == 0 else [])
            t1 = ve(lambda e, pdx=pdx, dt=dt, mg2=mg2, W=W:
                    e.scalar_tensor_tensor(
                        S("outb")[:, 0:W], pdx[:, 0:W],
                        S("b2_c")[:, dt:dt + 1],
                        mg2[:, dt * 512:dt * 512 + W], ALU.add, ALU.add),
                    deps=[tmm, tldt, tw0, ft.get(("outb",)),
                          ft.get(("outbg",)), ft.get(("outbo",))],
                    hazard=True)
            ft[("pd", dt % 2)] = t1
            ts_ = kb.dma("sync", bass.AP(d["s_outf"].tensor,
                                         dt * 128 * VRNG + Q0,
                                         [[VRNG, 128], [1, W]]),
                         S("outb")[:, 0:W], deps=[t1], q=f"qsp{p}")
            tsb = kb.dma("gpsimd", bass.AP(d["s_out"].tensor,
                                           dt * 128 * VRNG + Q0,
                                           [[VRNG, 128], [1, W]]),
                         S("outb")[:, 0:W],
                         deps=[t1, kb.p.qtot("qg0")], q="qg0")
            if tap_out is not None:
                kb.dma("sync", bass.AP(tap_out.tensor, dt * 128 * VRNG + Q0,
                                       [[VRNG, 128], [1, W]]),
                       S("outb")[:, 0:W], deps=[t1], q="qout")
                ft[("outbo",)] = kb.p.qtot("qout")
            ft[("outb",)] = kb.p.qtot(f"qsp{p}")
            ft[("outbg",)] = kb.p.qtot("qg0")
        ft[("h3g", p)] = t1
        ft[("mg2", p)] = t1





def emit_p5(kb):
    s, d = kb.sb, kb.d
    ve, sc, te = kb.ve, kb.sc, kb.te
    s("w5o", (128, 8 * 1024), BF16, scoped=True)
    s("b5", (128, 1024), BF16, scoped=True)
    kb.s["w5q"] = kb.s["w5o"][:, 0:4096]
    kb.s["w5k"] = kb.s["w5o"][:, 4096:8192]
    s("w5v", (128, 8 * 512), BF16, scoped=True)
    s("ones_rb", (128, 512), BF16, scoped=True)
    s("onesf_p5", (128, 128), F32, scoped=True)
    s("xn1own", (128, 8 * 2048), BF16, scoped=True)
    s("og5", (128, 8 * 2048), BF16, scoped=True)
    s("q5", (128, 4 * 512), BF16, scoped=True)
    s("k5", (128, 4 * 4096), BF16, scoped=True)
    s("v5", (128, 32 * 520), BF16, scoped=True)
    s("xf5", (128, 8 * 512), BF16, scoped=True)
    kb.s["blB"] = kb.s["xf5"][:, 0:2048]
    s("wt5", (128, 4 * 512), BF16, scoped=True)
    s("ot5", (64, 512), BF16, scoped=True)
    s("rr5", (64, 512), F32, scoped=True)
    s("st2", (128, 32), F32, scoped=True)
    s("stT", (32, 128), F32, scoped=True)
    s("strow", (1, 1024), F32, scoped=True)
    s("u5", (128, 512), F32, scoped=True)
    s("x2c", (128, 512), F32, scoped=True)
    s("ffc", (128, 2 * 512), F32, scoped=True)
    pq = [kb.psum(f"p5q{i}", (128, 512), F32, scoped=True) for i in range(2)]
    pav = kb.psum("p5av", (65, 512), F32, scoped=True)
    pR = kb.psum("p5R", (128, 512), F32, scoped=True)
    pRm = kb.psum("p5Rm", (128, 512), F32, scoped=True)
    pRs = kb.psum("p5Rs", (128, 512), F32, scoped=True)
    pT = kb.psum("p5T", (32, 128), F32, scoped=True)
    snap = dict(kb.s)
    S = snap.__getitem__
    ft = {}
    tw = []
    for bn, row in [("bq1", 0), ("bk1", 32), ("bv1", 64), ("bo1", 96)]:
        tw.append(kb.dma("gpsimd", S("b5")[row:row + 1, 0:1024],
                         d[bn], q="qw"))
    twt0 = kb.p.qtot("qw")
    t_onr = ve(lambda e: e.memset(S("ones_rb"), 1.0))
    t_onp = ve(lambda e: e.memset(S("onesf_p5"), 1.0))
    tvone = ve(lambda e: e.memset(
        bass.AP(S("v5").tensor, 64, [[32 * 520, 128], [65, 256]]), 1.0))
    tblend = []
    prevb = []
    for c in range(8):
        t1 = kb.dma("sync", S("xn1own")[:, c * 2048:(c + 1) * 2048],
                    bass.AP(d["s_xn1fm"].tensor, c * 128 * XNW + 1153,
                            [[XNW, 128], [1, 2048]]),
                    deps=prevb, q="qx0")
        t2 = kb.dma("sync", S("blB"),
                    bass.AP(d["s_xn1fm"].tensor, c * 128 * XNW + 3201,
                            [[XNW, 128], [1, 2048]]),
                    deps=prevb, q="qx0")
        tl = kb.p.qtot("qx0")
        t3 = ve(lambda e, c=c: e.tensor_tensor(
            S("blB"), S("blB"), S("xn1own")[:, c * 2048:(c + 1) * 2048],
            ALU.subtract), deps=[tl], hazard=True)
        t4 = ve(lambda e, c=c: e.scalar_tensor_tensor(
            S("xn1own")[:, c * 2048:(c + 1) * 2048], S("blB"),
            kb.s["c_msel"], S("xn1own")[:, c * 2048:(c + 1) * 2048],
            ALU.mult, ALU.add), deps=[t3], hazard=True)
        tblend.append(t4)
        prevb = [t4]
    ft[("xf5",)] = tblend[-1]
    tm1 = ve(lambda e: e.tensor_tensor(
        S("st2")[:, 0:16], kb.s["ln1_mean"][:, 16:32],
        kb.s["ln1_mean"][:, 0:16], ALU.subtract), deps=[])
    tm2 = ve(lambda e: e.scalar_tensor_tensor(
        S("st2")[:, 0:16], S("st2")[:, 0:16], kb.s["c_msel"],
        kb.s["ln1_mean"][:, 0:16], ALU.mult, ALU.add),
        deps=[tm1], hazard=True)
    ts1 = ve(lambda e: e.tensor_tensor(
        S("st2")[:, 16:32], kb.s["ln1_inv"][:, 16:32],
        kb.s["ln1_inv"][:, 0:16], ALU.subtract), deps=[])
    ts2 = ve(lambda e: e.scalar_tensor_tensor(
        S("st2")[:, 16:32], S("st2")[:, 16:32], kb.s["c_msel"],
        kb.s["ln1_inv"][:, 0:16], ALU.mult, ALU.add),
        deps=[ts1], hazard=True)
    ts3 = ve(lambda e: e.reciprocal(S("st2")[:, 16:32], S("st2")[:, 16:32]),
             deps=[ts2], hazard=True)
    tT1 = te(lambda e: e.transpose(pT[:, 0:128], S("st2"), kb.s["c_identf"]),
             deps=[tm2, ts3])
    tT2 = ve(lambda e: e.tensor_copy(S("stT"), pT[:, 0:128]), deps=[tT1])
    tT2b = kb.p.qtot("qga")

    to_all = []
    for hg in range(2):
        twg = []
        for dn, dstt, doff, dw in [("wqT", "w5o", 0, 8192),
                                   ("wkT", "w5o", 4096, 8192),
                                   ("wvT", "w5v", 0, 4096)]:
            twg.append(kb.dma("gpsimd",
                              bass.AP(S(dstt).tensor, doff,
                                      [[dw, 128], [512, 8], [1, 512]]),
                              bass.AP(d[dn].tensor, hg * 512,
                                      [[1024, 128], [128 * 1024, 8],
                                       [1, 512]]),
                              deps=[ft.get(("w5g",))], q="qw"))
        twt = kb.p.qtot("qw")
        tk_toks = []
        for tkg in range(8):
            tld = []
            for dc in range(8):
                tld.append(kb.dma("sync", S("xf5")[:, dc * 512:(dc + 1) * 512],
                                  bass.AP(d["s_xn1fm"].tensor,
                                          dc * 128 * XNW + 1153 + tkg * 512,
                                          [[XNW, 128], [1, 512]]),
                                  deps=[ft.get(("xf5",))], q="qx1"))
            tldt = kb.p.qtot("qx1")
            for oc in range(4):
                pbx = pq[oc % 2]
                tmm = None
                for dc in range(8):
                    tmm = te(lambda e, pbx=pbx, oc=oc, dc=dc: e.matmul(
                        pbx, S("w5k")[:, dc * 512 + oc * 128:
                                      dc * 512 + oc * 128 + 128],
                        S("xf5")[:, dc * 512:(dc + 1) * 512],
                        start=(dc == 0), stop=False),
                        deps=[twt, tldt, ft.get(("pq", oc % 2))]
                        if dc == 0 else [])
                tmb = te(lambda e, pbx=pbx, oc=oc, hg=hg: e.matmul(
                    pbx, S("b5")[32:33, hg * 512 + oc * 128:
                                 hg * 512 + oc * 128 + 128],
                    S("ones_rb")[32:33, :], start=False, stop=True,
                    tile_position=(32, 0)),
                    deps=[twt0, t_onr])
                t1 = sc(lambda e, pbx=pbx, oc=oc, tkg=tkg: e.activation(
                    S("k5")[:, oc * 4096 + tkg * 512:
                            oc * 4096 + tkg * 512 + 512],
                    pbx, ACT.Copy), deps=[tmb, ft.get(("k5",))])
                ft[("pq", oc % 2)] = t1
                tk_toks.append(t1)
            for tt in range(4):
                slot = tkg * 4 + tt
                pbx = pq[tt % 2]
                tmm = None
                for dc in range(8):
                    tmm = te(lambda e, pbx=pbx, tt=tt, dc=dc: e.matmul(
                        pbx, S("xf5")[:, dc * 512 + tt * 128:
                                      dc * 512 + tt * 128 + 128],
                        S("w5v")[:, dc * 512:(dc + 1) * 512],
                        start=(dc == 0), stop=False),
                        deps=[twt, tldt, ft.get(("pq", tt % 2))]
                        if dc == 0 else [])
                tmb = te(lambda e, pbx=pbx, hg=hg: e.matmul(
                    pbx, S("ones_rb")[64:65, 0:128],
                    S("b5")[64:65, hg * 512: hg * 512 + 512],
                    start=False, stop=True, tile_position=(64, 0)),
                    deps=[twt0, t_onr])
                t1 = sc(lambda e, pbx=pbx, slot=slot: e.activation(
                    bass.AP(S("v5").tensor, slot * 520,
                            [[32 * 520, 128], [65, 8], [1, 64]]),
                    pbx.rearrange("p (h e) -> p h e", e=64), ACT.Copy),
                    deps=[tmb, tvone, ft.get(("v5",))])
                ft[("pq", tt % 2)] = t1
                tk_toks.append(t1)
            ft[("xf5",)] = tk_toks[-1]
        ft[("w5g",)] = tk_toks[-1]
        tprj = list(tk_toks)
        for tqg in range(4):
            tq_toks = []
            for oc in range(4):
                pbx = pq[oc % 2]
                tmm = None
                for dc in range(8):
                    tmm = te(lambda e, pbx=pbx, oc=oc, tqg=tqg, dc=dc: e.matmul(
                        pbx, S("w5q")[:, dc * 512 + oc * 128:
                                      dc * 512 + oc * 128 + 128],
                        S("xn1own")[:, dc * 2048 + tqg * 512:
                                    dc * 2048 + tqg * 512 + 512],
                        start=(dc == 0), stop=False),
                        deps=[twt, tblend[dc], ft.get(("pq", oc % 2))]
                        if dc == 0 else [])
                tmb = te(lambda e, pbx=pbx, oc=oc, hg=hg: e.matmul(
                    pbx, S("b5")[0:1, hg * 512 + oc * 128:
                                 hg * 512 + oc * 128 + 128],
                    S("ones_rb")[0:1, :], start=False, stop=True),
                    deps=[twt0, t_onr])
                t1 = sc(lambda e, pbx=pbx, oc=oc: e.activation(
                    S("q5")[:, oc * 512:oc * 512 + 512],
                    pbx, ACT.Copy), deps=[tmb, ft.get(("q5",))])
                ft[("pq", oc % 2)] = t1
                tq_toks.append(t1)
            for h8 in range(8):
                oc, hs = h8 // 2, (h8 % 2) * 64
                gc = hg * 4 + h8 // 2
                tav = None
                for ck in range(32):
                    C0 = ck * 128
                    pscb = [pq[0], pq[1], pRm, pRs][ck % 4]
                    tmm = te(lambda e, pscb=pscb, oc=oc, hs=hs, C0=C0,
                             tqg=tqg: e.matmul(
                                 pscb[:, 0:512],
                                 S("k5")[hs:hs + 64,
                                         oc * 4096 + C0: oc * 4096 + C0 + 128],
                                 S("q5")[hs:hs + 64,
                                         oc * 512:oc * 512 + 512],
                                 start=True, stop=True, tile_position=(hs, 0)),
                             deps=tprj + tq_toks + [ft.get(("psc", ck % 4))]
                             if (h8 == 0 and ck < 4)
                             else [ft.get(("psc", ck % 4))])
                    t1 = sc(lambda e, pscb=pscb, ck=ck: e.activation(
                        S("wt5")[:, (ck % 4) * 512:(ck % 4) * 512 + 512],
                        pscb[:, 0:512], ACT.Exp, bias=-30.0, scale=0.125),
                        deps=[tmm, ft.get(("wt", ck % 4))])
                    ft[("psc", ck % 4)] = t1
                    slot = C0 // 128
                    tav = te(lambda e, slot=slot, h8=h8, ck=ck: e.matmul(
                        pav, S("v5")[:, slot * 520 + h8 * 65:
                                     slot * 520 + h8 * 65 + 65],
                        S("wt5")[:, (ck % 4) * 512:(ck % 4) * 512 + 512],
                        start=(ck == 0), stop=(ck == 31)),
                        deps=[t1] + ([ft.get(("pav",))] if ck == 0 else []))
                    ft[("wt", ck % 4)] = tav
                trc = ve(lambda e: e.reciprocal(S("u5")[64:65, 0:512],
                                                pav[64:65, :]),
                         deps=[tav, ft.get(("rc",))], hazard=True)
                tpr = te(lambda e: e.matmul(
                    pR[0:64, :], S("onesf_p5")[64:65, 0:64],
                    S("u5")[64:65, 0:512], start=True, stop=True,
                    tile_position=(64, 0)),
                    deps=[trc, t_onp, ft.get(("pR",))])
                tpc = sc(lambda e: e.activation(
                    S("rr5"), pR[0:64, :], ACT.Copy),
                    deps=[tpr, ft.get(("rr",))])
                ft[("pR",)] = tpc
                if hs == 0:
                    to = ve(lambda e, gc=gc, tqg=tqg: e.tensor_tensor(
                        S("og5")[0:64, gc * 2048 + tqg * 512:
                                 gc * 2048 + tqg * 512 + 512],
                        pav[0:64, :], S("rr5"), ALU.mult),
                        deps=[tpc, ft.get(("og5", gc, tqg))], hazard=True)
                    tog = to
                else:
                    to = ve(lambda e: e.tensor_tensor(
                        S("ot5"), pav[0:64, :], S("rr5"), ALU.mult),
                        deps=[tpc, ft.get(("ot",))], hazard=True)
                    tog = kb.dma("gpsimd",
                                 S("og5")[64:128, gc * 2048 + tqg * 512:
                                          gc * 2048 + tqg * 512 + 512],
                                 S("ot5"),
                                 deps=[to, ft.get(("og5", gc, tqg)),
                                       kb.p.qtot("qga")], q="qga")
                    tog = kb.p.qtot("qga")
                    ft[("ot",)] = tog
                ft[("rc",)] = to
                ft[("pav",)] = to
                ft[("rr",)] = to
                ft[("og5", gc, tqg)] = tog
                to_all.append(tog)
            ft[("pq", 0)] = ft.get(("psc", 0))
            ft[("pq", 1)] = ft.get(("psc", 1))
            ft[("pRm",)] = ft.get(("psc", 2))
            ft[("pRs",)] = ft.get(("psc", 3))
            ft[("q5",)] = to_all[-1]
        ft[("k5",)] = to_all[-1]
        ft[("v5",)] = to_all[-1]

    two = kb.dma("gpsimd", bass.AP(S("w5o").tensor, 0,
                                   [[8 * 1024, 128], [1024, 8], [1, 1024]]),
                 bass.AP(d["wo1"].tensor, 0,
                         [[1024, 128], [128 * 1024, 8], [1, 1024]]),
                 deps=to_all + [ft.get(("w5g",))], q="qw")
    twot = kb.p.qtot("qw")
    tap_x2f = kb.tap("x2f", (D, 2048))
    for tg in range(4):
        tr1 = kb.dma("gpsimd", S("strow")[0:1, 0:512],
                     S("stT")[tg * 4:tg * 4 + 4, :],
                     deps=[tT2b, ft.get(("strow",)), kb.p.qtot("qga")],
                     q="qga")
        tr2 = kb.dma("gpsimd", S("strow")[0:1, 512:1024],
                     S("stT")[16 + tg * 4:16 + tg * 4 + 4, :],
                     deps=[tT2b, ft.get(("strow",)), kb.p.qtot("qga")],
                     q="qga")
        trowt = kb.p.qtot("qga")
        trm = trs = None
        for j in range(4):
            trm = te(lambda e, j=j: e.matmul(
                pRm[:, j * 128:(j + 1) * 128],
                S("onesf_p5")[0:1, 0:128],
                S("strow")[0:1, j * 128:j * 128 + 128],
                start=True, stop=True),
                deps=[trowt, t_onp, ft.get(("pRm",))])
            trs = te(lambda e, j=j: e.matmul(
                pRs[:, j * 128:(j + 1) * 128],
                S("onesf_p5")[0:1, 0:128],
                S("strow")[0:1, 512 + j * 128:512 + j * 128 + 128],
                start=True, stop=True),
                deps=[trowt, t_onp, ft.get(("pRs",))])
        for dt in range(8):
            pwo = pq[dt % 2]
            tmm = None
            for ic in range(8):
                tmm = te(lambda e, pwo=pwo, ic=ic, dt=dt, tg=tg: e.matmul(
                    pwo, S("w5o")[:, ic * 1024 + dt * 128:
                                  ic * 1024 + dt * 128 + 128],
                    S("og5")[:, ic * 2048 + tg * 512:
                             ic * 2048 + tg * 512 + 512],
                    start=(ic == 0), stop=False),
                    deps=to_all + [twot, ft.get(("pq", dt % 2))]
                    if ic == 0 else [])
            tmb = te(lambda e, pwo=pwo, dt=dt: e.matmul(
                pwo, S("b5")[96:97, dt * 128: dt * 128 + 128],
                S("ones_rb")[96:97, :], start=False, stop=True,
                tile_position=(96, 0)),
                deps=[twt0, t_onr])
            u1 = ve(lambda e, dt=dt, tg=tg: e.tensor_tensor(
                S("u5"), S("xn1own")[:, dt * 2048 + tg * 512:
                                     dt * 2048 + tg * 512 + 512],
                pRs[:, 0:512], ALU.mult),
                deps=[trs, tblend[dt], ft.get(("u5",))], hazard=True)
            u2 = ve(lambda e: e.scalar_tensor_tensor(
                S("u5"), S("u5"), 1.0, pRm[:, 0:512], ALU.mult, ALU.add),
                deps=[u1, trm], hazard=True)
            u3 = ve(lambda e, pwo=pwo: e.scalar_tensor_tensor(
                S("x2c"), pwo, 1.0, S("u5"), ALU.mult, ALU.add),
                deps=[tmb, u2, ft.get(("x2c",))], hazard=True)
            tldf = kb.dma("sync", S("ffc")[:, (dt % 2) * 512:
                                           (dt % 2) * 512 + 512],
                          bass.AP(d["s_outf"].tensor,
                                  dt * 128 * VRNG + VHALO + tg * 512,
                                  [[VRNG, 128], [1, 512]]),
                          deps=[ft.get(("ffc", dt % 2))] +
                          (to_all if (tg == 0 and dt < 2) else []),
                          q="qx1")
            tldft = kb.p.qtot("qx1")
            u4 = ve(lambda e, dt=dt: e.tensor_tensor(
                S("x2c"), S("x2c"),
                S("ffc")[:, (dt % 2) * 512:(dt % 2) * 512 + 512],
                ALU.add), deps=[u3, tldft], hazard=True)
            ft[("ffc", dt % 2)] = u4
            ft[("u5",)] = u3
            ft[("pq", dt % 2)] = u3
            tst = kb.dma("sync", d["x2"][dt * 128:(dt + 1) * 128,
                                         tg * 512:(tg + 1) * 512],
                         S("x2c"), deps=[u4], q="qout")
            if tap_x2f is not None:
                kb.dma("sync", tap_x2f[dt * 128:(dt + 1) * 128,
                                       tg * 512:(tg + 1) * 512],
                       S("x2c"), deps=[u4], q="qout")
            ft[("x2c",)] = kb.p.qtot("qout")
        ft[("pRm",)] = u2
        ft[("pRs",)] = u1
        ft[("strow",)] = trs
    kb.tok["p5"] = [kb.p.qtot("qout")]


def build(taps=()):
    nc = bass.Bass("TRN2", target_bir_lowering=False)
    stack = ExitStack()
    with stack:
        kb = KB(nc, stack, taps)
        with nc.Block() as block:
            emit_p0(kb)
            kb.open_scope()
            emit_p1(kb)
            kb.close_scope()
            kb.open_scope()
            emit_p2(kb)
            kb.close_scope()
            kb.open_scope()
            emit_p3f(kb)
            kb.close_scope()
            kb.open_scope()
            emit_p3fb(kb)
            kb.close_scope()
            kb.open_scope()
            emit_p3a(kb)
            kb.close_scope()
            kb.open_scope()
            emit_p4pre(kb)
            kb.close_scope()
            for l in range(2):
                kb.open_scope()
                emit_p4ab(kb, l)
                kb.close_scope()
                kb.open_scope()
                emit_p4c1(kb, l)
                kb.close_scope()
                kb.open_scope()
                emit_p4c2(kb, l)
                kb.close_scope()
            kb.open_scope()
            emit_p5(kb)
            kb.close_scope()
            finalize(kb)
            kb.p.check()
            kb.p.replay(block)
    return nc, kb


def finalize(kb):
    kb.p.emit("sync", lambda e: e.nop(),
              deps=[kb.p.qtot("qout"), kb.p.qtot("qsp0"), kb.p.qtot("qsp1"),
                    kb.p.qtot("qga")])


def kernel(**inputs):
    sh, percore = prep_inputs(inputs)
    taps = tuple(globals().get("_TAPS", ()))
    in_maps = []
    for c in range(8):
        m = dict(sh)
        m.update(percore[c])
        in_maps.append(m)
    nc, kb = build(taps=taps)
    res = run_bass_kernel_spmd(nc, in_maps, core_ids=list(range(8)))
    out = np.zeros((B, T, D), np.float32)
    for c in range(8):
        b, half = c // 2, c % 2
        out[b, half * 2048:(half + 1) * 2048] = res.results[c]["x2"].T
    globals()["_LAST_RES"] = res
    return out
